# version 12
# baseline (speedup 1.0000x reference)
import math
import numpy as np
import ml_dtypes
from contextlib import ExitStack

import concourse.bass as bass
import concourse.mybir as mybir
from concourse.bass_utils import run_bass_kernel_spmd

F32 = mybir.dt.float32
BF16 = mybir.dt.bfloat16
AX = mybir.AxisListType
ALU = mybir.AluOpType
ACT = mybir.ActivationFunctionType

T = 2048
C = 1024
NT = T // 128
NC = C // 128
BF = ml_dtypes.bfloat16

ENGINES = ("tensor", "vector", "scalar", "gpsimd", "sync")
NDS = 4
DMAQ = {"sync": "qs", "gpsimd": "qg", "scalar": "qa"}
DMASEMS = [q + str(i) for q in DMAQ.values() for i in range(NDS)]


class Prog:

    def __init__(self, nc):
        self.nc = nc
        self.steps = {e: [] for e in ENGINES}
        self.cnt = {e: 0 for e in ENGINES}
        self.dcnt = {s: 0 for s in DMASEMS}
        self.drot = {q: 0 for q in DMAQ.values()}
        self.last_wait = {e: {} for e in ENGINES}

    def step(self, eng, body, deps=(), tag=""):
        waits = self._waits(eng, deps)
        self.cnt[eng] += 1
        tok = ("c_" + eng, self.cnt[eng])
        self.steps[eng].append((waits, body, [("c_" + eng, 1)], tag))
        return tok

    def dma(self, eng, pairs, deps=(), tag="", slow=False):
        q = DMAQ[eng]
        s = q + str(self.drot[q] % NDS)
        self.drot[q] += 1
        waits = self._waits(eng, deps)
        self.dcnt[s] += len(pairs)
        tok = (s, self.dcnt[s] * 16)

        def body(e, _pairs=pairs, _s=s, _slow=slow):
            for (o, i) in _pairs:
                e.dma_start(out=o, in_=i, allow_slow_non_contiguous=_slow).then_inc(self.sem[_s], 16)
            return None

        self.steps[eng].append((waits, body, [], tag))
        return tok

    def _waits(self, eng, deps):
        waits = {}
        lw = self.last_wait[eng]
        for (s, v) in deps:
            if s in self.dcnt:
                v = self.dcnt[s] * 16
            if lw.get(s, 0) >= v:
                continue
            if waits.get(s, 0) < v:
                waits[s] = v
        for s, v in waits.items():
            lw[s] = v
        return waits

    def render(self, blk, sems):
        self.sem = sems
        for eng in ENGINES:
            if not self.steps[eng]:
                continue
            section = getattr(blk, eng)

            def make(eng_name):
                def _(e):
                    for (waits, body, incs, tag) in self.steps[eng_name]:
                        for s, v in waits.items():
                            e.wait_ge(self.sem[s], v)
                        last = body(e)
                        for (s, dv) in incs:
                            if last is None:
                                e.nop().then_inc(self.sem[s], dv)
                            else:
                                last.then_inc(self.sem[s], dv)
                return _

            section(make(eng))


class Buf:
    __slots__ = ("w", "readers", "name", "init")

    def __init__(self, name, init_deps=()):
        self.name = name
        self.w = []
        self.readers = []
        self.init = list(init_deps)

    def rd_deps(self):
        return list(self.w)

    def wr_deps(self):
        return self.init + self.readers + self.w


class SBMap:

    def __init__(self):
        self.active = {}

    def claim(self, phys, c0, c1, name):
        lst = self.active.setdefault(phys, [])
        deps = []
        keep = []
        for (a, b, buf) in lst:
            if a < c1 and c0 < b:
                deps += buf.wr_deps()
                if a < c0:
                    keep.append((a, c0, buf))
                if c1 < b:
                    keep.append((c1, b, buf))
            else:
                keep.append((a, b, buf))
        nb = Buf(name, deps)
        keep.append((c0, c1, nb))
        self.active[phys] = keep
        return nb


def _host_constants():
    c = {}
    c["ident_f32"] = np.eye(128, dtype=np.float32)
    c["ident_bf16"] = np.eye(128, dtype=np.float32).astype(BF)
    c["ones_row"] = np.ones((1, 512), dtype=BF)
    c["ones_col"] = np.ones((128, 1), dtype=BF)
    kk, qq = np.meshgrid(np.arange(128), np.arange(128), indexing="ij")
    c["mla_diag_mask"] = (kk <= qq).astype(BF)
    cc, ll = np.meshgrid(np.arange(128), np.arange(64), indexing="ij")
    c["lat_mask"] = (cc <= ll).astype(BF)
    D = 64
    inv = 1.0 / (10000.0 ** (np.arange(0, D, 2, dtype=np.float64) / D))
    ang = np.arange(T, dtype=np.float64)[:, None] * inv[None, :]
    ctab = np.empty((T, 64), np.float64)
    stab = np.empty((T, 64), np.float64)
    ctab[:, 0::2] = np.cos(ang); ctab[:, 1::2] = np.cos(ang)
    stab[:, 0::2] = np.sin(ang); stab[:, 1::2] = np.sin(ang)
    c["l1a_cos"] = ctab.astype(np.float32)
    c["l1a_sin"] = stab.astype(np.float32)
    DH, WIN = 64, 512
    half = np.arange(0, DH, 2, dtype=np.float64)
    invf = 1.0 / (10000.0 ** (half / DH))
    t = np.arange(T, dtype=np.float64)
    freqs = t[:, None] * invf[None, :]
    freqs = np.concatenate([freqs, freqs], 1)
    base = (half + 0.4 * DH) / (1.4 * DH)
    power = (t - T // 2) / (WIN // 2)
    scale = base[None, :] ** power[:, None]
    scale = np.concatenate([scale, scale], 1)
    cosf, sinf = np.cos(freqs), np.sin(freqs)
    c["xpos_qc"] = (cosf * scale * 8.0).astype(np.float32)
    c["xpos_qs"] = (sinf * scale * 8.0).astype(np.float32)
    c["xpos_kc"] = (cosf / scale).astype(np.float32)
    c["xpos_ks"] = (sinf / scale).astype(np.float32)
    ii, jj = np.meshgrid(np.arange(128), np.arange(128), indexing="ij")
    c["loc_mask_left"] = (jj >= ii).astype(BF)
    c["loc_mask_diag"] = (jj <= ii).astype(BF)
    return c


def _per_layer(name, arr, l):
    return np.ascontiguousarray(arr[l])


class K:

    def __init__(self, nc, prog, sb, dram, st, const_names):
        self.nc = nc
        self.p = prog
        self.sb = sb
        self.dram = dram
        self.t = st
        self.consts = const_names
        self.psb = [Buf(f"bank{i}") for i in range(4)]
        self.psb2 = [Buf(f"bank{i}b") for i in range(4)]
        self.smbuf = [Buf(f"sm{i}") for i in range(8)]

    def nat(self, phys, tt, width=C, dtype_cols=None):
        return self.t[phys][:, tt * width:(tt + 1) * width]

    def claim(self, phys, c0, c1, name):
        return self.sb.claim(phys, c0, c1, name)


def _ceil(a, b):
    return (a + b - 1) // b


def build_program(depth_stages=99, dbg=False, sub=99):
    nc = bass.Bass("TRN2", target_bir_lowering=False)
    stack = ExitStack()

    dram = {}

    def din(name, shape, dtype=F32):
        dram[name] = nc.dram_tensor(name, list(shape), dtype, kind="ExternalInput").ap()
        return dram[name]

    def dout(name, shape, dtype=F32):
        dram[name] = nc.dram_tensor(name, list(shape), dtype, kind="ExternalOutput").ap()
        return dram[name]

    din("x_in", (T, C))
    din("L0_W1", (C, C)); din("L0_W2", (C, C))
    din("L0_bias", (1, C)); din("L0_etT", (C, 32))
    din("g1", (1, C)); din("g2", (1, C))
    din("Wq", (C, C)); din("Wdkv", (C, 512)); din("Wuk", (512, C)); din("Wuv", (512, C))
    din("Wo", (C, C)); din("gate_w", (C, 8))
    din("moe_w1", (8, C, 2048)); din("moe_w2", (8, 2048, C))
    din("lat_qT", (16, 64, 64))
    din("WkT", (C, C)); din("WvT", (C, C)); din("WgT", (C, C)); din("WpT", (C, C))
    for l in range(2):
        din(f"wqkv{l}", (C, 1536))
        din(f"qs{l}", (1, 64)); din(f"ks{l}", (1, 64))
        din(f"wo{l}", (512, C))
        din(f"lg{l}", (1, C)); din(f"lb{l}", (1, C))
        din(f"w1a{l}", (C, 2816)); din(f"w1g{l}", (C, 2816))
        din(f"b1a{l}", (1, 2816)); din(f"b1g{l}", (1, 2816))
        din(f"w2p{l}", (2816, C)); din(f"b2_{l}", (1, C))
    cst = _host_constants()
    for k, v in cst.items():
        din("cst_" + k, v.shape, BF16 if v.dtype == BF else F32)
    dout("y", (T, C))
    dout("xspill", (T, C))
    dout("rowspill", (2, 2048))
    if dbg:
        dout("dbgBA", (128, 24576), BF16); dout("dbgBB", (128, 24576), BF16); dout("dbgBC", (128, 16640), BF16)
        dout("dbgSM", (128, 512)); dout("dbgXF", (128, 16384)); dout("dbgEW", (128, 4096), BF16)

    st = {}
    def sbuf(name, shape, dtype):
        st[name] = stack.enter_context(nc.sbuf_tensor(name, list(shape), dtype))
        return st[name]

    sbuf("XF", (128, NT * C), F32)
    sbuf("BA", (128, 24576), BF16)
    sbuf("BB", (128, 24576), BF16)
    sbuf("BC", (128, 16640), BF16)
    sbuf("EW", (128, 4096), BF16)
    sbuf("SM", (128, 512), F32)
    sbuf("CONST", (128, 2048), BF16)

    ps = {}
    for i in range(4):
        ps[i] = stack.enter_context(nc.psum_tensor(f"ps{i}", [128, 1024], F32))
    st["PS"] = ps

    semnames = ["c_" + e for e in ENGINES] + DMASEMS
    sems = {s: stack.enter_context(nc.semaphore(s)) for s in semnames}

    prog = Prog(nc)
    sb = SBMap()
    kk = K(nc, prog, sb, dram, st, cst)
    kk.dbg = dbg
    kk.sub = sub

    emit_forward(kk, depth_stages)

    with nc.Block() as blk:
        prog.render(blk, sems)
    stack.close()
    return nc


def emit_forward(k, stages):
    p, t, dram = k.p, k.t, k.dram

    x_tok = p.dma("sync", [(t["XF"][:, :].rearrange("p (a c) -> p a c", c=C),
                            dram["x_in"][:, :].rearrange("(a p) c -> p a c", p=128))],
                  tag="load_x")

    xbuf = k.claim("XF", 0, NT * C, "x")
    xbuf.w.append(x_tok)

    cl = _load_consts(k)

    st = 0
    if stages > st:
        emit_l0_attn(k, xbuf, cl)
    st += 1
    if stages > st:
        xbuf = emit_l0_ffn(k, xbuf, cl, sub=k.sub) or xbuf
    st += 1
    if stages > st:
        xbuf = emit_l1_attn(k, xbuf, cl) or xbuf
    st += 1
    if stages > st:
        xbuf = emit_l1_ffn(k, xbuf, cl) or xbuf
    st += 1

    if k.dbg:
        dt1 = p.dma("sync", [(dram["dbgBA"][:, :], t["BA"][:, :]),
                             (dram["dbgBB"][:, :], t["BB"][:, :]),
                             (dram["dbgSM"][:, :], t["SM"][:, :]),
                             (dram["dbgXF"][:, :], t["XF"][:, :]),
                             (dram["dbgEW"][:, :], t["EW"][:, :]),
                             (dram["dbgBC"][:, :], t["BC"][:, :])], deps=[("c_tensor", p.cnt["tensor"]), ("c_vector", p.cnt["vector"]), ("c_scalar", p.cnt["scalar"])], tag="dbgdump")
        p.step("sync", lambda e: e.nop(), deps=[dt1], tag="dbgwait")
    if not getattr(k, "y_done", False):
        otok = p.dma("sync", [(dram["y"][:, :].rearrange("(a p) c -> p a c", p=128),
                               t["XF"][:, :].rearrange("p (a c) -> p a c", c=C))],
                     deps=xbuf.wr_deps(), tag="store_y")
        xbuf.readers.append(otok)
        p.step("sync", lambda e: e.nop(), deps=[otok], tag="final_wait")


def _load_consts(k):
    p, t, dram = k.p, k.t, k.dram
    CB = t["CONST"]
    cl = {}
    pairs = []
    cl["ident_bf16"] = CB[:, 0:128]
    pairs.append((CB[:, 0:128], dram["cst_ident_bf16"][:, :]))
    cl["mla_diag"] = CB[:, 128:256]
    pairs.append((CB[:, 128:256], dram["cst_mla_diag_mask"][:, :]))
    cl["loc_left"] = CB[:, 256:384]
    pairs.append((CB[:, 256:384], dram["cst_loc_mask_left"][:, :]))
    cl["loc_diag"] = CB[:, 384:512]
    pairs.append((CB[:, 384:512], dram["cst_loc_mask_diag"][:, :]))
    cl["lat_mask"] = CB[:, 512:576]
    pairs.append((CB[:, 512:576], dram["cst_lat_mask"][:, :]))
    tok = p.dma("sync", pairs, tag="load_consts")
    cbuf = k.claim("CONST", 0, 2048, "consts")
    cbuf.w.append(tok)
    cl["cbuf"] = cbuf
    cl["ones_col"] = CB[:, 576:577]
    cl["ident_f32"] = CB[:, 1536:1792].bitcast(F32)
    tok2 = p.dma("sync", [(cl["ident_f32"], dram["cst_ident_f32"][:, :])], tag="load_identf32")
    cbuf.w.append(tok2)
    cl["ones_row"] = CB[0:1, 1024:1536]
    def ones_body(e):
        e.memset(CB[:, 576:577], 1.0)
        return e.memset(CB[0:1, 1024:1536], 1.0)
    otok = k.p.step("vector", ones_body, tag="ones")
    cbuf.w.append(otok)
    cl["cst_tok"] = [tok, tok2, otok]
    return cl


def _ln_stats_pair(k, xbuf, ttA, smA, ttB, smB, eps=1e-5, want_one_plus=False,
                   rms=False, xphys="XF"):
    p, t = k.p, k.t
    SM = t["SM"]
    def cols(base):
        return [SM[:, base + i:base + i + 1] for i in range(8)]
    A = cols(smA); B = cols(smB)
    xtA = t[xphys][:, ttA * C:(ttA + 1) * C]
    xtB = t[xphys][:, ttB * C:(ttB + 1) * C]
    junk = k.t["PS"][3][:, 512:1024]
    smbA = k.smbuf[(smA // 8) % 8]; smbB = k.smbuf[(smB // 8) % 8]
    wdeps = smbA.wr_deps() + smbB.wr_deps() + k.psb2[3].wr_deps()
    for b in (smbA, smbB):
        b.w = []; b.readers = []; b.init = []
    xd = xbuf.rd_deps()

    def v1(e):
        e.tensor_reduce(A[0], xtA, axis=AX.X, op=ALU.add)
        return e.tensor_reduce(B[0], xtB, axis=AX.X, op=ALU.add)
    t1 = p.step("vector", v1, deps=xd + wdeps, tag=f"lnV1_{ttA}")

    def s1(e):
        e.activation(junk, xtA[:, 0:512], ACT.Square, accum_out=A[1])
        e.activation(junk, xtA[:, 512:1024], ACT.Square, accum_out=A[4])
        e.activation(junk, xtB[:, 0:512], ACT.Square, accum_out=B[1])
        return e.activation(junk, xtB[:, 512:1024], ACT.Square, accum_out=B[4])
    t2 = p.step("scalar", s1, deps=xd + wdeps, tag=f"lnS1_{ttA}")
    k.psb2[3].w.append(t2)
    xbuf.readers += [t1, t2]

    def v2(e):
        e.tensor_tensor(A[4], A[1], A[4], ALU.add)
        e.tensor_tensor(B[4], B[1], B[4], ALU.add)
        if not rms:
            e.tensor_scalar(A[2], A[0], 1.0 / C, None, ALU.mult)
            e.tensor_scalar(B[2], B[0], 1.0 / C, None, ALU.mult)
            e.tensor_scalar(A[3], A[0], -1.0, None, ALU.mult)
            e.tensor_scalar(B[3], B[0], -1.0, None, ALU.mult)
            e.scalar_tensor_tensor(A[4], A[2], A[3], A[4], ALU.mult, ALU.add)
            e.scalar_tensor_tensor(B[4], B[2], B[3], B[4], ALU.mult, ALU.add)
        e.tensor_scalar(A[1], A[4], 1.0 / C, eps, ALU.mult, ALU.add)
        return e.tensor_scalar(B[1], B[4], 1.0 / C, eps, ALU.mult, ALU.add)
    t3 = p.step("vector", v2, deps=[t1, t2], tag=f"lnV2_{ttA}")

    def s2(e):
        e.activation(A[0], A[1], ACT.Sqrt)
        return e.activation(B[0], B[1], ACT.Sqrt)
    t4 = p.step("scalar", s2, deps=[t3], tag=f"lnS2_{ttA}")

    def v3(e):
        e.reciprocal(A[5], A[0])
        e.reciprocal(B[5], B[0])
        e.tensor_tensor(A[4], A[5], A[5], ALU.mult)
        e.tensor_tensor(B[4], B[5], B[5], ALU.mult)
        e.tensor_tensor(A[4], A[4], A[1], ALU.mult)
        e.tensor_tensor(B[4], B[4], B[1], ALU.mult)
        e.tensor_scalar(A[4], A[4], -0.5, 1.5, ALU.mult, ALU.add)
        e.tensor_scalar(B[4], B[4], -0.5, 1.5, ALU.mult, ALU.add)
        e.tensor_tensor(A[5], A[5], A[4], ALU.mult)
        last = e.tensor_tensor(B[5], B[5], B[4], ALU.mult)
        if not rms:
            e.tensor_scalar(A[6], A[2], -1.0, None, ALU.mult)
            e.tensor_scalar(B[6], B[2], -1.0, None, ALU.mult)
            e.tensor_tensor(A[6], A[6], A[5], ALU.mult)
            last = e.tensor_tensor(B[6], B[6], B[5], ALU.mult)
        if want_one_plus:
            e.tensor_scalar(A[7], A[5], 1.0, None, ALU.add)
            last = e.tensor_scalar(B[7], B[5], 1.0, None, ALU.add)
        return last
    t5 = p.step("vector", v3, deps=[t4], tag=f"lnV3_{ttA}")
    for b, toks in ((smbA, [t1, t2, t3, t4, t5]), (smbB, [t1, t2, t3, t4, t5])):
        b.w = list(toks)
    return (A[5], A[6], A[7], smbA), (B[5], B[6], B[7], smbB), [t5]


def emit_l0_attn(k, xbuf, cl):
    p, t, dram = k.p, k.t, k.dram
    BA, BB, BC, EW, CB = t["BA"], t["BB"], t["BC"], t["EW"], t["CONST"]
    PS = t["PS"]
    psb = k.psb

    w1buf = k.claim("BB", 0, 8192, "l0_w1")
    w2buf = k.claim("BB", 8192, 16384, "l0_w2")
    biasbuf = k.claim("BB", 18944, 19968, "l0_bias")
    etbuf = k.claim("EW", 2048, 2560, "l0_etT")
    etT = EW[:, 0:4096].bitcast(F32)[:, 1024:1280]
    tok = p.dma("gpsimd", [
        (BB[:, 0:8192].rearrange("p (ct o) -> p ct o", o=C),
         dram["L0_W1"][:, :].rearrange("(ct p) o -> p ct o", p=128)),
        (BB[:, 8192:16384].rearrange("p (ct o) -> p ct o", o=C),
         dram["L0_W2"][:, :].rearrange("(ct p) o -> p ct o", p=128)),
        (BB[0:1, 18944:19968], dram["L0_bias"][:, :]),
    ], deps=w1buf.init + w2buf.init + biasbuf.init, tag="l0_wload")
    w1buf.w.append(tok); w2buf.w.append(tok); biasbuf.w.append(tok)
    tok2 = p.dma("sync", [
        (etT.rearrange("p (ct m) -> p ct m", m=32),
         dram["L0_etT"][:, :].rearrange("(ct p) m -> p ct m", p=128)),
    ], deps=etbuf.init, tag="l0_etload")
    etbuf.w.append(tok2)

    xnT = k.claim("BA", 0, 16384, "l0_xnT")
    xnbuf = [k.claim("EW", 0, 1024, "l0_xn_e"), k.claim("EW", 1024, 2048, "l0_xn_o")]
    stats = {}
    for tt in range(0, NT, 2):
        a, b, stok = _ln_stats_pair(k, xbuf, tt, 384 + tt * 8, tt + 1, 384 + (tt + 1) * 8)
        stats[tt] = (a, stok); stats[tt + 1] = (b, stok)
    for tt in range(NT):
        (rstd, negmur, _opr, smb), stok = stats[tt]
        xn_cols = (tt % 2) * 1024
        xnt_ap = EW[:, xn_cols:xn_cols + 1024]
        eb = xnbuf[tt % 2]
        xt = t["XF"][:, tt * C:(tt + 1) * C]
        tok = p.step("scalar",
                     lambda e, _o=xnt_ap, _x=xt, _r=rstd, _b=negmur:
                     e.activation(_o, _x, ACT.Identity, scale=_r, bias=_b),
                     deps=stok + eb.wr_deps() + xbuf.rd_deps(), tag=f"l0xn{tt}")
        smb.readers.append(tok)
        eb.w = [tok]; eb.readers = []
        xbuf.readers.append(tok)
        bank = psb[tt % 4]
        pb16 = PS[tt % 4][:, 0:512].bitcast(BF16)

        def tbody(e, _src=xnt_ap, _pb=pb16):
            last = None
            for ct2 in range(NC):
                last = e.transpose(_pb[:, ct2 * 128:(ct2 + 1) * 128],
                                  _src[:, ct2 * 128:(ct2 + 1) * 128], cl["ident_bf16"])
            return last
        ttok = p.step("tensor", tbody,
                      deps=[tok] + bank.wr_deps() + cl["cst_tok"], tag=f"l0tr{tt}")
        eb.readers.append(ttok)
        bank.w = [ttok]; bank.readers = []

        def ebody(e, _pb=pb16, _tt=tt):
            dst = BA[:, 0:16384].rearrange("p (ct t) -> p ct t", t=T)[:, :, _tt * 128:(_tt + 1) * 128]
            src = _pb.rearrange("p (ct t) -> p ct t", t=128)
            return e.tensor_copy(dst, src)
        etok = p.step("vector", ebody, deps=[ttok] + (xnT.init if tt == 0 else []), tag=f"l0te{tt}")
        bank.readers.append(etok)
        xnT.w.append(etok)

    hbuf = k.claim("BC", 0, 16384, "l0_hT")
    diagbuf = k.claim("BB", 16384, 18944, "l0_diag")
    for ct in range(NC):
        def dbody(e, _ct=ct):
            last = None
            for m in range(1, 21):
                last = e.tensor_scalar(BB[:, 16384 + (m - 1) * 128:16384 + m * 128],
                                       cl["ident_bf16"], etT[:, _ct * 32 + m:_ct * 32 + m + 1],
                                       None, ALU.mult)
            return last
        dtok = p.step("vector", dbody,
                      deps=diagbuf.wr_deps() + etbuf.rd_deps() + cl["cst_tok"], tag=f"l0dg{ct}")
        diagbuf.w = [dtok]; diagbuf.readers = []
        for j in range(4):
            bank = psb[j % 4]
            pbank = PS[j % 4][:, 0:512]

            def cbody(e, _ct=ct, _j=j, _pbank=pbank):
                last = None
                for i, m in enumerate(range(1, 21)):
                    lo = _j * 512 - m
                    if lo < 0:
                        rhs = BA[:, _ct * T + 0:_ct * T + 512 - m]
                        out = _pbank[:, m:512]
                    else:
                        rhs = BA[:, _ct * T + lo:_ct * T + lo + 512]
                        out = _pbank[:, 0:512]
                    last = e.matmul(out, BB[:, 16384 + (m - 1) * 128:16384 + m * 128],
                                    rhs, start=(i == 0), stop=(i == 19))
                return last
            mtok = p.step("tensor", cbody,
                          deps=[dtok] + xnT.rd_deps() + bank.wr_deps(), tag=f"l0cv{ct}_{j}")
            diagbuf.readers.append(mtok)
            xnT.readers.append(mtok)
            bank.w = [mtok]; bank.readers = []

            if j == 0:
                ztok = p.step("vector", lambda e, _pb=pbank: e.memset(_pb[:, 0:1], 0.0),
                              deps=[mtok], tag=f"l0cz{ct}")
                bank.w.append(ztok)

            def evbody(e, _ct=ct, _j=j, _pb=pbank):
                return e.copy(BC[:, _ct * T + _j * 512:_ct * T + (_j + 1) * 512], _pb)
            etok2 = p.step("scalar", evbody,
                           deps=bank.w + (hbuf.init if (ct == 0 and j == 0) else []),
                           tag=f"l0ce{ct}_{j}")
            bank.readers.append(etok2)
            hbuf.w.append(etok2)

    for tt in range(NT):
        for oc in range(2):
            idx = tt * 2 + oc
            bank = psb[idx % 4]
            pbank = PS[idx % 4][:, oc * 512:oc * 512 + 512] if False else PS[idx % 4][:, 0:512]

            def mbody(e, _tt=tt, _oc=oc, _pb=pbank):
                last = None
                n = 0
                for ct in range(NC):
                    last = e.matmul(_pb, BA[:, ct * T + _tt * 128:ct * T + _tt * 128 + 128],
                                    BB[:, ct * C + _oc * 512:ct * C + _oc * 512 + 512],
                                    start=(n == 0), stop=False)
                    n += 1
                for ct in range(NC):
                    last = e.matmul(_pb, BC[:, ct * T + _tt * 128:ct * T + _tt * 128 + 128],
                                    BB[:, 8192 + ct * C + _oc * 512:8192 + ct * C + _oc * 512 + 512],
                                    start=False, stop=False)
                    n += 1
                last = e.matmul(_pb, cl["ones_row"][:, 0:128],
                                BB[0:1, 18944 + _oc * 512:18944 + _oc * 512 + 512],
                                start=False, stop=True)
                return last
            mtok = p.step("tensor", mbody,
                          deps=xnT.rd_deps() + hbuf.rd_deps() + w1buf.rd_deps() +
                               w2buf.rd_deps() + biasbuf.rd_deps() + bank.wr_deps() + cl["cst_tok"],
                          tag=f"l0o{tt}_{oc}")
            for b in (xnT, hbuf, w1buf, w2buf, biasbuf):
                b.readers.append(mtok)
            bank.w = [mtok]; bank.readers = []

            def abody(e, _tt=tt, _oc=oc, _pb=pbank):
                xs = t["XF"][:, _tt * C + _oc * 512:_tt * C + _oc * 512 + 512]
                return e.tensor_tensor(xs, _pb, xs, ALU.add)
            atok = p.step("vector", abody, deps=[mtok] + xbuf.wr_deps(), tag=f"l0a{tt}_{oc}")
            bank.readers.append(atok)
            xbuf.w.append(atok)


def _rsqrt_pair(k, p, ina, inb, outa, outb, scra, scrb, deps, tag):
    s1 = p.step("scalar", lambda e: (e.activation(scra, ina, ACT.Sqrt),
                                     e.activation(scrb, inb, ACT.Sqrt))[-1],
                deps=deps, tag=tag + "_s")

    def v(e):
        e.reciprocal(outa, scra)
        e.reciprocal(outb, scrb)
        e.tensor_tensor(scra, outa, outa, ALU.mult)
        e.tensor_tensor(scrb, outb, outb, ALU.mult)
        e.tensor_tensor(scra, scra, ina, ALU.mult)
        e.tensor_tensor(scrb, scrb, inb, ALU.mult)
        e.tensor_scalar(scra, scra, -0.5, 1.5, ALU.mult, ALU.add)
        e.tensor_scalar(scrb, scrb, -0.5, 1.5, ALU.mult, ALU.add)
        e.tensor_tensor(outa, outa, scra, ALU.mult)
        return e.tensor_tensor(outb, outb, scrb, ALU.mult)
    return p.step("vector", v, deps=[s1], tag=tag + "_v")


def emit_l0_ffn(k, xbuf, cl, sub=99):
    p, t, dram = k.p, k.t, k.dram
    BA, BB, BC, EW, CB, SM = t["BA"], t["BB"], t["BC"], t["EW"], t["CONST"], t["SM"]
    PS = t["PS"]
    psb = k.psb
    XFb = t["XF"][:, :].bitcast(BF16)

    wqbuf = k.claim("BC", 0, 8192, "wq")
    wdkvbuf = k.claim("BC", 8192, 12288, "wdkv")
    tok = p.dma("gpsimd", [
        (BC[:, 0:8192].rearrange("p (ct o) -> p ct o", o=C),
         dram["Wq"][:, :].rearrange("(ct p) o -> p ct o", p=128)),
        (BC[:, 8192:12288].rearrange("p (ct o) -> p ct o", o=512),
         dram["Wdkv"][:, :].rearrange("(ct p) o -> p ct o", p=128)),
    ], deps=wqbuf.init + wdkvbuf.init, tag="l0f_wq")
    wqbuf.w.append(tok); wdkvbuf.w.append(tok)
    g1buf = k.claim("SM", 96, 112, "g1g2")
    tok = p.dma("sync", [
        (SM[:, 96:104], dram["g1"][0, :].rearrange("(ct p) -> p ct", p=128)),
        (SM[:, 104:112], dram["g2"][0, :].rearrange("(ct p) -> p ct", p=128)),
    ], deps=g1buf.init, tag="l0f_g12", slow=True)
    g1buf.w.append(tok)

    def gsc(e):
        last = None
        for ct in range(NC):
            g = SM[:, 96 + ct:97 + ct]
            last = e.tensor_scalar(BC[:, ct * C:(ct + 1) * C], BC[:, ct * C:(ct + 1) * C], g, None, ALU.mult)
            last = e.tensor_scalar(BC[:, 8192 + ct * 512:8192 + (ct + 1) * 512],
                                   BC[:, 8192 + ct * 512:8192 + (ct + 1) * 512], g, None, ALU.mult)
        return last
    gtok = p.step("vector", gsc, deps=wqbuf.rd_deps() + wdkvbuf.rd_deps() + g1buf.rd_deps(), tag="l0f_gsc")
    wqbuf.w.append(gtok); wdkvbuf.w.append(gtok)
    g1buf.readers.append(gtok)

    hT = k.claim("BA", 0, 16384, "l0f_hT")
    unbuf = [k.claim("EW", 0, 1024, "un_e"), k.claim("EW", 1024, 2048, "un_o")]
    stok_all = []
    st_info = {}
    for tt in range(0, NT, 2):
        bA, bB2 = 128 + tt * 16, 128 + (tt + 1) * 16
        a, b, stok = _ln_stats_pair(k, xbuf, tt, bA, tt + 1, bB2)
        A = lambda i, _b=bA: SM[:, _b + i:_b + i + 1]
        B = lambda i, _b=bB2: SM[:, _b + i:_b + i + 1]

        def v4(e, A=A, B=B):
            e.reciprocal(A(8), A(1))
            e.reciprocal(B(8), B(1))
            e.tensor_scalar(A(8), A(8), -1e-5, 1.0 + 1e-6, ALU.mult, ALU.add)
            return e.tensor_scalar(B(8), B(8), -1e-5, 1.0 + 1e-6, ALU.mult, ALU.add)
        t6 = p.step("vector", v4, deps=stok, tag=f"l0f_v4_{tt}")
        t7 = _rsqrt_pair(k, p, A(8), B(8), A(9), B(9), A(15), B(15), [t6], f"l0f_rr_{tt}")

        def v5(e, A=A, B=B):
            e.tensor_tensor(A(10), A(5), A(9), ALU.mult)
            e.tensor_tensor(B(10), B(5), B(9), ALU.mult)
            e.tensor_tensor(A(11), A(6), A(9), ALU.mult)
            return e.tensor_tensor(B(11), B(6), B(9), ALU.mult)
        t8 = p.step("vector", v5, deps=[t7], tag=f"l0f_v5_{tt}")
        stok_all.append(t8)
        st_info[tt] = (A, [t8]); st_info[tt + 1] = (B, [t8])

    for tt in range(NT):
        A, stok = st_info[tt]
        xt = t["XF"][:, tt * C:(tt + 1) * C]
        eb = unbuf[tt % 2]
        un_ap = EW[:, (tt % 2) * 1024:(tt % 2) * 1024 + 1024]
        tok = p.step("scalar",
                     lambda e, _o=un_ap, _x=xt, _r=A(10), _b=A(11):
                     e.activation(_o, _x, ACT.Identity, scale=_r, bias=_b),
                     deps=stok + eb.wr_deps() + xbuf.rd_deps(), tag=f"l0f_un{tt}")
        eb.w = [tok]; eb.readers = []
        xbuf.readers.append(tok)
        bank = psb[tt % 4]
        pb16 = PS[tt % 4][:, 0:512].bitcast(BF16)

        def tbody(e, _src=un_ap, _pb=pb16):
            last = None
            for ct2 in range(NC):
                last = e.transpose(_pb[:, ct2 * 128:(ct2 + 1) * 128],
                                   _src[:, ct2 * 128:(ct2 + 1) * 128], cl["ident_bf16"])
            return last
        ttok = p.step("tensor", tbody, deps=[tok] + bank.wr_deps() + cl["cst_tok"], tag=f"l0f_tr{tt}")
        eb.readers.append(ttok)
        bank.w = [ttok]; bank.readers = []

        def ebody(e, _pb=pb16, _tt=tt):
            dst = BA[:, 0:16384].rearrange("p (ct t) -> p ct t", t=T)[:, :, _tt * 128:(_tt + 1) * 128]
            return e.tensor_copy(dst, _pb.rearrange("p (ct t) -> p ct t", t=128))
        etok = p.step("vector", ebody, deps=[ttok] + (hT.init if tt == 0 else []), tag=f"l0f_te{tt}")
        bank.readers.append(etok)
        hT.w.append(etok)

    if sub < 1:
        return xbuf
    sp_tok = p.dma("sync", [(dram["xspill"][:, :].rearrange("(a p) c -> p a c", p=128),
                             t["XF"][:, :].rearrange("p (a c) -> p a c", c=C))],
                   deps=xbuf.wr_deps(), tag="l0f_spill")
    xbuf.readers.append(sp_tok)

    qtbuf = k.claim("XF", 0, 8192, "l0f_QT")
    ktbuf = k.claim("XF", 8192, 16384, "l0f_KT")
    QT = lambda c0, c1: XFb[:, c0:c1]
    for dqt in range(NC):
        for ch in range(4):
            idx = dqt * 4 + ch
            bank = psb[idx % 4]
            pbank = PS[idx % 4][:, 0:512]

            def mb(e, _d=dqt, _c=ch, _pb=pbank):
                last = None
                for ct in range(NC):
                    last = e.matmul(_pb, BC[:, ct * C + _d * 128:ct * C + _d * 128 + 128],
                                    BA[:, ct * T + _c * 512:ct * T + (_c + 1) * 512],
                                    start=(ct == 0), stop=(ct == NC - 1))
                return last
            mtok = p.step("tensor", mb, deps=wqbuf.rd_deps() + hT.rd_deps() + bank.wr_deps(),
                          tag=f"l0f_qt{dqt}_{ch}")
            wqbuf.readers.append(mtok); hT.readers.append(mtok)
            bank.w = [mtok]; bank.readers = []

            def ev(e, _d=dqt, _c=ch, _pb=pbank):
                return e.activation(XFb[:, _d * 2048 + _c * 512:_d * 2048 + (_c + 1) * 512],
                                    _pb, ACT.Identity, scale=0.125)
            etok = p.step("scalar", ev, deps=[mtok] + (qtbuf.init if idx == 0 else []), tag=f"l0f_qte{dqt}_{ch}")
            bank.readers.append(etok)
            qtbuf.w.append(etok)

    ctbuf = k.claim("BA", 16384, 24576, "l0f_cT")
    for lt in range(4):
        for ch in range(4):
            idx = lt * 4 + ch
            bank = psb[idx % 4]
            pbank = PS[idx % 4][:, 0:512]

            def mb(e, _l=lt, _c=ch, _pb=pbank):
                last = None
                for ct in range(NC):
                    last = e.matmul(_pb, BC[:, 8192 + ct * 512 + _l * 128:8192 + ct * 512 + _l * 128 + 128],
                                    BA[:, ct * T + _c * 512:ct * T + (_c + 1) * 512],
                                    start=(ct == 0), stop=(ct == NC - 1))
                return last
            mtok = p.step("tensor", mb, deps=wdkvbuf.rd_deps() + hT.rd_deps() + bank.wr_deps(),
                          tag=f"l0f_ct{lt}_{ch}")
            wdkvbuf.readers.append(mtok); hT.readers.append(mtok)
            bank.w = [mtok]; bank.readers = []

            def ev(e, _l=lt, _c=ch, _pb=pbank):
                return e.copy(BA[:, 16384 + _l * 2048 + _c * 512:16384 + _l * 2048 + (_c + 1) * 512], _pb)
            etok = p.step("scalar", ev, deps=[mtok] + (ctbuf.init if idx == 0 else []), tag=f"l0f_cte{lt}_{ch}")
            bank.readers.append(etok)
            ctbuf.w.append(etok)

    wukbuf = k.claim("BC", 12288, 16384, "wuk")
    tok = p.dma("gpsimd", [
        (BC[:, 12288:16384].rearrange("p (lt o) -> p lt o", o=C),
         dram["Wuk"][:, :].rearrange("(lt p) o -> p lt o", p=128)),
    ], deps=wukbuf.init, tag="l0f_wuk")
    wukbuf.w.append(tok)
    wuvbuf = k.claim("BC", 0, 4096, "wuv")
    tok = p.dma("gpsimd", [
        (BC[:, 0:4096].rearrange("p (lt o) -> p lt o", o=C),
         dram["Wuv"][:, :].rearrange("(lt p) o -> p lt o", p=128)),
    ], deps=wuvbuf.init, tag="l0f_wuv")
    wuvbuf.w.append(tok)

    for dkt in range(NC):
        for ch in range(4):
            idx = dkt * 4 + ch
            bank = psb[idx % 4]
            pbank = PS[idx % 4][:, 0:512]

            def mb(e, _d=dkt, _c=ch, _pb=pbank):
                last = None
                for lt in range(4):
                    last = e.matmul(_pb, BC[:, 12288 + lt * C + _d * 128:12288 + lt * C + _d * 128 + 128],
                                    BA[:, 16384 + lt * T + _c * 512:16384 + lt * T + (_c + 1) * 512],
                                    start=(lt == 0), stop=(lt == 3))
                return last
            mtok = p.step("tensor", mb, deps=wukbuf.rd_deps() + ctbuf.rd_deps() + bank.wr_deps(),
                          tag=f"l0f_kt{dkt}_{ch}")
            wukbuf.readers.append(mtok); ctbuf.readers.append(mtok)
            bank.w = [mtok]; bank.readers = []

            def ev(e, _d=dkt, _c=ch, _pb=pbank):
                return e.copy(XFb[:, 16384 + _d * 2048 + _c * 512:16384 + _d * 2048 + (_c + 1) * 512], _pb)
            etok = p.step("scalar", ev, deps=[mtok] + (ktbuf.init if idx == 0 else []), tag=f"l0f_kte{dkt}_{ch}")
            bank.readers.append(etok)
            ktbuf.w.append(etok)

    vbuf = k.claim("BA", 0, 16384, "l0f_V")
    VW = 1024
    for tt in range(NT):
        for dc in range(2):
            idx = tt * 2 + dc
            bank = psb[idx % 4]
            pbank = PS[idx % 4][:, 0:512]

            def mb(e, _tt=tt, _dc=dc, _pb=pbank):
                last = None
                for lt in range(4):
                    last = e.matmul(_pb, BA[:, 16384 + lt * T + _tt * 128:16384 + lt * T + _tt * 128 + 128],
                                    BC[:, lt * C + _dc * 512:lt * C + (_dc + 1) * 512],
                                    start=(lt == 0), stop=(lt == 3))
                return last
            mtok = p.step("tensor", mb, deps=wuvbuf.rd_deps() + ctbuf.rd_deps() + bank.wr_deps(),
                          tag=f"l0f_v{tt}_{dc}")
            wuvbuf.readers.append(mtok); ctbuf.readers.append(mtok)
            bank.w = [mtok]; bank.readers = []

            def ev(e, _tt=tt, _dc=dc, _pb=pbank):
                return e.copy(BA[:, _tt * VW + _dc * 512:_tt * VW + (_dc + 1) * 512], _pb)
            etok = p.step("scalar", ev, deps=[mtok] + (vbuf.init if idx == 0 else []), tag=f"l0f_ve{tt}_{dc}")
            bank.readers.append(etok)
            vbuf.w.append(etok)

    if sub < 3:
        return xbuf
    obuf = k.claim("BB", 0, 16384, "l0f_o")
    ptbufs = [k.claim("BA", 16384, 24576, "l0f_PTa"), k.claim("BB", 16384, 24576, "l0f_PTb")]
    PTT = [BA, BB]
    dcolbuf = k.claim("SM", 112, 128, "l0f_dcol")
    mla_items = [(h, qc) for h in range(16) for qc in range(4)]

    def mla_stage1(h, qc, par):
        ptbuf = ptbufs[par]
        PTX = PTT[par]
        pb = (h % 2) * 64
        kth = 16384 + (h // 2) * 2048
        qth = (h // 2) * 2048
        nkt = qc * 4 + 4
        pt_toks = []
        for kt in range(nkt):
            off = max(0, (kt - qc * 4)) * 128
            bank = psb[kt % 2]
            pbank = PS[kt % 2][:, 0:512]

            def mb(e, _kt=kt, _pb=pbank, _pbase=pb, _kth=kth, _qth=qth, _qc=qc, _o=off):
                return e.matmul(_pb[:, _o:512], XFb[_pbase:_pbase + 64, _kth + _kt * 128:_kth + (_kt + 1) * 128],
                                XFb[_pbase:_pbase + 64, _qth + _qc * 512 + _o:_qth + (_qc + 1) * 512],
                                start=True, stop=True)
            mtok = p.step("tensor", mb, deps=ktbuf.rd_deps() + qtbuf.rd_deps() + bank.wr_deps(),
                          tag=f"st{h}_{qc}_{kt}")
            ktbuf.readers.append(mtok); qtbuf.readers.append(mtok)
            bank.w = [mtok]; bank.readers = []

            def ev(e, _kt=kt, _pb=pbank, _o=off, _P=PTX):
                return e.activation(_P[:, 16384 + _kt * 512 + _o:16384 + (_kt + 1) * 512], _pb[:, _o:512], ACT.Exp)
            etok = p.step("scalar", ev, deps=[mtok] + (ptbuf.wr_deps() if kt == 0 else []),
                          tag=f"ste{h}_{qc}_{kt}")
            bank.readers.append(etok)
            if kt == 0:
                ptbuf.w = [etok]; ptbuf.readers = []
            else:
                ptbuf.w.append(etok)
            pt_toks.append(etok)
        def dmask(e, _qc=qc, _P=PTX):
            last = None
            for j in range(4):
                kt = _qc * 4 + j
                ap = _P[:, 16384 + kt * 512 + j * 128:16384 + kt * 512 + (j + 1) * 128]
                last = e.tensor_tensor(ap, ap, cl["mla_diag"], ALU.mult)
            return last
        mtok2 = p.step("vector", dmask, deps=pt_toks[qc * 4:] + cl["cst_tok"], tag=f"msk{h}_{qc}")
        ptbuf.w.append(mtok2)
        return (h, qc, par)

    def mla_stage2(state):
        h, qc, par = state
        ptbuf = ptbufs[par]
        PTX = PTT[par]
        pv_toks = []
        for j in range(4):
            jg = qc * 4 + j
            bank = psb[2 + (j % 2)]
            pbank = PS[2 + (j % 2)][:, 0:128]

            def pv(e, _j=j, _jg=jg, _h=h, _pb=pbank, _P=PTX):
                last = None
                for kt2 in range(_jg + 1):
                    pt_ap = _P[:, 16384 + kt2 * 512 + _j * 128:16384 + kt2 * 512 + (_j + 1) * 128]
                    e.matmul(_pb[:, 0:64], pt_ap,
                             BA[:, kt2 * VW + _h * 64:kt2 * VW + (_h + 1) * 64],
                             start=(kt2 == 0), stop=(kt2 == _jg))
                    last = e.matmul(_pb[:, 64:65], pt_ap, cl["ones_col"],
                                    start=False, stop=(kt2 == _jg))
                return last
            pvtok = p.step("tensor", pv, deps=ptbuf.rd_deps() + vbuf.rd_deps() + bank.wr_deps() + cl["cst_tok"],
                           tag=f"pv{h}_{jg}")
            ptbuf.readers.append(pvtok); vbuf.readers.append(pvtok)
            pv_toks.append(pvtok)
            bank.w = [pvtok]; bank.readers = []

            dcol = SM[:, 112 + (jg % 2) * 2:113 + (jg % 2) * 2]
            d1tok = p.step("vector", lambda e, _pb=pbank, _d=dcol: e.reciprocal(_d, _pb[:, 64:65]),
                           deps=[pvtok], tag=f"dr{h}_{jg}")
            def dv(e, _pb=pbank, _jg=jg, _h=h, _d=dcol):
                return e.tensor_scalar(BB[:, _jg * C + _h * 64:_jg * C + (_h + 1) * 64],
                                       _pb[:, 0:64], _d, None, ALU.mult)
            dtok = p.step("vector", dv, deps=[d1tok] + (obuf.init if (h == 0 and jg == 0) else []),
                          tag=f"dv{h}_{jg}")
            bank.readers.append(dtok)
            obuf.w.append(dtok)
        return pv_toks

    mpend = None
    for i2, (h, qc) in enumerate(mla_items):
        s1 = mla_stage1(h, qc, i2 % 2)
        if mpend is not None:
            mla_stage2(mpend)
        mpend = s1
    mla_stage2(mpend)

    wobuf = k.claim("BC", 4096, 12288, "wo")
    tok = p.dma("gpsimd", [
        (BC[:, 4096:12288].rearrange("p (ot o) -> p ot o", o=C),
         dram["Wo"][:, :].rearrange("(ot p) o -> p ot o", p=128)),
    ], deps=wobuf.init, tag="l0f_wo")
    wobuf.w.append(tok)
    s1buf = k.claim("BA", 0, 16384, "l0f_s1")
    xqs = []
    pbank2 = lambda i, cc: PS[i][:, cc * 512:(cc + 1) * 512]
    for tq in range(4):
        xq = k.claim("XF", tq * 4096, (tq + 1) * 4096, f"x_q{tq}")
        xqs.append(xq)
        rtok = p.dma("sync", [(t["XF"][:, tq * 4096:(tq + 1) * 4096].rearrange("p (a c) -> p a c", c=C),
                              dram["xspill"][:, :].rearrange("(a p) c -> p a c", p=128)[:, tq * 4:(tq + 1) * 4, :])],
                     deps=xq.init + [sp_tok], tag=f"l0f_xr{tq}")
        xq.w.append(rtok)
        otq = k.claim("EW", 0, 4096, f"l0f_oT{tq}")
        for g in range(4):
            bankt = psb[g % 2]
            ptb = PS[g % 2][:, 0:512].bitcast(BF16)

            def trb(e, _tq=tq, _g=g, _pb=ptb):
                last = None
                for z in range(8):
                    ot = _g * 2 + z // 4
                    i = z % 4
                    last = e.transpose(_pb[:, z * 128:(z + 1) * 128],
                                       BB[:, (_tq * 4 + i) * C + ot * 128:(_tq * 4 + i) * C + (ot + 1) * 128],
                                       cl["ident_bf16"])
                return last
            ttok = p.step("tensor", trb, deps=obuf.rd_deps() + bankt.wr_deps() + cl["cst_tok"],
                          tag=f"woT{tq}_{g}")
            obuf.readers.append(ttok)
            bankt.w = [ttok]; bankt.readers = []

            def tev(e, _g=g, _pb=ptb):
                return e.tensor_copy(EW[:, _g * 1024:(_g + 1) * 1024], _pb)
            etok = p.step("vector", tev, deps=[ttok] + (otq.wr_deps() if g < 2 else []), tag=f"woTe{tq}_{g}")
            bankt.readers.append(etok)
            otq.w.append(etok)
        bdeps = []
        for i in range(4):
            for cc in range(2):
                bdeps += (psb[i].wr_deps() if cc == 0 else k.psb2[i].wr_deps())

        def mmb(e, _tq=tq):
            last = None
            for ot in range(NC):
                for i in range(4):
                    for cc in range(2):
                        last = e.matmul(pbank2(i, cc),
                                        EW[:, ot * 512 + i * 128:ot * 512 + (i + 1) * 128],
                                        BC[:, 4096 + ot * C + cc * 512:4096 + ot * C + (cc + 1) * 512],
                                        start=(ot == 0), stop=(ot == NC - 1))
            return last
        mmtok = p.step("tensor", mmb, deps=otq.rd_deps() + wobuf.rd_deps() + bdeps, tag=f"woMM{tq}")
        otq.readers.append(mmtok); wobuf.readers.append(mmtok)
        for i in range(4):
            psb[i].w = [mmtok]; psb[i].readers = []
            k.psb2[i].w = [mmtok]; k.psb2[i].readers = []

        EWF = EW[:, 0:4096].bitcast(F32)
        if tq == 0:
            k.sfslot = [Buf("sf0"), Buf("sf1")]
        for i in range(4):
            tt = tq * 4 + i
            A, _ = st_info[tt]
            slot = k.sfslot[i % 2]
            sf = [EWF[:, (i % 2) * 1024:(i % 2) * 1024 + 512],
                  EWF[:, (i % 2) * 1024 + 512:(i % 2) * 1024 + 1024]]

            def ve(e, _i=i, _tt=tt, _A=A, _sf=sf):
                xs = [t["XF"][:, _tt * C:_tt * C + 512], t["XF"][:, _tt * C + 512:(_tt + 1) * C]]
                e.tensor_scalar(_sf[0], xs[0], _A(5), _A(6), ALU.mult, ALU.add)
                e.tensor_scalar(_sf[1], xs[1], _A(5), _A(6), ALU.mult, ALU.add)
                e.tensor_tensor(_sf[0], _sf[0], pbank2(_i, 0), ALU.add)
                e.tensor_tensor(_sf[1], _sf[1], pbank2(_i, 1), ALU.add)
                e.tensor_tensor(xs[0], xs[0], _sf[0], ALU.add)
                return e.tensor_tensor(xs[1], xs[1], _sf[1], ALU.add)
            vdeps = [mmtok] + xq.wr_deps() + otq.rd_deps() + st_info[tt][1] + slot.wr_deps()
            vtok = p.step("vector", ve, deps=vdeps, tag=f"l0f_s1v{tt}")
            xq.w.append(vtok)
            otq.readers.append(vtok)
            slot.w = [vtok]; slot.readers = []; slot.init = []
            psb[i].readers.append(vtok); k.psb2[i].readers.append(vtok)

            def se(e, _tt=tt, _A=A, _sf=sf):
                e.copy(BA[:, _tt * C:_tt * C + 512], _sf[0])
                e.copy(BA[:, _tt * C + 512:(_tt + 1) * C], _sf[1])
                e.activation(_sf[0], _sf[0], ACT.Square, accum_out=_A(12))
                return e.activation(_sf[1], _sf[1], ACT.Square, accum_out=_A(13))
            sdeps = [vtok] + (s1buf.init if tt == 0 else [])
            stok2 = p.step("scalar", se, deps=sdeps, tag=f"l0f_s1s{tt}")
            s1buf.w.append(stok2)
            slot.readers.append(stok2)
            otq.readers.append(stok2)

    nxq = k.claim("XF", 0, NT * C, "x_mid")
    nxq.w = [tok2 for q in xqs for tok2 in q.w]
    nxq.readers = [tok2 for q in xqs for tok2 in q.readers]
    if sub < 5:
        return nxq
    h2T = k.claim("BB", 0, 16384, "l0f_h2T")
    for tt in range(0, NT, 2):
        A, _ = st_info[tt]; B2, _ = st_info[tt + 1]

        def vr(e, _A=A, _B=B2):
            e.tensor_tensor(_A(15), _A(12), _A(13), ALU.add)
            e.tensor_tensor(_B(15), _B(12), _B(13), ALU.add)
            e.tensor_scalar(_A(15), _A(15), 1.0 / C, 1e-6, ALU.mult, ALU.add)
            return e.tensor_scalar(_B(15), _B(15), 1.0 / C, 1e-6, ALU.mult, ALU.add)
        vtok = p.step("vector", vr, deps=s1buf.rd_deps() + s1buf.w, tag=f"l0f_rr2a{tt}")
        rtok = _rsqrt_pair(k, p, A(15), B2(15), A(14), B2(14), A(12), B2(12), [vtok], f"l0f_rr2_{tt}")
        st_info[tt] = (A, [rtok]); st_info[tt + 1] = (B2, [rtok])

    h2buf = [k.claim("EW", 0, 1024, "h2_e"), k.claim("EW", 1024, 2048, "h2_o")]
    for tt in range(NT):
        A, stok = st_info[tt]
        eb = h2buf[tt % 2]
        h2_ap = EW[:, (tt % 2) * 1024:(tt % 2) * 1024 + 1024]
        tok = p.step("scalar",
                     lambda e, _o=h2_ap, _tt=tt, _r=A(14):
                     e.activation(_o, BA[:, _tt * C:(_tt + 1) * C], ACT.Identity, scale=_r),
                     deps=stok + eb.wr_deps() + s1buf.rd_deps(), tag=f"l0f_h2{tt}")
        eb.w = [tok]; eb.readers = []
        s1buf.readers.append(tok)
        bank = psb[tt % 4]
        pb16 = PS[tt % 4][:, 0:512].bitcast(BF16)

        def tbody(e, _src=h2_ap, _pb=pb16):
            last = None
            for ct2 in range(NC):
                last = e.transpose(_pb[:, ct2 * 128:(ct2 + 1) * 128],
                                   _src[:, ct2 * 128:(ct2 + 1) * 128], cl["ident_bf16"])
            return last
        ttok = p.step("tensor", tbody, deps=[tok] + bank.wr_deps() + cl["cst_tok"], tag=f"l0f_h2t{tt}")
        eb.readers.append(ttok)
        bank.w = [ttok]; bank.readers = []

        def ebody(e, _pb=pb16, _tt=tt):
            last = None
            for ct2 in range(NC):
                last = e.tensor_scalar(BB[:, ct2 * T + _tt * 128:ct2 * T + (_tt + 1) * 128],
                                       _pb[:, ct2 * 128:(ct2 + 1) * 128],
                                       SM[:, 104 + ct2:105 + ct2], None, ALU.mult)
            return last
        etok = p.step("vector", ebody, deps=[ttok] + g1buf.rd_deps() + (h2T.init if tt == 0 else []),
                      tag=f"l0f_h2e{tt}")
        bank.readers.append(etok)
        g1buf.readers.append(etok)
        h2T.w.append(etok)

    if sub < 6:
        return nxq
    w2slab = k.claim("BA", 0, 16384, "moe_w2")
    wstage = [k.claim("BA", 16384 + i * 1024, 16384 + (i + 1) * 1024, f"w1s{i}") for i in range(3)]
    gwbuf = k.claim("BA", 19456, 19520, "gatew")
    tok = p.dma("gpsimd", [
        (BA[:, 19456:19520].rearrange("p (ct o) -> p ct o", o=8),
         dram["gate_w"][:, :].rearrange("(ct p) o -> p ct o", p=128)),
    ], deps=gwbuf.init, tag="l0f_gw")
    gwbuf.w.append(tok)

    for tt in range(0, NT, 2):
        ltoks = []
        for z in range(2):
            tt2 = tt + z
            bank = psb[z]
            pbank = PS[z][:, 0:8]

            def gmb(e, _tt=tt2, _pb=pbank):
                last = None
                for ct in range(NC):
                    last = e.matmul(_pb, BB[:, ct * T + _tt * 128:ct * T + (_tt + 1) * 128],
                                    BA[:, 19456 + ct * 8:19456 + (ct + 1) * 8],
                                    start=(ct == 0), stop=(ct == NC - 1))
                return last
            mtok = p.step("tensor", gmb, deps=h2T.rd_deps() + gwbuf.rd_deps() + bank.wr_deps(),
                          tag=f"gt{tt2}")
            h2T.readers.append(mtok); gwbuf.readers.append(mtok)
            bank.w = [mtok]; bank.readers = []
            ltoks.append(mtok)
        par_g = (tt // 2) % 2
        eqA, eqB = SM[:, 64:72], SM[:, 72:80]
        l2A, l2B = SM[:, 80:88], SM[:, 88:96]
        CMB = lambda tt3: SM[:, 384 + tt3 * 8:392 + tt3 * 8]
        stA = lambda i, _b=112 + par_g * 8: SM[:, _b + i:_b + i + 1]
        stB = lambda i, _b=116 + par_g * 8: SM[:, _b + i:_b + i + 1]
        pA, pB = PS[0][:, 0:8], PS[1][:, 0:8]

        def cb1(e, _tt=tt):
            e.tensor_reduce(stA(0), pA, axis=AX.X, op=ALU.max)
            e.tensor_reduce(stB(0), pB, axis=AX.X, op=ALU.max)
            e.tensor_scalar(eqA, pA, stA(0), None, ALU.is_ge)
            e.tensor_scalar(eqB, pB, stB(0), None, ALU.is_ge)
            e.scalar_tensor_tensor(l2A, eqA, -1e30, pA, ALU.mult, ALU.add)
            e.scalar_tensor_tensor(l2B, eqB, -1e30, pB, ALU.mult, ALU.add)
            e.tensor_reduce(stA(1), l2A, axis=AX.X, op=ALU.max)
            e.tensor_reduce(stB(1), l2B, axis=AX.X, op=ALU.max)
            e.tensor_scalar(l2A, l2A, stA(1), None, ALU.is_ge)
            e.tensor_scalar(l2B, l2B, stB(1), None, ALU.is_ge)
            e.tensor_tensor(stA(2), stA(1), stA(0), ALU.subtract)
            return e.tensor_tensor(stB(2), stB(1), stB(0), ALU.subtract)
        c1 = p.step("vector", cb1, deps=ltoks + getattr(k, "_prev_cmb", []), tag=f"cmb1_{tt}")

        def cb2(e):
            e.activation(stA(2), stA(2), ACT.Exp)
            return e.activation(stB(2), stB(2), ACT.Exp)
        c2 = p.step("scalar", cb2, deps=[c1], tag=f"cmb2_{tt}")

        def cb3(e, _tt=tt):
            e.tensor_scalar(stA(3), stA(2), 1.0, None, ALU.add)
            e.tensor_scalar(stB(3), stB(2), 1.0, None, ALU.add)
            e.reciprocal(stA(3), stA(3))
            e.reciprocal(stB(3), stB(3))
            e.tensor_tensor(stA(2), stA(2), stA(3), ALU.mult)
            e.tensor_tensor(stB(2), stB(2), stB(3), ALU.mult)
            e.tensor_scalar(eqA, eqA, stA(3), None, ALU.mult)
            e.tensor_scalar(eqB, eqB, stB(3), None, ALU.mult)
            e.scalar_tensor_tensor(CMB(_tt), l2A, stA(2), eqA, ALU.mult, ALU.add)
            return e.scalar_tensor_tensor(CMB(_tt + 1), l2B, stB(2), eqB, ALU.mult, ALU.add)
        c3 = p.step("vector", cb3, deps=[c2], tag=f"cmb3_{tt}")
        psb[0].readers.append(c3); psb[1].readers.append(c3)
        k._prev_cmb = [c2, c3]

    if sub < 7:
        return nxq
    hidT = k.claim("BC", 0, 16384, "moe_hidT")
    w1slabA = k.claim("BA", 16384, 24576, "moe_w1a")
    w1slabB = k.claim("BB", 16384, 24576, "moe_w1b")

    def w1ap(ct, jt):
        if ct < 4:
            return BA[:, 16384 + ct * 2048 + jt * 128:16384 + ct * 2048 + (jt + 1) * 128]
        return BB[:, 16384 + (ct - 4) * 2048 + jt * 128:16384 + (ct - 4) * 2048 + (jt + 1) * 128]

    for e_i in range(8):
        w1tok = p.dma("gpsimd", [
            (BA[:, 16384:24576].rearrange("p (ct j) -> p ct j", j=2048),
             dram["moe_w1"][e_i].rearrange("(ct p) j -> p ct j", p=128)[:, 0:4, :]),
            (BB[:, 16384:24576].rearrange("p (ct j) -> p ct j", j=2048),
             dram["moe_w1"][e_i].rearrange("(ct p) j -> p ct j", p=128)[:, 4:8, :]),
        ], deps=w1slabA.wr_deps() + w1slabB.wr_deps(), tag=f"moe_w1_{e_i}")
        for b in (w1slabA, w1slabB):
            b.w = [w1tok]; b.readers = []; b.init = []
        w2tok = p.dma("gpsimd", [
            (BA[:, 0:16384].rearrange("p (jt2 c) -> p jt2 c", c=C),
             dram["moe_w2"][e_i].rearrange("(jt2 p) c -> p jt2 c", p=128)),
        ], deps=w2slab.wr_deps(), tag=f"moe_w2_{e_i}")
        w2slab.w = [w2tok]; w2slab.readers = []; w2slab.init = []
        for half in range(2):
            for jt in range(16):
                for ch in range(2):
                    bank = psb[ch]
                    pbank = PS[ch][:, 0:512]

                    def mb(e, _jt=jt, _h=half, _c=ch, _pb=pbank):
                        last = None
                        for ct in range(NC):
                            last = e.matmul(_pb, w1ap(ct, _jt),
                                            BB[:, ct * T + _h * 1024 + _c * 512:ct * T + _h * 1024 + (_c + 1) * 512],
                                            start=(ct == 0), stop=(ct == NC - 1))
                        return last
                    mtok = p.step("tensor", mb, deps=[w1tok] + h2T.rd_deps() + bank.wr_deps(),
                                  tag=f"hid{e_i}_{half}_{jt}_{ch}")
                    w1slabA.readers.append(mtok); w1slabB.readers.append(mtok)
                    h2T.readers.append(mtok)
                    bank.w = [mtok]; bank.readers = []

                    def ev(e, _jt=jt, _c=ch, _pb=pbank):
                        return e.activation(BC[:, _jt * 1024 + _c * 512:_jt * 1024 + (_c + 1) * 512],
                                            _pb, ACT.Silu)
                    etok = p.step("scalar", ev, deps=[mtok] + (hidT.wr_deps() if (jt == 0 and ch == 0) else []),
                                  tag=f"sil{e_i}_{half}_{jt}_{ch}")
                    bank.readers.append(etok)
                    if jt == 0 and ch == 0:
                        hidT.w = [etok]; hidT.readers = []; hidT.init = []
                    else:
                        hidT.w.append(etok)
            for qg in range(2):
                bdeps = []
                for i in range(4):
                    bdeps += psb[i].wr_deps() + k.psb2[i].wr_deps()

                def mb2(e, _h=half, _q=qg):
                    last = None
                    for jt in range(16):
                        for i in range(4):
                            for cc in range(2):
                                last = e.matmul(pbank2(i, cc),
                                                BC[:, jt * 1024 + _q * 512 + i * 128:jt * 1024 + _q * 512 + (i + 1) * 128],
                                                BA[:, jt * C + cc * 512:jt * C + (cc + 1) * 512],
                                                start=(jt == 0), stop=(jt == 15))
                    return last
                mtok = p.step("tensor", mb2, deps=hidT.rd_deps() + w2slab.rd_deps() + bdeps,
                              tag=f"ye{e_i}_{half}_{qg}")
                hidT.readers.append(mtok); w2slab.readers.append(mtok)
                for i in range(4):
                    psb[i].w = [mtok]; psb[i].readers = []
                    k.psb2[i].w = [mtok]; k.psb2[i].readers = []
                for i in range(4):
                    tt = half * 8 + qg * 4 + i
                    xq = xqs[tt // 4]

                    def ve(e, _i=i, _tt=tt, _e=e_i):
                        cmb = SM[:, 384 + _tt * 8 + _e:385 + _tt * 8 + _e]
                        xs0 = t["XF"][:, _tt * C:_tt * C + 512]
                        xs1 = t["XF"][:, _tt * C + 512:(_tt + 1) * C]
                        e.scalar_tensor_tensor(xs0, pbank2(_i, 0), cmb, xs0, ALU.mult, ALU.add)
                        return e.scalar_tensor_tensor(xs1, pbank2(_i, 1), cmb, xs1, ALU.mult, ALU.add)
                    vtok = p.step("vector", ve, deps=[mtok] + xq.wr_deps(), tag=f"moex{e_i}_{tt}")
                    xq.w.append(vtok)
                    psb[i].readers.append(vtok); k.psb2[i].readers.append(vtok)

    nx = k.claim("XF", 0, NT * C, "x_after_l0f")
    nx.w = [tok2 for q in xqs for tok2 in q.w]
    nx.readers = [tok2 for q in xqs for tok2 in q.readers]
    return nx


def emit_l1_attn(k, xbuf, cl):
    p, t, dram = k.p, k.t, k.dram
    BA, BB, BC, EW, CB, SM = t["BA"], t["BB"], t["BC"], t["EW"], t["CONST"], t["SM"]
    PS = t["PS"]
    psb = k.psb
    BAF = BA[:, :].bitcast(F32)

    wslab = k.claim("BB", 16384, 24576, "l1a_wk")
    tok = p.dma("gpsimd", [
        (BB[:, 16384:24576].rearrange("p (ct o) -> p ct o", o=C),
         dram["WkT"][:, :].rearrange("(ct p) o -> p ct o", p=128)),
    ], deps=wslab.init, tag="l1a_wk")
    wslab.w.append(tok)
    tabbuf = k.claim("EW", 2048, 4096, "l1a_tabs")
    tok = p.dma("gpsimd", [
        (EW[:, 2048:3072].rearrange("p (a d) -> p a d", d=64),
         dram["cst_l1a_cos"][:, :].rearrange("(a p) d -> p a d", p=128)),
        (EW[:, 3072:4096].rearrange("p (a d) -> p a d", d=64),
         dram["cst_l1a_sin"][:, :].rearrange("(a p) d -> p a d", p=128)),
    ], deps=tabbuf.init, tag="l1a_tabs")
    tabbuf.w.append(tok)
    lqbuf = k.claim("BB", 24576, 24576, "lq_dummy")
    lqb = k.claim("BA", 21504, 22528, "l1a_latq")
    tok = p.dma("gpsimd", [
        (BA[0:64, 21504:22528].rearrange("p (h d) -> p h d", d=64),
         dram["lat_qT"][:, :, :].rearrange("h d l -> d h l")),
        (BA[64:128, 21504:22528].rearrange("p (h d) -> p h d", d=64),
         dram["lat_qT"][:, :, :].rearrange("h d l -> d h l")),
    ], deps=lqb.init, tag="l1a_latq")
    lqb.w.append(tok)

    xnT = k.claim("BA", 0, 16384, "l1a_xnT")
    xnb = [k.claim("EW", 0, 1024, "l1a_xn_e"), k.claim("EW", 1024, 2048, "l1a_xn_o")]
    st_info = {}
    for tt in range(0, NT, 2):
        a, b, stok = _ln_stats_pair(k, xbuf, tt, 128 + tt * 16, tt + 1, 128 + (tt + 1) * 16)
        st_info[tt] = (a, stok); st_info[tt + 1] = (b, stok)
    for tt in range(NT):
        (rstd, negmur, _o, smb), stok = st_info[tt]
        xt = t["XF"][:, tt * C:(tt + 1) * C]
        eb = xnb[tt % 2]
        xn_ap = EW[:, (tt % 2) * 1024:(tt % 2) * 1024 + 1024]
        tok = p.step("scalar",
                     lambda e, _o2=xn_ap, _x=xt, _r=rstd, _b=negmur:
                     e.activation(_o2, _x, ACT.Identity, scale=_r, bias=_b),
                     deps=stok + eb.wr_deps() + xbuf.rd_deps(), tag=f"l1a_xn{tt}")
        smb.readers.append(tok)
        eb.w = [tok]; eb.readers = []
        xbuf.readers.append(tok)
        bank = psb[tt % 4]
        pb16 = PS[tt % 4][:, 0:512].bitcast(BF16)

        def tb(e, _src=xn_ap, _pb=pb16):
            last = None
            for ct in range(NC):
                last = e.transpose(_pb[:, ct * 128:(ct + 1) * 128],
                                   _src[:, ct * 128:(ct + 1) * 128], cl["ident_bf16"])
            return last
        ttok = p.step("tensor", tb, deps=[tok] + bank.wr_deps() + cl["cst_tok"], tag=f"l1a_tr{tt}")
        eb.readers.append(ttok)
        bank.w = [ttok]; bank.readers = []

        def ebd(e, _pb=pb16, _tt=tt):
            dst = BA[:, 0:16384].rearrange("p (ct t) -> p ct t", t=T)[:, :, _tt * 128:(_tt + 1) * 128]
            return e.tensor_copy(dst, _pb.rearrange("p (ct t) -> p ct t", t=128))
        etok = p.step("vector", ebd, deps=[ttok] + (xnT.init if tt == 0 else []), tag=f"l1a_te{tt}")
        bank.readers.append(etok)
        xnT.w.append(etok)

    ktbuf = k.claim("BB", 0, 16384, "l1a_KT")
    SEV = BAF[:, 8192:8704]; SOD = BAF[:, 8704:9216]
    REV = BAF[:, 9216:9728]; ROD = BAF[:, 9728:10240]
    KP = BA[:, 20480:21504]
    scrb = k.claim("BA", 16384, 21504, "l1a_kscr")
    def k_stage1(tt):
        bank = psb[tt % 2]
        pbank = PS[tt % 2][:, 0:1024]

        def mb(e, _tt=tt, _pb=pbank):
            last = None
            for cc in range(2):
                for ct in range(NC):
                    last = e.matmul(_pb[:, cc * 512:(cc + 1) * 512],
                                    BA[:, ct * T + _tt * 128:ct * T + (_tt + 1) * 128],
                                    BB[:, 16384 + ct * C + cc * 512:16384 + ct * C + (cc + 1) * 512],
                                    start=(ct == 0), stop=(ct == NC - 1))
            return last
        mtok = p.step("tensor", mb, deps=xnT.rd_deps() + wslab.rd_deps() + bank.wr_deps() +
                      (k.psb2[tt % 2].wr_deps()), tag=f"l1a_k{tt}")
        xnT.readers.append(mtok); wslab.readers.append(mtok)
        bank.w = [mtok]; bank.readers = []
        k.psb2[tt % 2].w = [mtok]; k.psb2[tt % 2].readers = []
        return (tt, mtok, bank, pbank)

    def k_stage2(state):
        tt, mtok, bank, pbank = state
        cte = EW[:, 2048 + tt * 64:2048 + (tt + 1) * 64][:, 0::2].unsqueeze(1).broadcast_to((128, 16, 32))
        ste = EW[:, 3072 + tt * 64:3072 + (tt + 1) * 64][:, 0::2].unsqueeze(1).broadcast_to((128, 16, 32))
        kev = pbank.rearrange("p (h d2 two) -> p h d2 two", d2=32, two=2)[:, :, :, 0]
        kod = pbank.rearrange("p (h d2 two) -> p h d2 two", d2=32, two=2)[:, :, :, 1]
        sev3 = SEV.rearrange("p (h d2) -> p h d2", d2=32)
        sod3 = SOD.rearrange("p (h d2) -> p h d2", d2=32)
        rev3 = REV.rearrange("p (h d2) -> p h d2", d2=32)
        rod3 = ROD.rearrange("p (h d2) -> p h d2", d2=32)
        kpe = KP.rearrange("p (h d2 two) -> p h d2 two", d2=32, two=2)[:, :, :, 0]
        kpo = KP.rearrange("p (h d2 two) -> p h d2 two", d2=32, two=2)[:, :, :, 1]

        def rp(e, _kev=kev, _kod=kod, _cte=cte, _ste=ste,
               _sev=sev3, _sod=sod3, _rev=rev3, _rod=rod3, _kpe=kpe, _kpo=kpo):
            e.tensor_tensor(_sev, _kev, _cte, ALU.mult)
            e.tensor_tensor(_sod, _kod, _cte, ALU.mult)
            e.tensor_tensor(_rev, _kod, _ste, ALU.mult)
            e.tensor_tensor(_rod, _kev, _ste, ALU.mult)
            e.tensor_tensor(_kpe, _sev, _rev, ALU.subtract)
            return e.tensor_tensor(_kpo, _sod, _rod, ALU.add)
        rtok = p.step("vector", rp, deps=[mtok] + tabbuf.rd_deps() + scrb.wr_deps(), tag=f"l1a_rp{tt}")
        bank.readers.append(rtok); k.psb2[tt % 2].readers.append(rtok)
        tabbuf.readers.append(rtok)
        scrb.w = [rtok]; scrb.readers = []; scrb.init = []

        bank2 = psb[2 + tt % 2]
        pb16 = PS[2 + tt % 2][:, 0:512].bitcast(BF16)

        def tb2(e, _pb=pb16):
            last = None
            for ct in range(NC):
                last = e.transpose(_pb[:, ct * 128:(ct + 1) * 128],
                                   KP[:, ct * 128:(ct + 1) * 128], cl["ident_bf16"])
            return last
        ttok = p.step("tensor", tb2, deps=[rtok] + bank2.wr_deps() + cl["cst_tok"], tag=f"l1a_ktr{tt}")
        scrb.readers.append(ttok)
        bank2.w = [ttok]; bank2.readers = []

        def ebd2(e, _pb=pb16, _tt=tt):
            dst = BB[:, 0:16384].rearrange("p (ct t) -> p ct t", t=T)[:, :, _tt * 128:(_tt + 1) * 128]
            return e.tensor_copy(dst, _pb.rearrange("p (ct t) -> p ct t", t=128))
        etok = p.step("vector", ebd2, deps=[ttok] + (ktbuf.init if tt == 0 else []), tag=f"l1a_kte{tt}")
        bank2.readers.append(etok)
        ktbuf.w.append(etok)

    pend = None
    for tt in range(NT):
        st1 = k_stage1(tt)
        if pend is not None:
            k_stage2(pend)
        pend = st1
    k_stage2(pend)

    wvslab = k.claim("BB", 16384, 24576, "l1a_wv")
    tok = p.dma("gpsimd", [
        (BB[:, 16384:24576].rearrange("p (ct o) -> p ct o", o=C),
         dram["WvT"][:, :].rearrange("(ct p) o -> p ct o", p=128)),
    ], deps=wvslab.init, tag="l1a_wv")
    wvslab.w.append(tok)
    vbuf = k.claim("BC", 0, 16384, "l1a_V")
    for tt in range(NT):
        for dc in range(2):
            idx = tt * 2 + dc
            bank = psb[idx % 4]
            pbank = PS[idx % 4][:, 0:512]

            def mb(e, _tt=tt, _dc=dc, _pb=pbank):
                last = None
                for ct in range(NC):
                    last = e.matmul(_pb, BA[:, ct * T + _tt * 128:ct * T + (_tt + 1) * 128],
                                    BB[:, 16384 + ct * C + _dc * 512:16384 + ct * C + (_dc + 1) * 512],
                                    start=(ct == 0), stop=(ct == NC - 1))
                return last
            mtok = p.step("tensor", mb, deps=xnT.rd_deps() + wvslab.rd_deps() + bank.wr_deps(),
                          tag=f"l1a_v{tt}_{dc}")
            xnT.readers.append(mtok); wvslab.readers.append(mtok)
            bank.w = [mtok]; bank.readers = []

            def ev(e, _tt=tt, _dc=dc, _pb=pbank):
                return e.copy(BC[:, _tt * 1024 + _dc * 512:_tt * 1024 + (_dc + 1) * 512], _pb)
            etok = p.step("scalar", ev, deps=[mtok] + (vbuf.init if idx == 0 else []), tag=f"l1a_ve{tt}_{dc}")
            bank.readers.append(etok)
            vbuf.w.append(etok)

    ptb = k.claim("EW", 0, 1024, "l1a_PT")
    attnb = k.claim("EW", 1024, 2048, "l1a_attn")
    accA, accB = SM[:, 384:448], SM[:, 448:512]
    dsc = SM[:, 112:128]
    for h in range(16):
        pbase = (h % 2) * 64
        kth = (h // 2) * 2048
        st_toks = []
        for g in range(4):
            bank = psb[g % 2]
            pbank = PS[g % 2][:, 0:256]

            def mb(e, _h=h, _g=g, _pb=pbank, _pbase=pbase, _kth=kth):
                last = None
                for n4 in range(4):
                    n = _g * 4 + n4
                    last = e.matmul(_pb[:, n4 * 64:(n4 + 1) * 64],
                                    BB[_pbase:_pbase + 64, _kth + n * 128:_kth + (n + 1) * 128],
                                    BA[_pbase:_pbase + 64, 21504 + _h * 64:21504 + (_h + 1) * 64],
                                    start=True, stop=True)
                return last
            mtok = p.step("tensor", mb, deps=ktbuf.rd_deps() + lqb.rd_deps() + bank.wr_deps(),
                          tag=f"l1a_st{h}_{g}")
            ktbuf.readers.append(mtok); lqb.readers.append(mtok)
            bank.w = [mtok]; bank.readers = []

            def ev(e, _g=g, _pb=pbank):
                return e.activation(EW[:, _g * 256:(_g + 1) * 256], _pb, ACT.Exp, scale=0.125)
            etok = p.step("scalar", ev, deps=[mtok] + (ptb.wr_deps() if g == 0 else []), tag=f"l1a_se{h}_{g}")
            bank.readers.append(etok)
            if g == 0:
                ptb.w = [etok]; ptb.readers = []; ptb.init = []
            else:
                ptb.w.append(etok)
            st_toks.append(etok)
        def mk(e):
            ap = EW[:, 0:1024].rearrange("p (n l) -> p n l", l=64)
            return e.tensor_tensor(ap, ap, cl["lat_mask"].unsqueeze(1).broadcast_to((128, 16, 64)), ALU.mult)
        mtok2 = p.step("vector", mk, deps=st_toks + cl["cst_tok"], tag=f"l1a_mk{h}")
        ptb.w.append(mtok2)
        pvbank = lambda g: PS[2 + g // 2][:, (g % 2) * 512:(g % 2) * 512 + 260]
        pvbuf = lambda g: (psb[2 + g // 2] if g % 2 == 0 else k.psb2[2 + g // 2])
        pv_toks = []
        for g in range(4):
            bank = pvbuf(g)
            pbank = pvbank(g)

            def pv(e, _h=h, _g=g, _pb=pbank):
                last = None
                for n4 in range(4):
                    n = _g * 4 + n4
                    e.matmul(_pb[0:64, n4 * 65:n4 * 65 + 64],
                             EW[:, n * 64:(n + 1) * 64],
                             BC[:, n * 1024 + _h * 64:n * 1024 + (_h + 1) * 64],
                             start=(n4 == 0), stop=False)
                    last = e.matmul(_pb[0:64, n4 * 65 + 64:n4 * 65 + 65],
                                    EW[:, n * 64:(n + 1) * 64], cl["ones_col"],
                                    start=False, stop=(n4 == 3))
                return last
            pvtok = p.step("tensor", pv, deps=ptb.rd_deps() + vbuf.rd_deps() + bank.wr_deps() + cl["cst_tok"],
                           tag=f"l1a_pv{h}_{g}")
            ptb.readers.append(pvtok); vbuf.readers.append(pvtok)
            bank.w = [pvtok]; bank.readers = []
            pv_toks.append(pvtok)
        def rc(e, _h=h):
            last = None
            for g in range(4):
                pb2 = pvbank(g)
                last = e.reciprocal(dsc[0:64, g * 4:(g + 1) * 4], pb2[0:64, 64:260:65])
            return last
        rtok = p.step("vector", rc, deps=pv_toks, tag=f"l1a_rc{h}")
        def ac(e, _h=h):
            e.memset(accA[0:64, :], 0.0)
            e.memset(accB[0:64, :], 0.0)
            for n in range(16):
                g, n4 = n // 4, n % 4
                pb2 = pvbank(g)
                acc = accA if n % 2 == 0 else accB
                e.scalar_tensor_tensor(acc[0:64, :], pb2[0:64, n4 * 65:n4 * 65 + 64],
                                       dsc[0:64, n:n + 1], acc[0:64, :], ALU.mult, ALU.add)
            e.memset(gsums[0:64, 0:1], 0.0)
            return e.tensor_tensor(EW[0:64, 1024 + _h * 64:1024 + (_h + 1) * 64],
                                   accA[0:64, :], accB[0:64, :], ALU.add)
        atok = p.step("vector", ac, deps=[rtok] + (attnb.wr_deps() if h == 0 else []), tag=f"l1a_ac{h}")
        for g in range(4):
            pvbuf(g).readers.append(atok)
        if h == 0:
            attnb.w = [atok]; attnb.readers = []; attnb.init = []
        else:
            attnb.w.append(atok)

    duptok = p.dma("sync", [(EW[64:128, 1024:2048], EW[0:64, 1024:2048])],
                   deps=attnb.wr_deps(), tag="l1a_attndup")
    attnb.w.append(duptok)
    attnb_ap = lambda _h: EW[(_h % 2) * 64:(_h % 2) * 64 + 64, 1024 + _h * 64:1024 + (_h + 1) * 64]

    wgslab = k.claim("BB", 16384, 24576, "l1a_wg")
    tok = p.dma("gpsimd", [
        (BB[:, 16384:24576].rearrange("p (ct o) -> p ct o", o=C),
         dram["WgT"][:, :].rearrange("(ct p) o -> p ct o", p=128)),
    ], deps=wgslab.init, tag="l1a_wg")
    wgslab.w.append(tok)
    wpslab = k.claim("BC", 0, 16384, "l1a_wp")
    tok = p.dma("gpsimd", [
        (BC[0:64, 0:16384].rearrange("p (ob c) -> p ob c", c=C),
         dram["WpT"][:, :].rearrange("(ob p) c -> p ob c", p=64)),
    ], deps=wpslab.init, tag="l1a_wp")
    wpslab.w.append(tok)
    gtbuf = k.claim("BB", 0, 16384, "l1a_GT")
    gscr = k.claim("EW", 0, 1024, "l1a_g")
    gsums = SM[:, 64:96]
    def g_stage1(tt):
        bank = psb[tt % 2]
        pbank = PS[tt % 2][:, 0:1024]

        def mb(e, _tt=tt, _pb=pbank):
            last = None
            for cc in range(2):
                for ct in range(NC):
                    last = e.matmul(_pb[:, cc * 512:(cc + 1) * 512],
                                    BA[:, ct * T + _tt * 128:ct * T + (_tt + 1) * 128],
                                    BB[:, 16384 + ct * C + cc * 512:16384 + ct * C + (cc + 1) * 512],
                                    start=(ct == 0), stop=(ct == NC - 1))
            return last
        mtok = p.step("tensor", mb, deps=xnT.rd_deps() + wgslab.rd_deps() + bank.wr_deps() + k.psb2[tt % 2].wr_deps(),
                      tag=f"l1a_g{tt}")
        xnT.readers.append(mtok); wgslab.readers.append(mtok)
        bank.w = [mtok]; bank.readers = []
        k.psb2[tt % 2].w = [mtok]; k.psb2[tt % 2].readers = []
        return (tt, mtok, bank, pbank)

    def g_stage2(state):
        tt, mtok, bank, pbank = state
        etok = p.step("scalar", lambda e, _pb=pbank: e.activation(EW[:, 0:1024], _pb, ACT.Exp, scale=0.125),
                      deps=[mtok] + gscr.wr_deps(), tag=f"l1a_ge{tt}")
        bank.readers.append(etok); k.psb2[tt % 2].readers.append(etok)
        gscr.w = [etok]; gscr.readers = []; gscr.init = []

        s1t = p.step("vector", lambda e: e.tensor_reduce(gsums[:, 0:16], EW[:, 0:1024].rearrange("p (h l) -> p h l", l=64), axis=AX.X, op=ALU.add),
                     deps=[etok], tag=f"l1a_gs{tt}")
        s2t = p.step("vector", lambda e: e.reciprocal(gsums[:, 16:32], gsums[:, 0:16]),
                     deps=[s1t], tag=f"l1a_gr{tt}")

        def nm(e):
            last = None
            for h in range(16):
                last = e.tensor_scalar(EW[:, h * 64:(h + 1) * 64], EW[:, h * 64:(h + 1) * 64],
                                       gsums[:, 16 + h:17 + h], None, ALU.mult)
            return last
        ntok = p.step("vector", nm, deps=[s2t], tag=f"l1a_gn{tt}")
        gscr.w.append(ntok)

        bank2 = psb[2 + tt % 2]
        pb16 = PS[2 + tt % 2][:, 0:512].bitcast(BF16)

        def tb3(e, _pb=pb16):
            last = None
            for ct in range(NC):
                last = e.transpose(_pb[:, ct * 128:(ct + 1) * 128],
                                   EW[:, ct * 128:(ct + 1) * 128], cl["ident_bf16"])
            return last
        ttok = p.step("tensor", tb3, deps=[ntok] + bank2.wr_deps() + cl["cst_tok"], tag=f"l1a_gtr{tt}")
        gscr.readers.append(ttok)
        bank2.w = [ttok]; bank2.readers = []

        def ebd3(e, _pb=pb16, _tt=tt):
            dst = BB[:, 0:16384].rearrange("p (ct t) -> p ct t", t=T)[:, :, _tt * 128:(_tt + 1) * 128]
            return e.tensor_copy(dst, _pb.rearrange("p (ct t) -> p ct t", t=128))
        etok2 = p.step("vector", ebd3, deps=[ttok] + (gtbuf.init if tt == 0 else []), tag=f"l1a_gte{tt}")
        bank2.readers.append(etok2)
        gtbuf.w.append(etok2)

    pend2 = None
    for tt in range(NT):
        st1 = g_stage1(tt)
        if pend2 is not None:
            g_stage2(pend2)
        pend2 = st1
    g_stage2(pend2)

    wscrs = [k.claim("EW", 0, 512, "l1a_wscr0"), k.claim("EW", 512, 1024, "l1a_wscr1")]
    for grp in range(8):
        tta, ttb = grp * 2, grp * 2 + 1
        bdeps = []
        for i in range(2):
            bdeps += psb[i].wr_deps() + k.psb2[i].wr_deps()
        acc_toks = []
        for h in range(16):
            pbase = (h % 2) * 64
            wscr = wscrs[h % 2]
            bankw = psb[2 + h % 2]
            pw = PS[2 + h % 2][:, 0:256]

            def wmm(e, _h=h, _grp=grp, _pb=pw, _pbase=pbase):
                return e.matmul(_pb[0:64, 0:256], attnb_ap(_h),
                                BB[_pbase:_pbase + 64, (_h // 2) * 2048 + _grp * 256:(_h // 2) * 2048 + (_grp + 1) * 256],
                                start=True, stop=True)
            wtok = p.step("tensor", wmm, deps=attnb.rd_deps() + gtbuf.rd_deps() + bankw.wr_deps(),
                          tag=f"l1a_w{grp}_{h}")
            attnb.readers.append(wtok); gtbuf.readers.append(wtok)
            bankw.w = [wtok]; bankw.readers = []

            def wev(e, _pb=pw, _h=h):
                return e.tensor_copy(EW[0:64, (_h % 2) * 512:(_h % 2) * 512 + 256], _pb[0:64, 0:256])
            wetok = p.step("vector", wev, deps=[wtok] + wscr.wr_deps(), tag=f"l1a_we{grp}_{h}")
            bankw.readers.append(wetok)
            wscr.w = [wetok]; wscr.readers = []; wscr.init = []

            def pmm(e, _h=h, _grp=grp):
                last = None
                for i in range(2):
                    for cc in range(2):
                        last = e.matmul(PS[i][:, cc * 512:(cc + 1) * 512],
                                        EW[0:64, (_h % 2) * 512 + i * 128:(_h % 2) * 512 + (i + 1) * 128],
                                        BC[0:64, _h * 1024 + cc * 512:_h * 1024 + (cc + 1) * 512],
                                        start=(_h == 0), stop=(_h == 15))
                return last
            ptok = p.step("tensor", pmm, deps=[wetok] + wpslab.rd_deps() + (bdeps if h == 0 else []),
                          tag=f"l1a_p{grp}_{h}")
            wscr.readers.append(ptok); wpslab.readers.append(ptok)
            for i in range(2):
                if h == 0:
                    psb[i].w = [ptok]; psb[i].readers = []
                    k.psb2[i].w = [ptok]; k.psb2[i].readers = []
                else:
                    psb[i].w.append(ptok); k.psb2[i].w.append(ptok)
            acc_toks.append(ptok)
        for i in range(2):
            tt = grp * 2 + i

            def av(e, _i=i, _tt=tt):
                xs0 = t["XF"][:, _tt * C:_tt * C + 512]
                xs1 = t["XF"][:, _tt * C + 512:(_tt + 1) * C]
                e.tensor_tensor(xs0, PS[_i][:, 0:512], xs0, ALU.add)
                return e.tensor_tensor(xs1, PS[_i][:, 512:1024], xs1, ALU.add)
            avtok = p.step("vector", av, deps=[acc_toks[-1]] + xbuf.wr_deps(), tag=f"l1a_x{tt}")
            xbuf.w.append(avtok)
            psb[i].readers.append(avtok); k.psb2[i].readers.append(avtok)
    return xbuf


def emit_l1_ffn(k, xbuf, cl):
    p, t, dram = k.p, k.t, k.dram
    BA, BB, BC, EW, CB, SM = t["BA"], t["BB"], t["BC"], t["EW"], t["CONST"], t["SM"]
    PS = t["PS"]
    psb, psb2 = k.psb, k.psb2
    XFf = t["XF"]
    EWF = EW[:, :].bitcast(F32)
    BBF = BB[:, :].bitcast(F32)
    BCF = BC[:, :].bitcast(F32)

    def bankap(i, cc):
        return PS[i][:, cc * 512:(cc + 1) * 512]

    def bankbuf(i, cc):
        return psb[i] if cc == 0 else psb2[i]

    st_info = {}
    for tt in range(0, NT, 2):
        a, b, stok = _ln_stats_pair(k, xbuf, tt, 128 + tt * 16, tt + 1, 128 + (tt + 1) * 16)
        st_info[tt] = (a, stok); st_info[tt + 1] = (b, stok)
    sp_tok = p.dma("sync", [(dram["xspill"][:, :].rearrange("(a p) c -> p a c", p=128),
                             XFf[:, :].rearrange("p (a c) -> p a c", c=C))],
                   deps=xbuf.wr_deps() + [tok for tt in range(0, NT, 2) for tok in st_info[tt][1]],
                   tag="l1f_spill")
    xbuf.readers.append(sp_tok)

    sT = k.claim("XF", 0, NT * C, "l1f_sT")
    ewb = k.claim("EW", 0, 4096, "l1f_p0")
    for tt in range(NT):
        (rstd, negmur, _opr, smb), stok = st_info[tt]
        xr = EWF[:, 0:1024]
        uu = EWF[:, 1024:2048]
        rtok = p.dma("sync", [(xr, dram["xspill"][:, :].rearrange("(a p) c -> p a c", p=128)[:, tt, :])],
                     deps=ewb.wr_deps() + [sp_tok], tag=f"l1f_xr{tt}")
        ewb.w = [rtok]; ewb.readers = []; ewb.init = []
        utok = p.step("scalar",
                      lambda e, _r=rstd, _b=negmur: e.activation(uu, xr, ACT.Identity, scale=_r, bias=_b),
                      deps=[rtok] + stok, tag=f"l1f_u{tt}")
        smb.readers.append(utok)
        ewb.w.append(utok)
        bank = psb[tt % 2]
        pbank = PS[tt % 2][:, 0:1024]

        def tb(e, _pb=pbank):
            last = None
            for ct in range(NC):
                last = e.transpose(_pb[:, ct * 128:(ct + 1) * 128],
                                   uu[:, ct * 128:(ct + 1) * 128], cl["ident_f32"])
            return last
        ttok = p.step("tensor", tb, deps=[utok] + bank.wr_deps() + psb2[tt % 2].wr_deps() + cl["cst_tok"],
                      tag=f"l1f_tr{tt}")
        ewb.readers.append(ttok)
        bank.w = [ttok]; bank.readers = []
        psb2[tt % 2].w = [ttok]; psb2[tt % 2].readers = []

        def ebd(e, _pb=pbank, _tt=tt):
            dst = XFf[:, :].rearrange("p (ct t) -> p ct t", t=T)[:, :, _tt * 128:(_tt + 1) * 128]
            return e.tensor_copy(dst, _pb.rearrange("p (ct t) -> p ct t", t=128))
        etok = p.step("vector", ebd, deps=[ttok] + (sT.init if tt == 0 else []), tag=f"l1f_se{tt}")
        bank.readers.append(etok); psb2[tt % 2].readers.append(etok)
        sT.w.append(etok)


    tabs = k.claim("BC", 8320, 12416, "l1f_tabs")
    tok = p.dma("gpsimd", [
        (BC[:, 8320 + i * 1024:8320 + (i + 1) * 1024].rearrange("p (a d) -> p a d", d=64),
         dram[nm][:, :].rearrange("(a p) d -> p a d", p=128))
        for i, nm in enumerate(["cst_xpos_qc", "cst_xpos_qs", "cst_xpos_kc", "cst_xpos_ks"])
    ], deps=tabs.init, tag="l1f_tabs")
    tabs.w.append(tok)

    ones_f32 = SM[0:1, 384:512]
    o32tok = p.step("vector", lambda e: e.memset(ones_f32, 1.0), tag="l1f_o32")

    for l in range(2):
        qrow, krow = SM[0:1, 0:64], SM[0:1, 64:128]
        tok = p.dma("sync", [(qrow, dram[f"qs{l}"][:, :]), (krow, dram[f"ks{l}"][:, :])],
                    deps=[], tag=f"l1f_qs{l}")
        qsb, ksb = SM[:, 128:192], SM[:, 192:256]

        def qb(e, _tok=tok):
            e.matmul(PS[2][:, 0:64], ones_f32, qrow, start=True, stop=True)
            return e.matmul(PS[2][:, 512:576], ones_f32, krow, start=True, stop=True)
        qtok = p.step("tensor", qb, deps=[tok, o32tok] + psb[2].wr_deps() + psb2[2].wr_deps(),
                      tag=f"l1f_qb{l}")
        psb[2].w = [qtok]; psb[2].readers = []
        psb2[2].w = [qtok]; psb2[2].readers = []

        def qe(e):
            e.tensor_copy(qsb, PS[2][:, 0:64])
            return e.tensor_copy(ksb, PS[2][:, 512:576])
        qetok = p.step("vector", qe, deps=[qtok], tag=f"l1f_qe{l}")
        psb[2].readers.append(qetok); psb2[2].readers.append(qetok)

        etabs = k.claim("BC", 12416, 16512, f"l1f_etabs{l}")
        ETC = 12416

        def mktabs(e):
            last = None
            for side, (ci, si, sb) in enumerate([(0, 1, qsb), (2, 3, ksb)]):
                raw_c = BC[:, 8320 + ci * 1024:8320 + (ci + 1) * 1024].rearrange("p (a d) -> p a d", d=64)
                raw_s = BC[:, 8320 + si * 1024:8320 + (si + 1) * 1024].rearrange("p (a d) -> p a d", d=64)
                eff_c = BC[:, ETC + ci * 1024:ETC + (ci + 1) * 1024].rearrange("p (a d) -> p a d", d=64)
                eff_s = BC[:, ETC + si * 1024:ETC + (si + 1) * 1024].rearrange("p (a d) -> p a d", d=64)
                e.tensor_tensor(eff_c, raw_c, sb.unsqueeze(1).broadcast_to((128, 16, 64)), ALU.mult)
                e.tensor_tensor(eff_s[:, :, 0:32], raw_s[:, :, 0:32],
                                sb[:, 32:64].unsqueeze(1).broadcast_to((128, 16, 32)), ALU.mult)
                last = e.tensor_tensor(eff_s[:, :, 32:64], raw_s[:, :, 32:64],
                                       sb[:, 0:32].unsqueeze(1).broadcast_to((128, 16, 32)), ALU.mult)
            return last
        mtok = p.step("vector", mktabs, deps=[qetok] + tabs.rd_deps() + etabs.wr_deps(), tag=f"l1f_mt{l}")
        tabs.readers.append(mtok)
        etabs.w = [mtok]; etabs.readers = []; etabs.init = list(etabs.init)

        hq = k.claim("BA", 0, 16384, f"l1f_hq{l}")

        def mkhq(e):
            last = None
            for ct in range(NC):
                src = XFf[:, ct * T:(ct + 1) * T]
                dst = BA[:, ct * T:(ct + 1) * T]
                if ct < 4:
                    last = e.copy(dst, src)
                else:
                    last = e.copy(dst[:, 1:T], src[:, 0:T - 1])
            return last
        hqt = p.step("scalar", mkhq, deps=sT.rd_deps() + hq.init, tag=f"l1f_hq{l}")
        sT.readers.append(hqt)

        def hqz(e):
            last = None
            for ct in range(4, NC):
                last = e.memset(BA[:, ct * T:ct * T + 1], 0.0)
            return last
        hqzt = p.step("vector", hqz, deps=[hqt], tag=f"l1f_hqz{l}")
        hq.w = [hqt, hqzt]

        wqkv = k.claim("BB", 0, 12288, f"l1f_wqkv{l}")
        tok = p.dma("gpsimd", [
            (BB[:, 0:12288].rearrange("p (ct o) -> p ct o", o=1536),
             dram[f"wqkv{l}"][:, :].rearrange("(ct p) o -> p ct o", p=128)),
        ], deps=wqkv.init, tag=f"l1f_wqkvL{l}")
        wqkv.w.append(tok)
        vbuf = k.claim("BC", 0, 8320, f"l1f_V{l}")
        onestok = p.step("vector",
                         lambda e: e.memset(BC[:, 0:8320].rearrange("p (tt h d) -> p tt h d", h=8, d=65)[:, :, :, 64:65], 1.0),
                         deps=vbuf.init, tag=f"l1f_vones{l}")
        vbuf.w.append(onestok)
        qtb = k.claim("BB", 12288, 20480, f"l1f_QT{l}")
        ktbB = k.claim("BB", 20480, 24576, f"l1f_KTb{l}")
        ktbA = k.claim("BA", 16384, 20480, f"l1f_KTa{l}")

        def kt_ap(h, c0, c1):
            kt = h // 2
            pb = (h % 2) * 64
            if kt < 2:
                return BB[pb:pb + 64, 20480 + kt * 2048 + c0:20480 + kt * 2048 + c1]
            return BA[pb:pb + 64, 16384 + (kt - 2) * 2048 + c0:16384 + (kt - 2) * 2048 + c1]

        def qt_ap(h, c0, c1):
            pb = (h % 2) * 64
            return BB[pb:pb + 64, 12288 + (h // 2) * 2048 + c0:12288 + (h // 2) * 2048 + c1]

        ewq = k.claim("EW", 0, 4096, f"l1f_ew{l}")
        for tt in range(NT):
            par = tt % 2
            QKV0 = PS[0] if par == 0 else PS[2]
            QKV1 = PS[1] if par == 0 else PS[3]
            qbufs = (psb[0], psb2[0], psb[1]) if par == 0 else (psb[2], psb2[2], psb[3])

            def mb(e, _tt=tt, _p0=QKV0, _p1=QKV1):
                last = None
                for oc in range(3):
                    pb = _p0[:, oc * 512:(oc + 1) * 512] if oc < 2 else _p1[:, 0:512]
                    for ct in range(NC):
                        last = e.matmul(pb, BA[:, ct * T + _tt * 128:ct * T + (_tt + 1) * 128],
                                        BB[:, ct * 1536 + oc * 512:ct * 1536 + (oc + 1) * 512],
                                        start=(ct == 0), stop=(ct == NC - 1))
                return last
            bdeps = []
            for b in qbufs:
                bdeps += b.wr_deps()
            mtok2 = p.step("tensor", mb, deps=hq.rd_deps() + wqkv.rd_deps() + bdeps, tag=f"l1f_qkv{l}_{tt}")
            hq.readers.append(mtok2); wqkv.readers.append(mtok2)
            for b in qbufs:
                b.w = [mtok2]; b.readers = []

            def sev(e, _tt=tt, _p0=QKV0, _p1=QKV1):
                e.copy(EWF[:, 0:512], _p0[:, 0:512])
                e.copy(EWF[:, 512:1024], _p0[:, 512:1024])
                e.activation(EWF[:, 1024:1536], _p0[:, 0:512], ACT.Square)
                e.activation(EWF[:, 1536:2048], _p0[:, 512:1024], ACT.Square)
                dst = BC[:, _tt * 520:(_tt + 1) * 520].rearrange("p (h d) -> p h d", d=65)[:, :, 0:64]
                return e.copy(dst, _p1[:, 0:512].rearrange("p (h d) -> p h d", d=64))
            setok = p.step("scalar", sev, deps=[mtok2] + ewq.wr_deps() + vbuf.wr_deps(), tag=f"l1f_sev{l}_{tt}")
            for b in qbufs:
                b.readers.append(setok)
            ewq.w = [setok]; ewq.readers = []; ewq.init = []
            vbuf.w.append(setok)

            if tt == 0:
                nrm = k.claim("SM", 256, 304, f"l1f_nrm{l}")
                k._l1f_nrm = nrm
            nrm = k._l1f_nrm

            def vn1(e):
                e.tensor_reduce(SM[:, 256:264], EWF[:, 1024:1536].rearrange("p (h d) -> p h d", d=64),
                                axis=AX.X, op=ALU.add)
                e.tensor_reduce(SM[:, 264:272], EWF[:, 1536:2048].rearrange("p (h d) -> p h d", d=64),
                                axis=AX.X, op=ALU.add)
                e.tensor_scalar(SM[:, 256:264], SM[:, 256:264], 1e-24, None, ALU.max)
                return e.tensor_scalar(SM[:, 264:272], SM[:, 264:272], 1e-24, None, ALU.max)
            v1t = p.step("vector", vn1, deps=[setok] + nrm.wr_deps(), tag=f"l1f_n1{l}_{tt}")
            nrm.w = [v1t]; nrm.readers = []; nrm.init = []
            s1t = p.step("scalar", lambda e: (e.activation(SM[:, 272:280], SM[:, 256:264], ACT.Sqrt),
                                              e.activation(SM[:, 280:288], SM[:, 264:272], ACT.Sqrt))[-1],
                         deps=[v1t], tag=f"l1f_n2{l}_{tt}")
            nrm.w.append(s1t)

            def vn2(e):
                e.reciprocal(SM[:, 288:296], SM[:, 272:280])
                e.reciprocal(SM[:, 296:304], SM[:, 280:288])
                e.tensor_tensor(SM[:, 272:280], SM[:, 288:296], SM[:, 288:296], ALU.mult)
                e.tensor_tensor(SM[:, 280:288], SM[:, 296:304], SM[:, 296:304], ALU.mult)
                e.tensor_tensor(SM[:, 272:280], SM[:, 272:280], SM[:, 256:264], ALU.mult)
                e.tensor_tensor(SM[:, 280:288], SM[:, 280:288], SM[:, 264:272], ALU.mult)
                e.tensor_scalar(SM[:, 272:280], SM[:, 272:280], -0.5, 1.5, ALU.mult, ALU.add)
                e.tensor_scalar(SM[:, 280:288], SM[:, 280:288], -0.5, 1.5, ALU.mult, ALU.add)
                e.tensor_tensor(SM[:, 288:296], SM[:, 288:296], SM[:, 272:280], ALU.mult)
                return e.tensor_tensor(SM[:, 296:304], SM[:, 296:304], SM[:, 280:288], ALU.mult)
            v2t = p.step("vector", vn2, deps=[s1t], tag=f"l1f_n3{l}_{tt}")
            nrm.w.append(v2t)

            def vnm(e):
                last = None
                for h in range(8):
                    last = e.tensor_scalar(EWF[:, h * 64:(h + 1) * 64], EWF[:, h * 64:(h + 1) * 64],
                                           SM[:, 288 + h:289 + h], None, ALU.mult)
                for h in range(8):
                    last = e.tensor_scalar(EWF[:, 512 + h * 64:512 + (h + 1) * 64],
                                           EWF[:, 512 + h * 64:512 + (h + 1) * 64],
                                           SM[:, 296 + h:297 + h], None, ALU.mult)
                return last
            v3t = p.step("vector", vnm, deps=[v2t], tag=f"l1f_nm{l}_{tt}")
            nrm.readers.append(v3t)
            ewq.w.append(v3t)

            def rope(e, _tt=tt):
                qn = EWF[:, 0:512].rearrange("p (h d) -> p h d", d=64)
                kn = EWF[:, 512:1024].rearrange("p (h d) -> p h d", d=64)
                qp = EW[:, 2048:2560].rearrange("p (h d) -> p h d", d=64)
                kp = EW[:, 2560:3072].rearrange("p (h d) -> p h d", d=64)
                bq = EW[:, 3072:3584].rearrange("p (h d) -> p h d", d=64)
                bk = EW[:, 3584:4096].rearrange("p (h d) -> p h d", d=64)
                tc = lambda i: BC[:, ETC + i * 1024 + _tt * 64:ETC + i * 1024 + (_tt + 1) * 64]
                qc = tc(0).unsqueeze(1).broadcast_to((128, 8, 64))
                qs_ = tc(1); kc = tc(2).unsqueeze(1).broadcast_to((128, 8, 64)); ks_ = tc(3)
                e.tensor_tensor(qp, qn, qc, ALU.mult)
                e.tensor_tensor(kp, kn, kc, ALU.mult)
                e.tensor_tensor(bq[:, :, 0:32], qn[:, :, 32:64],
                                qs_[:, 0:32].unsqueeze(1).broadcast_to((128, 8, 32)), ALU.mult)
                e.tensor_tensor(bk[:, :, 0:32], kn[:, :, 32:64],
                                ks_[:, 0:32].unsqueeze(1).broadcast_to((128, 8, 32)), ALU.mult)
                e.tensor_tensor(bq[:, :, 32:64], qn[:, :, 0:32],
                                qs_[:, 32:64].unsqueeze(1).broadcast_to((128, 8, 32)), ALU.mult)
                e.tensor_tensor(bk[:, :, 32:64], kn[:, :, 0:32],
                                ks_[:, 32:64].unsqueeze(1).broadcast_to((128, 8, 32)), ALU.mult)
                e.tensor_tensor(qp[:, :, 0:32], qp[:, :, 0:32], bq[:, :, 0:32], ALU.subtract)
                e.tensor_tensor(kp[:, :, 0:32], kp[:, :, 0:32], bk[:, :, 0:32], ALU.subtract)
                e.tensor_tensor(qp[:, :, 32:64], qp[:, :, 32:64], bq[:, :, 32:64], ALU.add)
                return e.tensor_tensor(kp[:, :, 32:64], kp[:, :, 32:64], bk[:, :, 32:64], ALU.add)
            v4t = p.step("gpsimd", rope, deps=[v3t] + etabs.rd_deps(), tag=f"l1f_rope{l}_{tt}")
            etabs.readers.append(v4t)
            ewq.w.append(v4t)

            bank2 = psb2[1] if par == 0 else psb2[3]
            pb16 = (PS[1] if par == 0 else PS[3])[:, 512:1024].bitcast(BF16)

            def tb2(e, _pb=pb16):
                last = None
                for z in range(4):
                    last = e.transpose(_pb[:, z * 128:(z + 1) * 128],
                                       EW[:, 2048 + z * 128:2048 + (z + 1) * 128], cl["ident_bf16"])
                for z in range(4):
                    last = e.transpose(_pb[:, 512 + z * 128:512 + (z + 1) * 128],
                                       EW[:, 2560 + z * 128:2560 + (z + 1) * 128], cl["ident_bf16"])
                return last
            t2t = p.step("tensor", tb2, deps=[v4t] + bank2.wr_deps() + cl["cst_tok"], tag=f"l1f_qtr{l}_{tt}")
            ewq.readers.append(t2t)
            bank2.w = [t2t]; bank2.readers = []

            def ebd2(e, _pb=pb16, _tt=tt):
                dstq = BB[:, 12288:20480].rearrange("p (qt t) -> p qt t", t=T)[:, :, _tt * 128:(_tt + 1) * 128]
                e.tensor_copy(dstq, _pb[:, 0:512].rearrange("p (qt t) -> p qt t", t=128))
                dstk1 = BB[:, 20480:24576].rearrange("p (kt t) -> p kt t", t=T)[:, :, _tt * 128:(_tt + 1) * 128]
                e.tensor_copy(dstk1, _pb[:, 512:768].rearrange("p (kt t) -> p kt t", t=128))
                dstk2 = BA[:, 16384:20480].rearrange("p (kt t) -> p kt t", t=T)[:, :, _tt * 128:(_tt + 1) * 128]
                return e.tensor_copy(dstk2, _pb[:, 768:1024].rearrange("p (kt t) -> p kt t", t=128))
            e2t = p.step("vector", ebd2,
                         deps=[t2t] + (qtb.init + ktbB.init + ktbA.init if tt == 0 else []),
                         tag=f"l1f_qte{l}_{tt}")
            bank2.readers.append(e2t)
            qtb.w.append(e2t); ktbB.w.append(e2t); ktbA.w.append(e2t)

        obuf = k.claim("BA", 0, 8192, f"l1f_o{l}")
        pscr = [k.claim("BA", 8192 + i * 640, 8192 + (i + 1) * 640, f"l1f_ps{l}_{i}") for i in range(2)]
        ptscr = [k.claim("BA", 9472 + i * 640, 9472 + (i + 1) * 640, f"l1f_pts{l}_{i}") for i in range(2)]
        items = [(h, w, s) for w in range(4) for s in range(4) for h in range(8)]

        def la_S(idx):
            h, w, s = items[idx]
            jq = w * 4 + s
            nk = (s + 1) if w == 0 else 5
            kst = 0 if w == 0 else (w - 1) * 512 + s * 128
            par = idx % 2
            sbank = psb[par]
            spb = PS[par][:, 0:1024]

            def smm(e, _h=h, _jq=jq, _kst=kst, _nk=nk, _pb=spb):
                n1 = min(_nk * 128, 512)
                last = e.matmul(_pb[:, 0:n1], qt_ap(_h, _jq * 128, (_jq + 1) * 128),
                                kt_ap(_h, _kst, _kst + n1), start=True, stop=True)
                if _nk * 128 > 512:
                    last = e.matmul(_pb[:, 512:_nk * 128], qt_ap(_h, _jq * 128, (_jq + 1) * 128),
                                    kt_ap(_h, _kst + 512, _kst + _nk * 128), start=True, stop=True)
                return last
            sdeps = qtb.rd_deps() + (ktbB.rd_deps() if h < 4 else ktbA.rd_deps()) + \
                sbank.wr_deps() + psb2[par].wr_deps()
            stok2 = p.step("tensor", smm, deps=sdeps, tag=f"l1f_s{l}_{h}_{jq}")
            qtb.readers.append(stok2)
            (ktbB if h < 4 else ktbA).readers.append(stok2)
            sbank.w = [stok2]; sbank.readers = []
            psb2[par].w = [stok2]; psb2[par].readers = []

            ngm = SM[:, 256 + par:257 + par]
            mxt = p.step("vector",
                         lambda e, _pb=spb, _nk=nk, _n=ngm:
                         e.tensor_reduce(_n, _pb[:, 0:_nk * 128], axis=AX.X, op=ALU.max, negate=True),
                         deps=[stok2], tag=f"l1f_mx{l}_{h}_{jq}")
            sbank.readers.append(mxt); psb2[par].readers.append(mxt)

            pb_ = pscr[par]
            pslot = 8192 + par * 640
            ext = p.step("scalar",
                         lambda e, _pb=spb, _nk=nk, _n=ngm, _ps=pslot:
                         e.activation(BA[:, _ps:_ps + _nk * 128], _pb[:, 0:_nk * 128],
                                      ACT.Exp, bias=_n),
                         deps=[mxt] + pb_.wr_deps(), tag=f"l1f_ex{l}_{h}_{jq}")
            sbank.readers.append(ext); psb2[par].readers.append(ext)
            pb_.w = [ext]; pb_.readers = []; pb_.init = []

            def msk(e, _w=w, _nk=nk, _ps=pslot):
                if _w == 0:
                    ap = BA[:, _ps + (_nk - 1) * 128:_ps + _nk * 128]
                    return e.tensor_tensor(ap, ap, cl["loc_diag"], ALU.mult)
                ap0 = BA[:, _ps:_ps + 128]
                e.tensor_tensor(ap0, ap0, cl["loc_left"], ALU.mult)
                ap1 = BA[:, _ps + 512:_ps + 640]
                return e.tensor_tensor(ap1, ap1, cl["loc_diag"], ALU.mult)
            mkt = p.step("gpsimd", msk, deps=[ext] + cl["cst_tok"], tag=f"l1f_mk{l}_{h}_{jq}")
            pb_.w.append(mkt)
            return (idx, nk, kst, pslot, mkt)

        def la_rest(state):
            idx, nk, kst, pslot, mkt = state
            h, w, s = items[idx]
            jq = w * 4 + s
            par = idx % 2
            pb_ = pscr[par]
            ptb_ = ptscr[par]
            tb_bank = psb2[2 + par]
            ptb16 = PS[2 + par][:, 512:1024].bitcast(BF16)

            def ptr(e, _nk=nk, _ps=pslot, _pb=ptb16):
                last = None
                for z in range(_nk):
                    last = e.transpose(_pb[:, z * 128:(z + 1) * 128],
                                       BA[:, _ps + z * 128:_ps + (z + 1) * 128], cl["ident_bf16"])
                return last
            ptt = p.step("tensor", ptr, deps=[mkt] + tb_bank.wr_deps() + cl["cst_tok"],
                         tag=f"l1f_pt{l}_{h}_{jq}")
            pb_.readers.append(ptt)
            tb_bank.w = [ptt]; tb_bank.readers = []

            ptslot = 9472 + par * 640
            pte = p.step("scalar",
                         lambda e, _nk=nk, _pb=ptb16, _ps=ptslot:
                         e.copy(BA[:, _ps:_ps + _nk * 128], _pb[:, 0:_nk * 128]),
                         deps=[ptt] + ptb_.wr_deps(), tag=f"l1f_pte{l}_{h}_{jq}")
            tb_bank.readers.append(pte)
            ptb_.w = [pte]; ptb_.readers = []; ptb_.init = []

            pvbank = psb[2 + par]
            pvpb = PS[2 + par][:, 0:65]

            def pvm(e, _h=h, _nk=nk, _kst=kst, _ps=ptslot, _pb=pvpb):
                last = None
                for z in range(_nk):
                    ktile = (_kst + z * 128) // 128
                    last = e.matmul(_pb, BA[:, _ps + z * 128:_ps + (z + 1) * 128],
                                    BC[:, ktile * 520 + _h * 65:ktile * 520 + (_h + 1) * 65],
                                    start=(z == 0), stop=(z == _nk - 1))
                return last
            pvt = p.step("tensor", pvm, deps=[pte] + vbuf.rd_deps() + pvbank.wr_deps(),
                         tag=f"l1f_pv{l}_{h}_{jq}")
            ptb_.readers.append(pvt); vbuf.readers.append(pvt)
            pvbank.w = [pvt]; pvbank.readers = []

            dcol = SM[:, 258 + par:259 + par]
            d1 = p.step("vector", lambda e, _pb=pvpb, _d=dcol: e.reciprocal(_d, _pb[:, 64:65]),
                        deps=[pvt], tag=f"l1f_dr{l}_{h}_{jq}")
            d2 = p.step("vector",
                        lambda e, _pb=pvpb, _d=dcol, _jq=jq, _h=h:
                        e.tensor_scalar(BA[:, _jq * 512 + _h * 64:_jq * 512 + (_h + 1) * 64],
                                        _pb[:, 0:64], _d, None, ALU.mult),
                        deps=[d1] + (obuf.init if idx == 0 else []),
                        tag=f"l1f_dv{l}_{h}_{jq}")
            pvbank.readers.append(d2)
            obuf.w.append(d2)

        pending = None
        for idx in range(len(items)):
            st = la_S(idx)
            if pending is not None:
                la_rest(pending)
            pending = st
        la_rest(pending)

        woslab = k.claim("BB", 0, 4096, f"l1f_wo{l}")
        tok = p.dma("gpsimd", [
            (BB[:, 0:4096].rearrange("p (ot c) -> p ot c", c=C),
             dram[f"wo{l}"][:, :].rearrange("(ot p) c -> p ot c", p=128)),
        ], deps=woslab.init, tag=f"l1f_woL{l}")
        woslab.w.append(tok)
        otb = k.claim("BA", 8192, 16384, f"l1f_oT{l}")
        for g in range(8):
            bankt = psb[g % 2]
            ptb16 = PS[g % 2][:, 0:512].bitcast(BF16)

            def trb(e, _g=g, _pb=ptb16):
                last = None
                for z in range(8):
                    idx = _g * 8 + z
                    ot, tt = idx // 16, idx % 16
                    last = e.transpose(_pb[:, z * 128:(z + 1) * 128],
                                       BA[:, tt * 512 + ot * 128:tt * 512 + (ot + 1) * 128],
                                       cl["ident_bf16"])
                return last
            ttok = p.step("tensor", trb, deps=obuf.rd_deps() + bankt.wr_deps() + cl["cst_tok"],
                          tag=f"l1f_otr{l}_{g}")
            obuf.readers.append(ttok)
            bankt.w = [ttok]; bankt.readers = []

            def tev(e, _g=g, _pb=ptb16):
                last = None
                for z in range(8):
                    idx = _g * 8 + z
                    ot, tt = idx // 16, idx % 16
                    last = e.tensor_copy(BA[:, 8192 + ot * T + tt * 128:8192 + ot * T + (tt + 1) * 128],
                                         _pb[:, z * 128:(z + 1) * 128])
                return last
            etok = p.step("vector", tev, deps=[ttok] + (otb.init if g == 0 else []), tag=f"l1f_ote{l}_{g}")
            bankt.readers.append(etok)
            otb.w.append(etok)

        for ct in range(NC):
            for ch in range(4):
                idx = ct * 4 + ch
                bank = psb[2 + idx % 2]
                pbank = PS[2 + idx % 2][:, 0:512]

                def womm(e, _ct=ct, _ch=ch, _pb=pbank):
                    last = None
                    for ot in range(4):
                        last = e.matmul(_pb, BB[:, ot * C + _ct * 128:ot * C + (_ct + 1) * 128],
                                        BA[:, 8192 + ot * T + _ch * 512:8192 + ot * T + (_ch + 1) * 512],
                                        start=(ot == 0), stop=(ot == 3))
                    return last
                wt = p.step("tensor", womm, deps=otb.rd_deps() + woslab.rd_deps() + bank.wr_deps(),
                            tag=f"l1f_wom{l}_{ct}_{ch}")
                otb.readers.append(wt); woslab.readers.append(wt)
                bank.w = [wt]; bank.readers = []

                def wadd(e, _ct=ct, _ch=ch, _pb=pbank):
                    ssl = XFf[:, _ct * T + _ch * 512:_ct * T + (_ch + 1) * 512]
                    return e.tensor_tensor(ssl, _pb, ssl, ALU.add)
                at = p.step("vector", wadd, deps=[wt] + sT.wr_deps(), tag=f"l1f_woa{l}_{ct}_{ch}")
                bank.readers.append(at)
                sT.w.append(at)

        colbuf = k.claim("SM", 288, 360, f"l1f_cols{l}")
        tok = p.dma("sync", [
            (SM[:, 288:296], dram[f"lg{l}"][0, :].rearrange("(ct p) -> p ct", p=128)),
            (SM[:, 296:304], dram[f"lb{l}"][0, :].rearrange("(ct p) -> p ct", p=128)),
            (SM[:, 304:312], dram[f"b2_{l}"][0, :].rearrange("(ct p) -> p ct", p=128)),
            (SM[:, 312:334], dram[f"b1a{l}"][0, :].rearrange("(jt p) -> p jt", p=128)),
            (SM[:, 334:356], dram[f"b1g{l}"][0, :].rearrange("(jt p) -> p jt", p=128)),
        ], deps=colbuf.init, tag=f"l1f_cols{l}", slow=True)
        colbuf.w.append(tok)

        hb = k.claim("BA", 0, 16384, f"l1f_hb{l}")

        def mkhb(e):
            last = None
            for ct in range(NC):
                src2 = XFf[:, ct * T:(ct + 1) * T]
                dst = BA[:, ct * T:(ct + 1) * T]
                if ct < 4:
                    last = e.copy(dst, src2)
                else:
                    last = e.copy(dst[:, 1:T], src2[:, 0:T - 1])
            return last
        hbt = p.step("scalar", mkhb, deps=sT.rd_deps() + hb.init, tag=f"l1f_hb{l}")
        sT.readers.append(hbt)
        hbz = p.step("vector", lambda e: [e.memset(BA[:, ct * T:ct * T + 1], 0.0) for ct in range(4, NC)][-1],
                     deps=[hbt], tag=f"l1f_hbz{l}")
        hb.w = [hbt, hbz]

        strows = k.claim("BB", 0, 20480, f"l1f_strows{l}")
        sqscr = k.claim("EW", 0, 4096, f"l1f_sq{l}")
        row_toks = []
        for ch in range(4):
            mm_deps = psb[2].wr_deps() + psb2[2].wr_deps()
            for ct in range(NC):
                sqt = p.step("scalar",
                             lambda e, _ct=ct, _ch=ch:
                             e.activation(EW[:, (_ct % 2) * 1024:(_ct % 2) * 1024 + 512],
                                          BA[:, _ct * T + _ch * 512:_ct * T + (_ch + 1) * 512],
                                          ACT.Square),
                             deps=hb.rd_deps() + sqscr.wr_deps(), tag=f"l1f_sqp{l}_{ch}_{ct}")
                hb.readers.append(sqt)
                sqscr.w = [sqt]; sqscr.readers = []; sqscr.init = []

                def omm(e, _ct=ct, _ch=ch):
                    e.matmul(PS[2][0:1, 0:512], cl["ones_col"],
                             BA[:, _ct * T + _ch * 512:_ct * T + (_ch + 1) * 512],
                             start=(_ct == 0), stop=(_ct == NC - 1))
                    return e.matmul(PS[2][0:1, 512:1024], cl["ones_col"],
                                    EW[:, (_ct % 2) * 1024:(_ct % 2) * 1024 + 512],
                                    start=(_ct == 0), stop=(_ct == NC - 1))
                omt = p.step("tensor", omm,
                             deps=[sqt] + hb.rd_deps() + (mm_deps if ct == 0 else []) + cl["cst_tok"],
                             tag=f"l1f_om{l}_{ch}_{ct}")
                hb.readers.append(omt)
                sqscr.readers.append(omt)
                if ct == 0:
                    psb[2].w = [omt]; psb[2].readers = []
                    psb2[2].w = [omt]; psb2[2].readers = []
                else:
                    psb[2].w.append(omt); psb2[2].w.append(omt)
            rt = p.step("vector",
                        lambda e, _ch=ch: (e.tensor_copy(BBF[0:1, _ch * 512:(_ch + 1) * 512], PS[2][0:1, 0:512]),
                                           e.tensor_copy(BBF[0:1, 2048 + _ch * 512:2048 + (_ch + 1) * 512],
                                                         PS[2][0:1, 512:1024]))[-1],
                        deps=psb[2].rd_deps() + psb2[2].rd_deps() + (strows.init if ch == 0 else []),
                        tag=f"l1f_row{l}_{ch}")
            psb[2].readers.append(rt); psb2[2].readers.append(rt)
            strows.w.append(rt)
            row_toks.append(rt)

        R = lambda c0, c1: BBF[0:1, c0:c1]

        def rows1(e):
            e.tensor_tensor(R(8192, 10240), R(0, 2048), R(0, 2048), ALU.mult)
            e.tensor_scalar(R(4096, 6144), R(0, 2048), 1.0 / C, None, ALU.mult)
            e.tensor_scalar(R(8192, 10240), R(8192, 10240), -1.0 / C, None, ALU.mult)
            e.tensor_scalar(R(0, 2048), R(0, 2048), 1.0, None, ALU.mult)
            e.tensor_tensor(R(2048, 4096), R(2048, 4096), R(8192, 10240), ALU.add)
            e.tensor_scalar(R(8192, 10240), R(8192, 10240), 1.0, None, ALU.mult)
            return e.tensor_scalar(R(2048, 4096), R(2048, 4096), 1.0 / C, 1e-5, ALU.mult, ALU.add)
        r1 = p.step("vector", rows1, deps=row_toks, tag=f"l1f_rs1{l}")
        r2 = p.step("scalar", lambda e: e.activation(R(8192, 10240), R(2048, 4096), ACT.Sqrt),
                    deps=[r1], tag=f"l1f_rs2{l}")

        def rows2(e):
            e.reciprocal(R(6144, 8192), R(8192, 10240))
            e.tensor_tensor(R(8192, 10240), R(6144, 8192), R(6144, 8192), ALU.mult)
            e.tensor_tensor(R(8192, 10240), R(8192, 10240), R(2048, 4096), ALU.mult)
            e.tensor_scalar(R(8192, 10240), R(8192, 10240), -0.5, 1.5, ALU.mult, ALU.add)
            e.tensor_tensor(R(6144, 8192), R(6144, 8192), R(8192, 10240), ALU.mult)
            e.tensor_scalar(R(4096, 6144), R(4096, 6144), -1.0, None, ALU.mult)
            return e.tensor_tensor(R(4096, 6144), R(4096, 6144), R(6144, 8192), ALU.mult)
        r3 = p.step("vector", rows2, deps=[r2], tag=f"l1f_rs3{l}")
        strows.w += [r1, r2, r3]
        rsp = p.dma("sync", [(dram["rowspill"][0:1, :], R(4096, 6144)),
                             (dram["rowspill"][1:2, :], R(6144, 8192))],
                    deps=[r3], tag=f"l1f_rsp{l}")
        strows.readers.append(rsp)

        w2slab = k.claim("BB", 0, 22528, f"l1f_w2{l}")
        tokw2 = p.dma("gpsimd", [
            (BB[:, 0:22528].rearrange("p (jt c) -> p jt c", c=C),
             dram[f"w2p{l}"][:, :].rearrange("(jt p) c -> p jt c", p=128)),
        ], deps=w2slab.init + [rsp], tag=f"l1f_w2L{l}")
        w2slab.w.append(tokw2)

        wst = [k.claim("BC", 2048 + i * 1024, 2048 + (i + 1) * 1024, f"l1f_w1s{l}_{i}") for i in range(6)]

        for qtr in range(4):
            rowrest = k.claim("EW", 2048, 4096, f"l1f_rows{l}_{qtr}")
            rrt = p.dma("sync", [(EWF[0:1, 1024:1536], dram["rowspill"][0:1, qtr * 512:(qtr + 1) * 512]),
                                 (EWF[0:1, 1536:2048], dram["rowspill"][1:2, qtr * 512:(qtr + 1) * 512])],
                        deps=rowrest.init + [rsp], tag=f"l1f_rr{l}_{qtr}")
            rowrest.w = [rrt]
            def bc1(e):
                e.matmul(PS[3][:, 0:512], ones_f32, EWF[0:1, 1024:1536], start=True, stop=True)
                return e.matmul(PS[3][:, 512:1024], ones_f32, EWF[0:1, 1536:2048], start=True, stop=True)
            b1t = p.step("tensor", bc1, deps=[rrt, o32tok] + psb[3].wr_deps() + psb2[3].wr_deps(),
                         tag=f"l1f_bc{l}_{qtr}")
            rowrest.readers.append(b1t)
            psb[3].w = [b1t]; psb[3].readers = []
            psb2[3].w = [b1t]; psb2[3].readers = []
            b2t = p.step("vector", lambda e: (e.tensor_copy(EWF[:, 0:512], PS[3][:, 0:512]),
                                              e.tensor_copy(EWF[:, 512:1024], PS[3][:, 512:1024]))[-1],
                         deps=[b1t] + sqscr.wr_deps() + sqscr.readers, tag=f"l1f_bce{l}_{qtr}")
            psb[3].readers.append(b2t); psb2[3].readers.append(b2t)
            sqscr.w = [b2t]; sqscr.readers = []

            h2nq = k.claim("BA", 0, 4096, f"l1f_h2n{l}_{qtr}")
            t0buf = k.claim("BC", 0, 2048, f"l1f_t0{l}_{qtr}")
            t0A, t0B = BCF[:, 0:512], BCF[:, 512:1024]

            def h2pair(e, _ca, _cb, _q):
                lo = _q * 512
                for t0, ct in ((t0A, _ca), (t0B, _cb)):
                    if ct < 4 or lo > 0:
                        off = 0 if ct < 4 else -1
                        e.tensor_tensor(t0, XFf[:, ct * T + lo + off:ct * T + lo + off + 512],
                                        EWF[:, 512:1024], ALU.mult)
                    else:
                        e.tensor_tensor(t0[:, 1:512], XFf[:, ct * T:ct * T + 511],
                                        EWF[:, 513:1024], ALU.mult)
                        e.memset(t0[:, 0:1], 0.0)
                for t0, ct in ((t0A, _ca), (t0B, _cb)):
                    e.tensor_tensor(t0, t0, EWF[:, 0:512], ALU.add)
                last = None
                for t0, ct in ((t0A, _ca), (t0B, _cb)):
                    last = e.tensor_scalar(BA[:, ct * 512:(ct + 1) * 512], t0,
                                           SM[:, 288 + ct:289 + ct], SM[:, 296 + ct:297 + ct],
                                           ALU.mult, ALU.add)
                return last
            for cp in range(0, NC, 2):
                htok = p.step("vector",
                              lambda e, _ca=cp, _cb=cp + 1, _q=qtr: h2pair(e, _ca, _cb, _q),
                              deps=[b2t] + sT.rd_deps() + colbuf.rd_deps() + h2nq.wr_deps() +
                                   (h2nq.init + t0buf.init if cp == 0 else []),
                              tag=f"l1f_h2n{l}_{qtr}_{cp}")
                sT.readers.append(htok); colbuf.readers.append(htok)
                h2nq.w.append(htok)
                t0buf.w.append(htok)
                sqscr.readers.append(htok)

            hts = k.claim("BA", 4096, 15360, f"l1f_ht{l}_{qtr}")
            gsc2 = k.claim("EW", 2048, 4096, f"l1f_ab{l}_{qtr}")
            for jt in range(22):
                sa, sg = wst[(jt % 3) * 2], wst[(jt % 3) * 2 + 1]
                base_a = 2048 + (jt % 3) * 2 * 1024
                base_g = base_a + 1024
                wtok = p.dma("gpsimd", [
                    (BC[:, base_a:base_a + 1024].rearrange("p (ct j) -> p ct j", j=128),
                     dram[f"w1a{l}"][:, jt * 128:(jt + 1) * 128].rearrange("(ct p) j -> p ct j", p=128)),
                    (BC[:, base_g:base_g + 1024].rearrange("p (ct j) -> p ct j", j=128),
                     dram[f"w1g{l}"][:, jt * 128:(jt + 1) * 128].rearrange("(ct p) j -> p ct j", p=128)),
                ], deps=sa.wr_deps() + sg.wr_deps(), tag=f"l1f_w1{l}_{qtr}_{jt}")
                sa.w = [wtok]; sa.readers = []; sa.init = []
                sg.w = [wtok]; sg.readers = []; sg.init = []

                bank = psb[jt % 2]
                pa = PS[jt % 2][:, 0:512]
                pg = PS[jt % 2][:, 512:1024]

                def ymm(e, _ba=base_a, _bg=base_g, _pa=pa, _pg=pg):
                    last = None
                    for ct in range(NC):
                        e.matmul(_pa, BC[:, _ba + ct * 128:_ba + (ct + 1) * 128],
                                 BA[:, ct * 512:(ct + 1) * 512], start=(ct == 0), stop=(ct == NC - 1))
                        last = e.matmul(_pg, BC[:, _bg + ct * 128:_bg + (ct + 1) * 128],
                                        BA[:, ct * 512:(ct + 1) * 512], start=(ct == 0), stop=(ct == NC - 1))
                    return last
                ymt = p.step("tensor", ymm,
                             deps=[wtok] + h2nq.rd_deps() + bank.wr_deps() + psb2[jt % 2].wr_deps(),
                             tag=f"l1f_y{l}_{qtr}_{jt}")
                sa.readers.append(ymt); sg.readers.append(ymt)
                h2nq.readers.append(ymt)
                bank.w = [ymt]; bank.readers = []
                psb2[jt % 2].w = [ymt]; psb2[jt % 2].readers = []

                def gact(e, _jt=jt, _pa=pa, _pg=pg):
                    e.activation(EW[:, 2048 + (_jt % 2) * 1024:2048 + (_jt % 2) * 1024 + 512], _pa,
                                 ACT.Identity, bias=SM[:, 312 + _jt:313 + _jt])
                    return e.activation(EW[:, 2048 + (_jt % 2) * 1024 + 512:2048 + (_jt % 2) * 1024 + 1024],
                                        _pg, ACT.Gelu, bias=SM[:, 334 + _jt:335 + _jt])
                gat = p.step("scalar", gact, deps=[ymt] + gsc2.wr_deps() + colbuf.rd_deps(),
                             tag=f"l1f_g{l}_{qtr}_{jt}")
                bank.readers.append(gat); psb2[jt % 2].readers.append(gat)
                colbuf.readers.append(gat)
                gsc2.w = [gat]; gsc2.readers = []; gsc2.init = []

                htk = p.step("vector",
                             lambda e, _jt=jt:
                             e.tensor_tensor(BA[:, 4096 + _jt * 512:4096 + (_jt + 1) * 512],
                                             EW[:, 2048 + (_jt % 2) * 1024:2048 + (_jt % 2) * 1024 + 512],
                                             EW[:, 2048 + (_jt % 2) * 1024 + 512:2048 + (_jt % 2) * 1024 + 1024],
                                             ALU.mult),
                             deps=[gat] + (hts.init if jt == 0 else []), tag=f"l1f_h{l}_{qtr}_{jt}")
                gsc2.readers.append(htk)
                hts.w.append(htk)

            for ct in range(NC):
                bank = psb[2 + ct % 2]
                pbank = PS[2 + ct % 2][:, 0:512]

                def wmm2(e, _ct=ct, _pb=pbank):
                    last = None
                    for jt in range(22):
                        last = e.matmul(_pb, BB[:, jt * C + _ct * 128:jt * C + (_ct + 1) * 128],
                                        BA[:, 4096 + jt * 512:4096 + (jt + 1) * 512],
                                        start=(jt == 0), stop=(jt == 21))
                    return last
                wt2 = p.step("tensor", wmm2, deps=hts.rd_deps() + w2slab.rd_deps() + bank.wr_deps(),
                             tag=f"l1f_w2m{l}_{qtr}_{ct}")
                hts.readers.append(wt2); w2slab.readers.append(wt2)
                bank.w = [wt2]; bank.readers = []

                def wadd2(e, _ct=ct, _q=qtr, _pb=pbank):
                    ssl = XFf[:, _ct * T + _q * 512:_ct * T + (_q + 1) * 512]
                    return e.scalar_tensor_tensor(ssl, _pb, SM[:, 304 + _ct:305 + _ct], ssl,
                                                  ALU.add, ALU.add)
                at2 = p.step("vector", wadd2, deps=[wt2] + sT.wr_deps() + colbuf.rd_deps(),
                             tag=f"l1f_w2a{l}_{qtr}_{ct}")
                bank.readers.append(at2)
                colbuf.readers.append(at2)
                sT.w.append(at2)

    k.y_done = True
    yscr = k.claim("EW", 0, 4096, "l1f_fin")
    ytoks = []
    for tt in range(NT):
        bank = psb[tt % 2]
        pbank = PS[tt % 2][:, 0:1024]

        def ftb(e, _tt=tt, _pb=pbank):
            last = None
            for ct in range(NC):
                last = e.transpose(_pb[:, ct * 128:(ct + 1) * 128],
                                   XFf[:, ct * T + _tt * 128:ct * T + (_tt + 1) * 128],
                                   cl["ident_f32"])
            return last
        ftok = p.step("tensor", ftb, deps=sT.rd_deps() + bank.wr_deps() + psb2[tt % 2].wr_deps() + cl["cst_tok"],
                      tag=f"l1f_fy{tt}")
        sT.readers.append(ftok)
        bank.w = [ftok]; bank.readers = []
        psb2[tt % 2].w = [ftok]; psb2[tt % 2].readers = []

        rtok = p.dma("sync", [(EWF[:, 0:1024], dram["xspill"][:, :].rearrange("(a p) c -> p a c", p=128)[:, tt, :])],
                     deps=yscr.wr_deps() + [sp_tok], tag=f"l1f_fx{tt}")
        yscr.w = [rtok]; yscr.readers = []; yscr.init = []

        vt = p.step("vector",
                    lambda e, _pb=pbank: e.tensor_tensor(EWF[:, 1024:2048], _pb, EWF[:, 0:1024], ALU.add),
                    deps=[ftok, rtok], tag=f"l1f_fv{tt}")
        bank.readers.append(vt); psb2[tt % 2].readers.append(vt)
        yscr.w.append(vt)

        ytok = p.dma("sync", [(dram["y"][:, :].rearrange("(a p) c -> p a c", p=128)[:, tt, :], EWF[:, 1024:2048])],
                     deps=[vt], tag=f"l1f_fy{tt}")
        yscr.readers.append(ytok)
        ytoks.append(ytok)
    p.step("sync", lambda e: e.nop(), deps=ytoks, tag="l1f_ywait")
    return xbuf


def _prep_inputs(inputs):
    g = {n: np.asarray(v) for n, v in inputs.items()}
    cst = _host_constants()
    base = {}
    base["L0_W1"] = np.ascontiguousarray(g["L0_attn_W1"], np.float32)
    base["L0_W2"] = np.ascontiguousarray(g["L0_attn_W2"], np.float32)
    base["L0_bias"] = np.ascontiguousarray(g["L0_attn_bias"][None, :], np.float32)
    etT = np.zeros((C, 32), np.float32)
    etT[:, :21] = g["L0_attn_embeddings_table"].T
    base["L0_etT"] = etT
    base["g1"] = np.ascontiguousarray(g["L0_ffn_g1"][None, :], np.float32)
    base["g2"] = np.ascontiguousarray(g["L0_ffn_g2"][None, :], np.float32)
    base["Wq"] = np.ascontiguousarray(g["L0_ffn_Wq"], np.float32)
    base["Wdkv"] = np.ascontiguousarray(g["L0_ffn_Wdkv"], np.float32)
    base["Wuk"] = np.ascontiguousarray(g["L0_ffn_Wuk"], np.float32)
    base["Wuv"] = np.ascontiguousarray(g["L0_ffn_Wuv"], np.float32)
    base["Wo"] = np.ascontiguousarray(g["L0_ffn_Wo"], np.float32)
    base["gate_w"] = np.ascontiguousarray(g["L0_ffn_gate_w"], np.float32)
    base["moe_w1"] = np.ascontiguousarray(g["L0_ffn_w1"], np.float32)
    base["moe_w2"] = np.ascontiguousarray(g["L0_ffn_w2"], np.float32)
    base["lat_qT"] = np.ascontiguousarray(np.transpose(g["L1_attn_latent_queries"][0], (1, 2, 0)), np.float32)
    base["WkT"] = np.ascontiguousarray(g["L1_attn_W_k"].T, np.float32)
    base["WvT"] = np.ascontiguousarray(g["L1_attn_W_v"].T, np.float32)
    base["WgT"] = np.ascontiguousarray(g["L1_attn_W_g"].T, np.float32)
    base["WpT"] = np.ascontiguousarray(g["L1_attn_W_p"].T, np.float32)
    for l in range(2):
        base[f"wqkv{l}"] = np.ascontiguousarray(g["L1_ffn_wqkv"][l], np.float32)
        base[f"qs{l}"] = np.ascontiguousarray(g["L1_ffn_q_scale"][l][None, :], np.float32)
        base[f"ks{l}"] = np.ascontiguousarray(g["L1_ffn_k_scale"][l][None, :], np.float32)
        base[f"wo{l}"] = np.ascontiguousarray(g["L1_ffn_wo"][l], np.float32)
        base[f"lg{l}"] = np.ascontiguousarray(g["L1_ffn_ln_g"][l][None, :], np.float32)
        base[f"lb{l}"] = np.ascontiguousarray(g["L1_ffn_ln_b"][l][None, :], np.float32)
        w1 = np.asarray(g["L1_ffn_w1"][l], np.float32)
        w1a = np.zeros((C, 2816), np.float32); w1a[:, :2730] = w1[:, :2730]
        w1g = np.zeros((C, 2816), np.float32); w1g[:, :2730] = w1[:, 2730:]
        base[f"w1a{l}"] = w1a; base[f"w1g{l}"] = w1g
        b1 = np.asarray(g["L1_ffn_b1"][l], np.float32)
        b1a = np.zeros((1, 2816), np.float32); b1a[0, :2730] = b1[:2730]
        b1g = np.zeros((1, 2816), np.float32); b1g[0, :2730] = b1[2730:]
        base[f"b1a{l}"] = b1a; base[f"b1g{l}"] = b1g
        w2p = np.zeros((2816, C), np.float32); w2p[:2730] = np.asarray(g["L1_ffn_w2"][l], np.float32)
        base[f"w2p{l}"] = w2p
        base[f"b2_{l}"] = np.ascontiguousarray(g["L1_ffn_b2"][l][None, :], np.float32)
    for name, v in cst.items():
        base["cst_" + name] = np.ascontiguousarray(v)
    in_maps = []
    X = np.ascontiguousarray(g["L0_attn_input_data"], np.float32)
    for b in range(8):
        m = dict(base)
        m["x_in"] = np.ascontiguousarray(X[b])
        in_maps.append(m)
    return in_maps


def kernel(**inputs):
    in_maps = _prep_inputs(inputs)
    nc = build_program()
    res = run_bass_kernel_spmd(nc, in_maps, core_ids=list(range(8)))
    return np.stack([np.asarray(r["y"], np.float32) for r in res.results], 0)


# version 13
# speedup vs baseline: 1.0179x; 1.0179x over previous
import math
import numpy as np
import ml_dtypes
from contextlib import ExitStack

import concourse.bass as bass
import concourse.mybir as mybir
from concourse.bass_utils import run_bass_kernel_spmd

F32 = mybir.dt.float32
BF16 = mybir.dt.bfloat16
AX = mybir.AxisListType
ALU = mybir.AluOpType
ACT = mybir.ActivationFunctionType

T = 2048
C = 1024
NT = T // 128
NC = C // 128
BF = ml_dtypes.bfloat16

ENGINES = ("tensor", "vector", "scalar", "gpsimd", "sync")
NDS = 4
DMAQ = {"sync": "qs", "gpsimd": "qg", "scalar": "qa"}
DMASEMS = [q + str(i) for q in DMAQ.values() for i in range(NDS)]


class Prog:

    def __init__(self, nc):
        self.nc = nc
        self.steps = {e: [] for e in ENGINES}
        self.cnt = {e: 0 for e in ENGINES}
        self.dcnt = {s: 0 for s in DMASEMS}
        self.drot = {q: 0 for q in DMAQ.values()}
        self.last_wait = {e: {} for e in ENGINES}

    def step(self, eng, body, deps=(), tag=""):
        waits = self._waits(eng, deps)
        self.cnt[eng] += 1
        tok = ("c_" + eng, self.cnt[eng])
        self.steps[eng].append((waits, body, [("c_" + eng, 1)], tag))
        return tok

    def dma(self, eng, pairs, deps=(), tag="", slow=False):
        q = DMAQ[eng]
        s = q + str(self.drot[q] % NDS)
        self.drot[q] += 1
        waits = self._waits(eng, deps)
        self.dcnt[s] += len(pairs)
        tok = (s, self.dcnt[s] * 16)

        def body(e, _pairs=pairs, _s=s, _slow=slow):
            for (o, i) in _pairs:
                e.dma_start(out=o, in_=i, allow_slow_non_contiguous=_slow).then_inc(self.sem[_s], 16)
            return None

        self.steps[eng].append((waits, body, [], tag))
        return tok

    def _waits(self, eng, deps):
        waits = {}
        lw = self.last_wait[eng]
        for (s, v) in deps:
            if s in self.dcnt:
                v = self.dcnt[s] * 16
            if lw.get(s, 0) >= v:
                continue
            if waits.get(s, 0) < v:
                waits[s] = v
        for s, v in waits.items():
            lw[s] = v
        return waits

    def render(self, blk, sems):
        self.sem = sems
        for eng in ENGINES:
            if not self.steps[eng]:
                continue
            section = getattr(blk, eng)

            def make(eng_name):
                def _(e):
                    for (waits, body, incs, tag) in self.steps[eng_name]:
                        for s, v in waits.items():
                            e.wait_ge(self.sem[s], v)
                        last = body(e)
                        for (s, dv) in incs:
                            if last is None:
                                e.nop().then_inc(self.sem[s], dv)
                            else:
                                last.then_inc(self.sem[s], dv)
                return _

            section(make(eng))


class Buf:
    __slots__ = ("w", "readers", "name", "init")

    def __init__(self, name, init_deps=()):
        self.name = name
        self.w = []
        self.readers = []
        self.init = list(init_deps)

    def rd_deps(self):
        return list(self.w)

    def wr_deps(self):
        return self.init + self.readers + self.w


class SBMap:

    def __init__(self):
        self.active = {}

    def claim(self, phys, c0, c1, name):
        lst = self.active.setdefault(phys, [])
        deps = []
        keep = []
        for (a, b, buf) in lst:
            if a < c1 and c0 < b:
                deps += buf.wr_deps()
                if a < c0:
                    keep.append((a, c0, buf))
                if c1 < b:
                    keep.append((c1, b, buf))
            else:
                keep.append((a, b, buf))
        nb = Buf(name, deps)
        keep.append((c0, c1, nb))
        self.active[phys] = keep
        return nb


def _host_constants():
    c = {}
    c["ident_f32"] = np.eye(128, dtype=np.float32)
    c["ident_bf16"] = np.eye(128, dtype=np.float32).astype(BF)
    c["ones_row"] = np.ones((1, 512), dtype=BF)
    c["ones_col"] = np.ones((128, 1), dtype=BF)
    kk, qq = np.meshgrid(np.arange(128), np.arange(128), indexing="ij")
    c["mla_diag_mask"] = (kk <= qq).astype(BF)
    cc, ll = np.meshgrid(np.arange(128), np.arange(64), indexing="ij")
    c["lat_mask"] = (cc <= ll).astype(BF)
    D = 64
    inv = 1.0 / (10000.0 ** (np.arange(0, D, 2, dtype=np.float64) / D))
    ang = np.arange(T, dtype=np.float64)[:, None] * inv[None, :]
    ctab = np.empty((T, 64), np.float64)
    stab = np.empty((T, 64), np.float64)
    ctab[:, 0::2] = np.cos(ang); ctab[:, 1::2] = np.cos(ang)
    stab[:, 0::2] = np.sin(ang); stab[:, 1::2] = np.sin(ang)
    c["l1a_cos"] = ctab.astype(np.float32)
    c["l1a_sin"] = stab.astype(np.float32)
    DH, WIN = 64, 512
    half = np.arange(0, DH, 2, dtype=np.float64)
    invf = 1.0 / (10000.0 ** (half / DH))
    t = np.arange(T, dtype=np.float64)
    freqs = t[:, None] * invf[None, :]
    freqs = np.concatenate([freqs, freqs], 1)
    base = (half + 0.4 * DH) / (1.4 * DH)
    power = (t - T // 2) / (WIN // 2)
    scale = base[None, :] ** power[:, None]
    scale = np.concatenate([scale, scale], 1)
    cosf, sinf = np.cos(freqs), np.sin(freqs)
    c["xpos_qc"] = (cosf * scale * 8.0).astype(np.float32)
    c["xpos_qs"] = (sinf * scale * 8.0).astype(np.float32)
    c["xpos_kc"] = (cosf / scale).astype(np.float32)
    c["xpos_ks"] = (sinf / scale).astype(np.float32)
    ii, jj = np.meshgrid(np.arange(128), np.arange(128), indexing="ij")
    c["loc_mask_left"] = (jj >= ii).astype(BF)
    c["loc_mask_diag"] = (jj <= ii).astype(BF)
    return c


def _per_layer(name, arr, l):
    return np.ascontiguousarray(arr[l])


class K:

    def __init__(self, nc, prog, sb, dram, st, const_names):
        self.nc = nc
        self.p = prog
        self.sb = sb
        self.dram = dram
        self.t = st
        self.consts = const_names
        self.psb = [Buf(f"bank{i}") for i in range(4)]
        self.psb2 = [Buf(f"bank{i}b") for i in range(4)]
        self.smbuf = [Buf(f"sm{i}") for i in range(8)]

    def nat(self, phys, tt, width=C, dtype_cols=None):
        return self.t[phys][:, tt * width:(tt + 1) * width]

    def claim(self, phys, c0, c1, name):
        return self.sb.claim(phys, c0, c1, name)


def _ceil(a, b):
    return (a + b - 1) // b


def build_program(depth_stages=99, dbg=False, sub=99):
    nc = bass.Bass("TRN2", target_bir_lowering=False)
    stack = ExitStack()

    dram = {}

    def din(name, shape, dtype=F32):
        dram[name] = nc.dram_tensor(name, list(shape), dtype, kind="ExternalInput").ap()
        return dram[name]

    def dout(name, shape, dtype=F32):
        dram[name] = nc.dram_tensor(name, list(shape), dtype, kind="ExternalOutput").ap()
        return dram[name]

    din("x_in", (T, C))
    din("L0_W1", (C, C)); din("L0_W2", (C, C))
    din("L0_bias", (1, C)); din("L0_etT", (C, 32))
    din("g1", (1, C)); din("g2", (1, C))
    din("Wq", (C, C)); din("Wdkv", (C, 512)); din("Wuk", (512, C)); din("Wuv", (512, C))
    din("Wo", (C, C)); din("gate_w", (C, 8))
    din("moe_w1", (8, C, 2048)); din("moe_w2", (8, 2048, C))
    din("lat_qT", (16, 64, 64))
    din("WkT", (C, C)); din("WvT", (C, C)); din("WgT", (C, C)); din("WpT", (C, C))
    for l in range(2):
        din(f"wqkv{l}", (C, 1536))
        din(f"qs{l}", (1, 64)); din(f"ks{l}", (1, 64))
        din(f"wo{l}", (512, C))
        din(f"lg{l}", (1, C)); din(f"lb{l}", (1, C))
        din(f"w1a{l}", (C, 2816)); din(f"w1g{l}", (C, 2816))
        din(f"b1a{l}", (1, 2816)); din(f"b1g{l}", (1, 2816))
        din(f"w2p{l}", (2816, C)); din(f"b2_{l}", (1, C))
    cst = _host_constants()
    for k, v in cst.items():
        din("cst_" + k, v.shape, BF16 if v.dtype == BF else F32)
    dout("y", (T, C))
    dout("xspill", (T, C))
    dout("rowspill", (2, 2048))
    if dbg:
        dout("dbgBA", (128, 24576), BF16); dout("dbgBB", (128, 24576), BF16); dout("dbgBC", (128, 16640), BF16)
        dout("dbgSM", (128, 512)); dout("dbgXF", (128, 16384)); dout("dbgEW", (128, 4096), BF16)

    st = {}
    def sbuf(name, shape, dtype):
        st[name] = stack.enter_context(nc.sbuf_tensor(name, list(shape), dtype))
        return st[name]

    sbuf("XF", (128, NT * C), F32)
    sbuf("BA", (128, 24576), BF16)
    sbuf("BB", (128, 24576), BF16)
    sbuf("BC", (128, 16640), BF16)
    sbuf("EW", (128, 4096), BF16)
    sbuf("SM", (128, 512), F32)
    sbuf("CONST", (128, 2048), BF16)

    ps = {}
    for i in range(4):
        ps[i] = stack.enter_context(nc.psum_tensor(f"ps{i}", [128, 1024], F32))
    st["PS"] = ps

    semnames = ["c_" + e for e in ENGINES] + DMASEMS
    sems = {s: stack.enter_context(nc.semaphore(s)) for s in semnames}

    prog = Prog(nc)
    sb = SBMap()
    kk = K(nc, prog, sb, dram, st, cst)
    kk.dbg = dbg
    kk.sub = sub

    emit_forward(kk, depth_stages)

    with nc.Block() as blk:
        prog.render(blk, sems)
    stack.close()
    return nc


def emit_forward(k, stages):
    p, t, dram = k.p, k.t, k.dram

    x_tok = p.dma("sync", [(t["XF"][:, :].rearrange("p (a c) -> p a c", c=C),
                            dram["x_in"][:, :].rearrange("(a p) c -> p a c", p=128))],
                  tag="load_x")

    xbuf = k.claim("XF", 0, NT * C, "x")
    xbuf.w.append(x_tok)

    cl = _load_consts(k)

    st = 0
    if stages > st:
        emit_l0_attn(k, xbuf, cl)
    st += 1
    if stages > st:
        xbuf = emit_l0_ffn(k, xbuf, cl, sub=k.sub) or xbuf
    st += 1
    if stages > st:
        xbuf = emit_l1_attn(k, xbuf, cl) or xbuf
    st += 1
    if stages > st:
        xbuf = emit_l1_ffn(k, xbuf, cl) or xbuf
    st += 1

    if k.dbg:
        dt1 = p.dma("sync", [(dram["dbgBA"][:, :], t["BA"][:, :]),
                             (dram["dbgBB"][:, :], t["BB"][:, :]),
                             (dram["dbgSM"][:, :], t["SM"][:, :]),
                             (dram["dbgXF"][:, :], t["XF"][:, :]),
                             (dram["dbgEW"][:, :], t["EW"][:, :]),
                             (dram["dbgBC"][:, :], t["BC"][:, :])], deps=[("c_tensor", p.cnt["tensor"]), ("c_vector", p.cnt["vector"]), ("c_scalar", p.cnt["scalar"])], tag="dbgdump")
        p.step("sync", lambda e: e.nop(), deps=[dt1], tag="dbgwait")
    if not getattr(k, "y_done", False):
        otok = p.dma("sync", [(dram["y"][:, :].rearrange("(a p) c -> p a c", p=128),
                               t["XF"][:, :].rearrange("p (a c) -> p a c", c=C))],
                     deps=xbuf.wr_deps(), tag="store_y")
        xbuf.readers.append(otok)
        p.step("sync", lambda e: e.nop(), deps=[otok], tag="final_wait")


def _load_consts(k):
    p, t, dram = k.p, k.t, k.dram
    CB = t["CONST"]
    cl = {}
    pairs = []
    cl["ident_bf16"] = CB[:, 0:128]
    pairs.append((CB[:, 0:128], dram["cst_ident_bf16"][:, :]))
    cl["mla_diag"] = CB[:, 128:256]
    pairs.append((CB[:, 128:256], dram["cst_mla_diag_mask"][:, :]))
    cl["loc_left"] = CB[:, 256:384]
    pairs.append((CB[:, 256:384], dram["cst_loc_mask_left"][:, :]))
    cl["loc_diag"] = CB[:, 384:512]
    pairs.append((CB[:, 384:512], dram["cst_loc_mask_diag"][:, :]))
    cl["lat_mask"] = CB[:, 512:576]
    pairs.append((CB[:, 512:576], dram["cst_lat_mask"][:, :]))
    tok = p.dma("sync", pairs, tag="load_consts")
    cbuf = k.claim("CONST", 0, 2048, "consts")
    cbuf.w.append(tok)
    cl["cbuf"] = cbuf
    cl["ones_col"] = CB[:, 576:577]
    cl["ident_f32"] = CB[:, 1536:1792].bitcast(F32)
    tok2 = p.dma("sync", [(cl["ident_f32"], dram["cst_ident_f32"][:, :])], tag="load_identf32")
    cbuf.w.append(tok2)
    cl["ones_row"] = CB[0:1, 1024:1536]
    def ones_body(e):
        e.memset(CB[:, 576:577], 1.0)
        return e.memset(CB[0:1, 1024:1536], 1.0)
    otok = k.p.step("vector", ones_body, tag="ones")
    cbuf.w.append(otok)
    cl["cst_tok"] = [tok, tok2, otok]
    return cl


def _ln_stats_pair(k, xbuf, ttA, smA, ttB, smB, eps=1e-5, want_one_plus=False,
                   rms=False, xphys="XF"):
    p, t = k.p, k.t
    SM = t["SM"]
    def cols(base):
        return [SM[:, base + i:base + i + 1] for i in range(8)]
    A = cols(smA); B = cols(smB)
    xtA = t[xphys][:, ttA * C:(ttA + 1) * C]
    xtB = t[xphys][:, ttB * C:(ttB + 1) * C]
    junk = k.t["PS"][3][:, 512:1024]
    smbA = k.smbuf[(smA // 8) % 8]; smbB = k.smbuf[(smB // 8) % 8]
    wdeps = smbA.wr_deps() + smbB.wr_deps() + k.psb2[3].wr_deps()
    for b in (smbA, smbB):
        b.w = []; b.readers = []; b.init = []
    xd = xbuf.rd_deps()

    def v1(e):
        e.tensor_reduce(A[0], xtA, axis=AX.X, op=ALU.add)
        return e.tensor_reduce(B[0], xtB, axis=AX.X, op=ALU.add)
    t1 = p.step("vector", v1, deps=xd + wdeps, tag=f"lnV1_{ttA}")

    def s1(e):
        e.activation(junk, xtA[:, 0:512], ACT.Square, accum_out=A[1])
        e.activation(junk, xtA[:, 512:1024], ACT.Square, accum_out=A[4])
        e.activation(junk, xtB[:, 0:512], ACT.Square, accum_out=B[1])
        return e.activation(junk, xtB[:, 512:1024], ACT.Square, accum_out=B[4])
    t2 = p.step("scalar", s1, deps=xd + wdeps, tag=f"lnS1_{ttA}")
    k.psb2[3].w.append(t2)
    xbuf.readers += [t1, t2]

    def v2(e):
        e.tensor_tensor(A[4], A[1], A[4], ALU.add)
        e.tensor_tensor(B[4], B[1], B[4], ALU.add)
        if not rms:
            e.tensor_scalar(A[2], A[0], 1.0 / C, None, ALU.mult)
            e.tensor_scalar(B[2], B[0], 1.0 / C, None, ALU.mult)
            e.tensor_scalar(A[3], A[0], -1.0, None, ALU.mult)
            e.tensor_scalar(B[3], B[0], -1.0, None, ALU.mult)
            e.scalar_tensor_tensor(A[4], A[2], A[3], A[4], ALU.mult, ALU.add)
            e.scalar_tensor_tensor(B[4], B[2], B[3], B[4], ALU.mult, ALU.add)
        e.tensor_scalar(A[1], A[4], 1.0 / C, eps, ALU.mult, ALU.add)
        return e.tensor_scalar(B[1], B[4], 1.0 / C, eps, ALU.mult, ALU.add)
    t3 = p.step("vector", v2, deps=[t1, t2], tag=f"lnV2_{ttA}")

    def s2(e):
        e.activation(A[0], A[1], ACT.Sqrt)
        return e.activation(B[0], B[1], ACT.Sqrt)
    t4 = p.step("scalar", s2, deps=[t3], tag=f"lnS2_{ttA}")

    def v3(e):
        e.reciprocal(A[5], A[0])
        e.reciprocal(B[5], B[0])
        e.tensor_tensor(A[4], A[5], A[5], ALU.mult)
        e.tensor_tensor(B[4], B[5], B[5], ALU.mult)
        e.tensor_tensor(A[4], A[4], A[1], ALU.mult)
        e.tensor_tensor(B[4], B[4], B[1], ALU.mult)
        e.tensor_scalar(A[4], A[4], -0.5, 1.5, ALU.mult, ALU.add)
        e.tensor_scalar(B[4], B[4], -0.5, 1.5, ALU.mult, ALU.add)
        e.tensor_tensor(A[5], A[5], A[4], ALU.mult)
        last = e.tensor_tensor(B[5], B[5], B[4], ALU.mult)
        if not rms:
            e.tensor_scalar(A[6], A[2], -1.0, None, ALU.mult)
            e.tensor_scalar(B[6], B[2], -1.0, None, ALU.mult)
            e.tensor_tensor(A[6], A[6], A[5], ALU.mult)
            last = e.tensor_tensor(B[6], B[6], B[5], ALU.mult)
        if want_one_plus:
            e.tensor_scalar(A[7], A[5], 1.0, None, ALU.add)
            last = e.tensor_scalar(B[7], B[5], 1.0, None, ALU.add)
        return last
    t5 = p.step("vector", v3, deps=[t4], tag=f"lnV3_{ttA}")
    for b, toks in ((smbA, [t1, t2, t3, t4, t5]), (smbB, [t1, t2, t3, t4, t5])):
        b.w = list(toks)
    return (A[5], A[6], A[7], smbA), (B[5], B[6], B[7], smbB), [t5]


def emit_l0_attn(k, xbuf, cl):
    p, t, dram = k.p, k.t, k.dram
    BA, BB, BC, EW, CB = t["BA"], t["BB"], t["BC"], t["EW"], t["CONST"]
    PS = t["PS"]
    psb = k.psb

    w1buf = k.claim("BB", 0, 8192, "l0_w1")
    w2buf = k.claim("BB", 8192, 16384, "l0_w2")
    biasbuf = k.claim("BB", 18944, 19968, "l0_bias")
    etbuf = k.claim("EW", 2048, 2560, "l0_etT")
    etT = EW[:, 0:4096].bitcast(F32)[:, 1024:1280]
    tok = p.dma("gpsimd", [
        (BB[:, 0:8192].rearrange("p (ct o) -> p ct o", o=C),
         dram["L0_W1"][:, :].rearrange("(ct p) o -> p ct o", p=128)),
        (BB[:, 8192:16384].rearrange("p (ct o) -> p ct o", o=C),
         dram["L0_W2"][:, :].rearrange("(ct p) o -> p ct o", p=128)),
        (BB[0:1, 18944:19968], dram["L0_bias"][:, :]),
    ], deps=w1buf.init + w2buf.init + biasbuf.init, tag="l0_wload")
    w1buf.w.append(tok); w2buf.w.append(tok); biasbuf.w.append(tok)
    tok2 = p.dma("sync", [
        (etT.rearrange("p (ct m) -> p ct m", m=32),
         dram["L0_etT"][:, :].rearrange("(ct p) m -> p ct m", p=128)),
    ], deps=etbuf.init, tag="l0_etload")
    etbuf.w.append(tok2)

    xnT = k.claim("BA", 0, 16384, "l0_xnT")
    xnbuf = [k.claim("EW", 0, 1024, "l0_xn_e"), k.claim("EW", 1024, 2048, "l0_xn_o")]
    stats = {}
    for tt in range(0, NT, 2):
        a, b, stok = _ln_stats_pair(k, xbuf, tt, 384 + tt * 8, tt + 1, 384 + (tt + 1) * 8)
        stats[tt] = (a, stok); stats[tt + 1] = (b, stok)
    for tt in range(NT):
        (rstd, negmur, _opr, smb), stok = stats[tt]
        xn_cols = (tt % 2) * 1024
        xnt_ap = EW[:, xn_cols:xn_cols + 1024]
        eb = xnbuf[tt % 2]
        xt = t["XF"][:, tt * C:(tt + 1) * C]
        tok = p.step("scalar",
                     lambda e, _o=xnt_ap, _x=xt, _r=rstd, _b=negmur:
                     e.activation(_o, _x, ACT.Identity, scale=_r, bias=_b),
                     deps=stok + eb.wr_deps() + xbuf.rd_deps(), tag=f"l0xn{tt}")
        smb.readers.append(tok)
        eb.w = [tok]; eb.readers = []
        xbuf.readers.append(tok)
        bank = psb[tt % 4]
        pb16 = PS[tt % 4][:, 0:512].bitcast(BF16)

        def tbody(e, _src=xnt_ap, _pb=pb16):
            last = None
            for ct2 in range(NC):
                last = e.transpose(_pb[:, ct2 * 128:(ct2 + 1) * 128],
                                  _src[:, ct2 * 128:(ct2 + 1) * 128], cl["ident_bf16"])
            return last
        ttok = p.step("tensor", tbody,
                      deps=[tok] + bank.wr_deps() + cl["cst_tok"], tag=f"l0tr{tt}")
        eb.readers.append(ttok)
        bank.w = [ttok]; bank.readers = []

        def ebody(e, _pb=pb16, _tt=tt):
            dst = BA[:, 0:16384].rearrange("p (ct t) -> p ct t", t=T)[:, :, _tt * 128:(_tt + 1) * 128]
            src = _pb.rearrange("p (ct t) -> p ct t", t=128)
            return e.tensor_copy(dst, src)
        etok = p.step("vector", ebody, deps=[ttok] + (xnT.init if tt == 0 else []), tag=f"l0te{tt}")
        bank.readers.append(etok)
        xnT.w.append(etok)

    hbuf = k.claim("BC", 0, 16384, "l0_hT")
    diagbuf = k.claim("BB", 16384, 18944, "l0_diag")
    for ct in range(NC):
        def dbody(e, _ct=ct):
            last = None
            for m in range(1, 21):
                last = e.tensor_scalar(BB[:, 16384 + (m - 1) * 128:16384 + m * 128],
                                       cl["ident_bf16"], etT[:, _ct * 32 + m:_ct * 32 + m + 1],
                                       None, ALU.mult)
            return last
        dtok = p.step("vector", dbody,
                      deps=diagbuf.wr_deps() + etbuf.rd_deps() + cl["cst_tok"], tag=f"l0dg{ct}")
        diagbuf.w = [dtok]; diagbuf.readers = []
        for j in range(4):
            bank = psb[j % 4]
            pbank = PS[j % 4][:, 0:512]

            def cbody(e, _ct=ct, _j=j, _pbank=pbank):
                last = None
                for i, m in enumerate(range(1, 21)):
                    lo = _j * 512 - m
                    if lo < 0:
                        rhs = BA[:, _ct * T + 0:_ct * T + 512 - m]
                        out = _pbank[:, m:512]
                    else:
                        rhs = BA[:, _ct * T + lo:_ct * T + lo + 512]
                        out = _pbank[:, 0:512]
                    last = e.matmul(out, BB[:, 16384 + (m - 1) * 128:16384 + m * 128],
                                    rhs, start=(i == 0), stop=(i == 19))
                return last
            mtok = p.step("tensor", cbody,
                          deps=[dtok] + xnT.rd_deps() + bank.wr_deps(), tag=f"l0cv{ct}_{j}")
            diagbuf.readers.append(mtok)
            xnT.readers.append(mtok)
            bank.w = [mtok]; bank.readers = []

            if j == 0:
                ztok = p.step("vector", lambda e, _pb=pbank: e.memset(_pb[:, 0:1], 0.0),
                              deps=[mtok], tag=f"l0cz{ct}")
                bank.w.append(ztok)

            def evbody(e, _ct=ct, _j=j, _pb=pbank):
                return e.copy(BC[:, _ct * T + _j * 512:_ct * T + (_j + 1) * 512], _pb)
            etok2 = p.step("scalar", evbody,
                           deps=bank.w + (hbuf.init if (ct == 0 and j == 0) else []),
                           tag=f"l0ce{ct}_{j}")
            bank.readers.append(etok2)
            hbuf.w.append(etok2)

    for tt in range(NT):
        for oc in range(2):
            idx = tt * 2 + oc
            bank = psb[idx % 4]
            pbank = PS[idx % 4][:, oc * 512:oc * 512 + 512] if False else PS[idx % 4][:, 0:512]

            def mbody(e, _tt=tt, _oc=oc, _pb=pbank):
                last = None
                n = 0
                for ct in range(NC):
                    last = e.matmul(_pb, BA[:, ct * T + _tt * 128:ct * T + _tt * 128 + 128],
                                    BB[:, ct * C + _oc * 512:ct * C + _oc * 512 + 512],
                                    start=(n == 0), stop=False)
                    n += 1
                for ct in range(NC):
                    last = e.matmul(_pb, BC[:, ct * T + _tt * 128:ct * T + _tt * 128 + 128],
                                    BB[:, 8192 + ct * C + _oc * 512:8192 + ct * C + _oc * 512 + 512],
                                    start=False, stop=False)
                    n += 1
                last = e.matmul(_pb, cl["ones_row"][:, 0:128],
                                BB[0:1, 18944 + _oc * 512:18944 + _oc * 512 + 512],
                                start=False, stop=True)
                return last
            mtok = p.step("tensor", mbody,
                          deps=xnT.rd_deps() + hbuf.rd_deps() + w1buf.rd_deps() +
                               w2buf.rd_deps() + biasbuf.rd_deps() + bank.wr_deps() + cl["cst_tok"],
                          tag=f"l0o{tt}_{oc}")
            for b in (xnT, hbuf, w1buf, w2buf, biasbuf):
                b.readers.append(mtok)
            bank.w = [mtok]; bank.readers = []

            def abody(e, _tt=tt, _oc=oc, _pb=pbank):
                xs = t["XF"][:, _tt * C + _oc * 512:_tt * C + _oc * 512 + 512]
                return e.tensor_tensor(xs, _pb, xs, ALU.add)
            atok = p.step("vector", abody, deps=[mtok] + xbuf.wr_deps(), tag=f"l0a{tt}_{oc}")
            bank.readers.append(atok)
            xbuf.w.append(atok)


def _rsqrt_pair(k, p, ina, inb, outa, outb, scra, scrb, deps, tag):
    s1 = p.step("scalar", lambda e: (e.activation(scra, ina, ACT.Sqrt),
                                     e.activation(scrb, inb, ACT.Sqrt))[-1],
                deps=deps, tag=tag + "_s")

    def v(e):
        e.reciprocal(outa, scra)
        e.reciprocal(outb, scrb)
        e.tensor_tensor(scra, outa, outa, ALU.mult)
        e.tensor_tensor(scrb, outb, outb, ALU.mult)
        e.tensor_tensor(scra, scra, ina, ALU.mult)
        e.tensor_tensor(scrb, scrb, inb, ALU.mult)
        e.tensor_scalar(scra, scra, -0.5, 1.5, ALU.mult, ALU.add)
        e.tensor_scalar(scrb, scrb, -0.5, 1.5, ALU.mult, ALU.add)
        e.tensor_tensor(outa, outa, scra, ALU.mult)
        return e.tensor_tensor(outb, outb, scrb, ALU.mult)
    return p.step("vector", v, deps=[s1], tag=tag + "_v")


def emit_l0_ffn(k, xbuf, cl, sub=99):
    p, t, dram = k.p, k.t, k.dram
    BA, BB, BC, EW, CB, SM = t["BA"], t["BB"], t["BC"], t["EW"], t["CONST"], t["SM"]
    PS = t["PS"]
    psb = k.psb
    XFb = t["XF"][:, :].bitcast(BF16)

    wqbuf = k.claim("BC", 0, 8192, "wq")
    wdkvbuf = k.claim("BC", 8192, 12288, "wdkv")
    tok = p.dma("gpsimd", [
        (BC[:, 0:8192].rearrange("p (ct o) -> p ct o", o=C),
         dram["Wq"][:, :].rearrange("(ct p) o -> p ct o", p=128)),
        (BC[:, 8192:12288].rearrange("p (ct o) -> p ct o", o=512),
         dram["Wdkv"][:, :].rearrange("(ct p) o -> p ct o", p=128)),
    ], deps=wqbuf.init + wdkvbuf.init, tag="l0f_wq")
    wqbuf.w.append(tok); wdkvbuf.w.append(tok)
    g1buf = k.claim("SM", 96, 112, "g1g2")
    tok = p.dma("sync", [
        (SM[:, 96:104], dram["g1"][0, :].rearrange("(ct p) -> p ct", p=128)),
        (SM[:, 104:112], dram["g2"][0, :].rearrange("(ct p) -> p ct", p=128)),
    ], deps=g1buf.init, tag="l0f_g12", slow=True)
    g1buf.w.append(tok)

    def gsc(e):
        last = None
        for ct in range(NC):
            g = SM[:, 96 + ct:97 + ct]
            last = e.tensor_scalar(BC[:, ct * C:(ct + 1) * C], BC[:, ct * C:(ct + 1) * C], g, None, ALU.mult)
            last = e.tensor_scalar(BC[:, 8192 + ct * 512:8192 + (ct + 1) * 512],
                                   BC[:, 8192 + ct * 512:8192 + (ct + 1) * 512], g, None, ALU.mult)
        return last
    gtok = p.step("vector", gsc, deps=wqbuf.rd_deps() + wdkvbuf.rd_deps() + g1buf.rd_deps(), tag="l0f_gsc")
    wqbuf.w.append(gtok); wdkvbuf.w.append(gtok)
    g1buf.readers.append(gtok)

    hT = k.claim("BA", 0, 16384, "l0f_hT")
    unbuf = [k.claim("EW", 0, 1024, "un_e"), k.claim("EW", 1024, 2048, "un_o")]
    stok_all = []
    st_info = {}
    for tt in range(0, NT, 2):
        bA, bB2 = 128 + tt * 16, 128 + (tt + 1) * 16
        a, b, stok = _ln_stats_pair(k, xbuf, tt, bA, tt + 1, bB2)
        A = lambda i, _b=bA: SM[:, _b + i:_b + i + 1]
        B = lambda i, _b=bB2: SM[:, _b + i:_b + i + 1]

        def v4(e, A=A, B=B):
            e.reciprocal(A(8), A(1))
            e.reciprocal(B(8), B(1))
            e.tensor_scalar(A(8), A(8), -1e-5, 1.0 + 1e-6, ALU.mult, ALU.add)
            return e.tensor_scalar(B(8), B(8), -1e-5, 1.0 + 1e-6, ALU.mult, ALU.add)
        t6 = p.step("vector", v4, deps=stok, tag=f"l0f_v4_{tt}")
        t7 = _rsqrt_pair(k, p, A(8), B(8), A(9), B(9), A(15), B(15), [t6], f"l0f_rr_{tt}")

        def v5(e, A=A, B=B):
            e.tensor_tensor(A(10), A(5), A(9), ALU.mult)
            e.tensor_tensor(B(10), B(5), B(9), ALU.mult)
            e.tensor_tensor(A(11), A(6), A(9), ALU.mult)
            return e.tensor_tensor(B(11), B(6), B(9), ALU.mult)
        t8 = p.step("vector", v5, deps=[t7], tag=f"l0f_v5_{tt}")
        stok_all.append(t8)
        st_info[tt] = (A, [t8]); st_info[tt + 1] = (B, [t8])

    for tt in range(NT):
        A, stok = st_info[tt]
        xt = t["XF"][:, tt * C:(tt + 1) * C]
        eb = unbuf[tt % 2]
        un_ap = EW[:, (tt % 2) * 1024:(tt % 2) * 1024 + 1024]
        tok = p.step("scalar",
                     lambda e, _o=un_ap, _x=xt, _r=A(10), _b=A(11):
                     e.activation(_o, _x, ACT.Identity, scale=_r, bias=_b),
                     deps=stok + eb.wr_deps() + xbuf.rd_deps(), tag=f"l0f_un{tt}")
        eb.w = [tok]; eb.readers = []
        xbuf.readers.append(tok)
        bank = psb[tt % 4]
        pb16 = PS[tt % 4][:, 0:512].bitcast(BF16)

        def tbody(e, _src=un_ap, _pb=pb16):
            last = None
            for ct2 in range(NC):
                last = e.transpose(_pb[:, ct2 * 128:(ct2 + 1) * 128],
                                   _src[:, ct2 * 128:(ct2 + 1) * 128], cl["ident_bf16"])
            return last
        ttok = p.step("tensor", tbody, deps=[tok] + bank.wr_deps() + cl["cst_tok"], tag=f"l0f_tr{tt}")
        eb.readers.append(ttok)
        bank.w = [ttok]; bank.readers = []

        def ebody(e, _pb=pb16, _tt=tt):
            dst = BA[:, 0:16384].rearrange("p (ct t) -> p ct t", t=T)[:, :, _tt * 128:(_tt + 1) * 128]
            return e.tensor_copy(dst, _pb.rearrange("p (ct t) -> p ct t", t=128))
        etok = p.step("vector", ebody, deps=[ttok] + (hT.init if tt == 0 else []), tag=f"l0f_te{tt}")
        bank.readers.append(etok)
        hT.w.append(etok)

    if sub < 1:
        return xbuf
    sp_tok = p.dma("sync", [(dram["xspill"][:, :].rearrange("(a p) c -> p a c", p=128),
                             t["XF"][:, :].rearrange("p (a c) -> p a c", c=C))],
                   deps=xbuf.wr_deps(), tag="l0f_spill")
    xbuf.readers.append(sp_tok)

    qtbuf = k.claim("XF", 0, 8192, "l0f_QT")
    ktbuf = k.claim("XF", 8192, 16384, "l0f_KT")
    QT = lambda c0, c1: XFb[:, c0:c1]
    for dqt in range(NC):
        for ch in range(4):
            idx = dqt * 4 + ch
            bank = psb[idx % 4]
            pbank = PS[idx % 4][:, 0:512]

            def mb(e, _d=dqt, _c=ch, _pb=pbank):
                last = None
                for ct in range(NC):
                    last = e.matmul(_pb, BC[:, ct * C + _d * 128:ct * C + _d * 128 + 128],
                                    BA[:, ct * T + _c * 512:ct * T + (_c + 1) * 512],
                                    start=(ct == 0), stop=(ct == NC - 1))
                return last
            mtok = p.step("tensor", mb, deps=wqbuf.rd_deps() + hT.rd_deps() + bank.wr_deps(),
                          tag=f"l0f_qt{dqt}_{ch}")
            wqbuf.readers.append(mtok); hT.readers.append(mtok)
            bank.w = [mtok]; bank.readers = []

            def ev(e, _d=dqt, _c=ch, _pb=pbank):
                return e.activation(XFb[:, _d * 2048 + _c * 512:_d * 2048 + (_c + 1) * 512],
                                    _pb, ACT.Identity, scale=0.125)
            etok = p.step("scalar", ev, deps=[mtok] + (qtbuf.init if idx == 0 else []), tag=f"l0f_qte{dqt}_{ch}")
            bank.readers.append(etok)
            qtbuf.w.append(etok)

    ctbuf = k.claim("BA", 16384, 24576, "l0f_cT")
    for lt in range(4):
        for ch in range(4):
            idx = lt * 4 + ch
            bank = psb[idx % 4]
            pbank = PS[idx % 4][:, 0:512]

            def mb(e, _l=lt, _c=ch, _pb=pbank):
                last = None
                for ct in range(NC):
                    last = e.matmul(_pb, BC[:, 8192 + ct * 512 + _l * 128:8192 + ct * 512 + _l * 128 + 128],
                                    BA[:, ct * T + _c * 512:ct * T + (_c + 1) * 512],
                                    start=(ct == 0), stop=(ct == NC - 1))
                return last
            mtok = p.step("tensor", mb, deps=wdkvbuf.rd_deps() + hT.rd_deps() + bank.wr_deps(),
                          tag=f"l0f_ct{lt}_{ch}")
            wdkvbuf.readers.append(mtok); hT.readers.append(mtok)
            bank.w = [mtok]; bank.readers = []

            def ev(e, _l=lt, _c=ch, _pb=pbank):
                return e.copy(BA[:, 16384 + _l * 2048 + _c * 512:16384 + _l * 2048 + (_c + 1) * 512], _pb)
            etok = p.step("scalar", ev, deps=[mtok] + (ctbuf.init if idx == 0 else []), tag=f"l0f_cte{lt}_{ch}")
            bank.readers.append(etok)
            ctbuf.w.append(etok)

    wukbuf = k.claim("BC", 12288, 16384, "wuk")
    tok = p.dma("gpsimd", [
        (BC[:, 12288:16384].rearrange("p (lt o) -> p lt o", o=C),
         dram["Wuk"][:, :].rearrange("(lt p) o -> p lt o", p=128)),
    ], deps=wukbuf.init, tag="l0f_wuk")
    wukbuf.w.append(tok)
    wuvbuf = k.claim("BC", 0, 4096, "wuv")
    tok = p.dma("gpsimd", [
        (BC[:, 0:4096].rearrange("p (lt o) -> p lt o", o=C),
         dram["Wuv"][:, :].rearrange("(lt p) o -> p lt o", p=128)),
    ], deps=wuvbuf.init, tag="l0f_wuv")
    wuvbuf.w.append(tok)

    for dkt in range(NC):
        for ch in range(4):
            idx = dkt * 4 + ch
            bank = psb[idx % 4]
            pbank = PS[idx % 4][:, 0:512]

            def mb(e, _d=dkt, _c=ch, _pb=pbank):
                last = None
                for lt in range(4):
                    last = e.matmul(_pb, BC[:, 12288 + lt * C + _d * 128:12288 + lt * C + _d * 128 + 128],
                                    BA[:, 16384 + lt * T + _c * 512:16384 + lt * T + (_c + 1) * 512],
                                    start=(lt == 0), stop=(lt == 3))
                return last
            mtok = p.step("tensor", mb, deps=wukbuf.rd_deps() + ctbuf.rd_deps() + bank.wr_deps(),
                          tag=f"l0f_kt{dkt}_{ch}")
            wukbuf.readers.append(mtok); ctbuf.readers.append(mtok)
            bank.w = [mtok]; bank.readers = []

            def ev(e, _d=dkt, _c=ch, _pb=pbank):
                return e.copy(XFb[:, 16384 + _d * 2048 + _c * 512:16384 + _d * 2048 + (_c + 1) * 512], _pb)
            etok = p.step("scalar", ev, deps=[mtok] + (ktbuf.init if idx == 0 else []), tag=f"l0f_kte{dkt}_{ch}")
            bank.readers.append(etok)
            ktbuf.w.append(etok)

    vbuf = k.claim("BA", 0, 16384, "l0f_V")
    VW = 1024
    for tt in range(NT):
        for dc in range(2):
            idx = tt * 2 + dc
            bank = psb[idx % 4]
            pbank = PS[idx % 4][:, 0:512]

            def mb(e, _tt=tt, _dc=dc, _pb=pbank):
                last = None
                for lt in range(4):
                    last = e.matmul(_pb, BA[:, 16384 + lt * T + _tt * 128:16384 + lt * T + _tt * 128 + 128],
                                    BC[:, lt * C + _dc * 512:lt * C + (_dc + 1) * 512],
                                    start=(lt == 0), stop=(lt == 3))
                return last
            mtok = p.step("tensor", mb, deps=wuvbuf.rd_deps() + ctbuf.rd_deps() + bank.wr_deps(),
                          tag=f"l0f_v{tt}_{dc}")
            wuvbuf.readers.append(mtok); ctbuf.readers.append(mtok)
            bank.w = [mtok]; bank.readers = []

            def ev(e, _tt=tt, _dc=dc, _pb=pbank):
                return e.copy(BA[:, _tt * VW + _dc * 512:_tt * VW + (_dc + 1) * 512], _pb)
            etok = p.step("scalar", ev, deps=[mtok] + (vbuf.init if idx == 0 else []), tag=f"l0f_ve{tt}_{dc}")
            bank.readers.append(etok)
            vbuf.w.append(etok)

    if sub < 3:
        return xbuf
    obuf = k.claim("BB", 0, 16384, "l0f_o")
    ptbufs = [k.claim("BA", 16384, 24576, "l0f_PTa"), k.claim("BB", 16384, 24576, "l0f_PTb")]
    PTT = [BA, BB]
    dcolbuf = k.claim("SM", 112, 128, "l0f_dcol")
    mla_items = [(h, qc) for h in range(16) for qc in range(4)]

    def mla_stage1(h, qc, par):
        ptbuf = ptbufs[par]
        PTX = PTT[par]
        pb = (h % 2) * 64
        kth = 16384 + (h // 2) * 2048
        qth = (h // 2) * 2048
        nkt = qc * 4 + 4
        pt_toks = []
        for kt in range(nkt):
            off = max(0, (kt - qc * 4)) * 128
            bank = psb[kt % 2]
            pbank = PS[kt % 2][:, 0:512]

            def mb(e, _kt=kt, _pb=pbank, _pbase=pb, _kth=kth, _qth=qth, _qc=qc, _o=off):
                return e.matmul(_pb[:, _o:512], XFb[_pbase:_pbase + 64, _kth + _kt * 128:_kth + (_kt + 1) * 128],
                                XFb[_pbase:_pbase + 64, _qth + _qc * 512 + _o:_qth + (_qc + 1) * 512],
                                start=True, stop=True)
            mtok = p.step("tensor", mb, deps=ktbuf.rd_deps() + qtbuf.rd_deps() + bank.wr_deps(),
                          tag=f"st{h}_{qc}_{kt}")
            ktbuf.readers.append(mtok); qtbuf.readers.append(mtok)
            bank.w = [mtok]; bank.readers = []

            def ev(e, _kt=kt, _pb=pbank, _o=off, _P=PTX):
                return e.activation(_P[:, 16384 + _kt * 512 + _o:16384 + (_kt + 1) * 512], _pb[:, _o:512], ACT.Exp)
            etok = p.step("scalar", ev, deps=[mtok] + (ptbuf.wr_deps() if kt == 0 else []),
                          tag=f"ste{h}_{qc}_{kt}")
            bank.readers.append(etok)
            if kt == 0:
                ptbuf.w = [etok]; ptbuf.readers = []
            else:
                ptbuf.w.append(etok)
            pt_toks.append(etok)
        def dmask(e, _qc=qc, _P=PTX):
            last = None
            for j in range(4):
                kt = _qc * 4 + j
                ap = _P[:, 16384 + kt * 512 + j * 128:16384 + kt * 512 + (j + 1) * 128]
                last = e.tensor_tensor(ap, ap, cl["mla_diag"], ALU.mult)
            return last
        mtok2 = p.step("vector", dmask, deps=pt_toks[qc * 4:] + cl["cst_tok"], tag=f"msk{h}_{qc}")
        ptbuf.w.append(mtok2)
        return (h, qc, par)

    def mla_stage2(state):
        h, qc, par = state
        ptbuf = ptbufs[par]
        PTX = PTT[par]
        pv_toks = []
        for j in range(4):
            jg = qc * 4 + j
            bank = psb[2 + (j % 2)]
            pbank = PS[2 + (j % 2)][:, 0:128]

            def pv(e, _j=j, _jg=jg, _h=h, _pb=pbank, _P=PTX):
                last = None
                for kt2 in range(_jg + 1):
                    pt_ap = _P[:, 16384 + kt2 * 512 + _j * 128:16384 + kt2 * 512 + (_j + 1) * 128]
                    e.matmul(_pb[:, 0:64], pt_ap,
                             BA[:, kt2 * VW + _h * 64:kt2 * VW + (_h + 1) * 64],
                             start=(kt2 == 0), stop=(kt2 == _jg))
                    last = e.matmul(_pb[:, 64:65], pt_ap, cl["ones_col"],
                                    start=False, stop=(kt2 == _jg))
                return last
            pvtok = p.step("tensor", pv, deps=ptbuf.rd_deps() + vbuf.rd_deps() + bank.wr_deps() + cl["cst_tok"],
                           tag=f"pv{h}_{jg}")
            ptbuf.readers.append(pvtok); vbuf.readers.append(pvtok)
            pv_toks.append(pvtok)
            bank.w = [pvtok]; bank.readers = []

            dcol = SM[:, 112 + (jg % 2) * 2:113 + (jg % 2) * 2]
            d1tok = p.step("vector", lambda e, _pb=pbank, _d=dcol: e.reciprocal(_d, _pb[:, 64:65]),
                           deps=[pvtok], tag=f"dr{h}_{jg}")
            def dv(e, _pb=pbank, _jg=jg, _h=h, _d=dcol):
                return e.tensor_scalar(BB[:, _jg * C + _h * 64:_jg * C + (_h + 1) * 64],
                                       _pb[:, 0:64], _d, None, ALU.mult)
            dtok = p.step("vector", dv, deps=[d1tok] + (obuf.init if (h == 0 and jg == 0) else []),
                          tag=f"dv{h}_{jg}")
            bank.readers.append(dtok)
            obuf.w.append(dtok)
        return pv_toks

    mpend = None
    for i2, (h, qc) in enumerate(mla_items):
        s1 = mla_stage1(h, qc, i2 % 2)
        if mpend is not None:
            mla_stage2(mpend)
        mpend = s1
    mla_stage2(mpend)

    wobuf = k.claim("BC", 4096, 12288, "wo")
    tok = p.dma("gpsimd", [
        (BC[:, 4096:12288].rearrange("p (ot o) -> p ot o", o=C),
         dram["Wo"][:, :].rearrange("(ot p) o -> p ot o", p=128)),
    ], deps=wobuf.init, tag="l0f_wo")
    wobuf.w.append(tok)
    s1buf = k.claim("BA", 0, 16384, "l0f_s1")
    xqs = []
    pbank2 = lambda i, cc: PS[i][:, cc * 512:(cc + 1) * 512]
    for tq in range(4):
        xq = k.claim("XF", tq * 4096, (tq + 1) * 4096, f"x_q{tq}")
        xqs.append(xq)
        rtok = p.dma("sync", [(t["XF"][:, tq * 4096:(tq + 1) * 4096].rearrange("p (a c) -> p a c", c=C),
                              dram["xspill"][:, :].rearrange("(a p) c -> p a c", p=128)[:, tq * 4:(tq + 1) * 4, :])],
                     deps=xq.init + [sp_tok], tag=f"l0f_xr{tq}")
        xq.w.append(rtok)
        otq = k.claim("EW", 0, 4096, f"l0f_oT{tq}")
        for g in range(4):
            bankt = psb[g % 2]
            ptb = PS[g % 2][:, 0:512].bitcast(BF16)

            def trb(e, _tq=tq, _g=g, _pb=ptb):
                last = None
                for z in range(8):
                    ot = _g * 2 + z // 4
                    i = z % 4
                    last = e.transpose(_pb[:, z * 128:(z + 1) * 128],
                                       BB[:, (_tq * 4 + i) * C + ot * 128:(_tq * 4 + i) * C + (ot + 1) * 128],
                                       cl["ident_bf16"])
                return last
            ttok = p.step("tensor", trb, deps=obuf.rd_deps() + bankt.wr_deps() + cl["cst_tok"],
                          tag=f"woT{tq}_{g}")
            obuf.readers.append(ttok)
            bankt.w = [ttok]; bankt.readers = []

            def tev(e, _g=g, _pb=ptb):
                return e.tensor_copy(EW[:, _g * 1024:(_g + 1) * 1024], _pb)
            etok = p.step("vector", tev, deps=[ttok] + (otq.wr_deps() if g < 2 else []), tag=f"woTe{tq}_{g}")
            bankt.readers.append(etok)
            otq.w.append(etok)
        bdeps = []
        for i in range(4):
            for cc in range(2):
                bdeps += (psb[i].wr_deps() if cc == 0 else k.psb2[i].wr_deps())

        def mmb(e, _tq=tq):
            last = None
            for ot in range(NC):
                for i in range(4):
                    for cc in range(2):
                        last = e.matmul(pbank2(i, cc),
                                        EW[:, ot * 512 + i * 128:ot * 512 + (i + 1) * 128],
                                        BC[:, 4096 + ot * C + cc * 512:4096 + ot * C + (cc + 1) * 512],
                                        start=(ot == 0), stop=(ot == NC - 1))
            return last
        mmtok = p.step("tensor", mmb, deps=otq.rd_deps() + wobuf.rd_deps() + bdeps, tag=f"woMM{tq}")
        otq.readers.append(mmtok); wobuf.readers.append(mmtok)
        for i in range(4):
            psb[i].w = [mmtok]; psb[i].readers = []
            k.psb2[i].w = [mmtok]; k.psb2[i].readers = []

        EWF = EW[:, 0:4096].bitcast(F32)
        if tq == 0:
            k.sfslot = [Buf("sf0"), Buf("sf1")]
        for i in range(4):
            tt = tq * 4 + i
            A, _ = st_info[tt]
            slot = k.sfslot[i % 2]
            sf = [EWF[:, (i % 2) * 1024:(i % 2) * 1024 + 512],
                  EWF[:, (i % 2) * 1024 + 512:(i % 2) * 1024 + 1024]]

            def ve(e, _i=i, _tt=tt, _A=A, _sf=sf):
                xs = [t["XF"][:, _tt * C:_tt * C + 512], t["XF"][:, _tt * C + 512:(_tt + 1) * C]]
                e.tensor_scalar(_sf[0], xs[0], _A(5), _A(6), ALU.mult, ALU.add)
                e.tensor_scalar(_sf[1], xs[1], _A(5), _A(6), ALU.mult, ALU.add)
                e.tensor_tensor(_sf[0], _sf[0], pbank2(_i, 0), ALU.add)
                e.tensor_tensor(_sf[1], _sf[1], pbank2(_i, 1), ALU.add)
                e.tensor_tensor(xs[0], xs[0], _sf[0], ALU.add)
                return e.tensor_tensor(xs[1], xs[1], _sf[1], ALU.add)
            vdeps = [mmtok] + xq.wr_deps() + otq.rd_deps() + st_info[tt][1] + slot.wr_deps()
            vtok = p.step("vector", ve, deps=vdeps, tag=f"l0f_s1v{tt}")
            xq.w.append(vtok)
            otq.readers.append(vtok)
            slot.w = [vtok]; slot.readers = []; slot.init = []
            psb[i].readers.append(vtok); k.psb2[i].readers.append(vtok)

            def se(e, _tt=tt, _A=A, _sf=sf):
                e.copy(BA[:, _tt * C:_tt * C + 512], _sf[0])
                e.copy(BA[:, _tt * C + 512:(_tt + 1) * C], _sf[1])
                e.activation(_sf[0], _sf[0], ACT.Square, accum_out=_A(12))
                return e.activation(_sf[1], _sf[1], ACT.Square, accum_out=_A(13))
            sdeps = [vtok] + (s1buf.init if tt == 0 else [])
            stok2 = p.step("scalar", se, deps=sdeps, tag=f"l0f_s1s{tt}")
            s1buf.w.append(stok2)
            slot.readers.append(stok2)
            otq.readers.append(stok2)

    nxq = k.claim("XF", 0, NT * C, "x_mid")
    nxq.w = [tok2 for q in xqs for tok2 in q.w]
    nxq.readers = [tok2 for q in xqs for tok2 in q.readers]
    if sub < 5:
        return nxq
    h2T = k.claim("BB", 0, 16384, "l0f_h2T")
    for tt in range(0, NT, 2):
        A, _ = st_info[tt]; B2, _ = st_info[tt + 1]

        def vr(e, _A=A, _B=B2):
            e.tensor_tensor(_A(15), _A(12), _A(13), ALU.add)
            e.tensor_tensor(_B(15), _B(12), _B(13), ALU.add)
            e.tensor_scalar(_A(15), _A(15), 1.0 / C, 1e-6, ALU.mult, ALU.add)
            return e.tensor_scalar(_B(15), _B(15), 1.0 / C, 1e-6, ALU.mult, ALU.add)
        vtok = p.step("vector", vr, deps=s1buf.rd_deps() + s1buf.w, tag=f"l0f_rr2a{tt}")
        rtok = _rsqrt_pair(k, p, A(15), B2(15), A(14), B2(14), A(12), B2(12), [vtok], f"l0f_rr2_{tt}")
        st_info[tt] = (A, [rtok]); st_info[tt + 1] = (B2, [rtok])

    h2buf = [k.claim("EW", 0, 1024, "h2_e"), k.claim("EW", 1024, 2048, "h2_o")]
    for tt in range(NT):
        A, stok = st_info[tt]
        eb = h2buf[tt % 2]
        h2_ap = EW[:, (tt % 2) * 1024:(tt % 2) * 1024 + 1024]
        tok = p.step("scalar",
                     lambda e, _o=h2_ap, _tt=tt, _r=A(14):
                     e.activation(_o, BA[:, _tt * C:(_tt + 1) * C], ACT.Identity, scale=_r),
                     deps=stok + eb.wr_deps() + s1buf.rd_deps(), tag=f"l0f_h2{tt}")
        eb.w = [tok]; eb.readers = []
        s1buf.readers.append(tok)
        bank = psb[tt % 4]
        pb16 = PS[tt % 4][:, 0:512].bitcast(BF16)

        def tbody(e, _src=h2_ap, _pb=pb16):
            last = None
            for ct2 in range(NC):
                last = e.transpose(_pb[:, ct2 * 128:(ct2 + 1) * 128],
                                   _src[:, ct2 * 128:(ct2 + 1) * 128], cl["ident_bf16"])
            return last
        ttok = p.step("tensor", tbody, deps=[tok] + bank.wr_deps() + cl["cst_tok"], tag=f"l0f_h2t{tt}")
        eb.readers.append(ttok)
        bank.w = [ttok]; bank.readers = []

        def ebody(e, _pb=pb16, _tt=tt):
            last = None
            for ct2 in range(NC):
                last = e.tensor_scalar(BB[:, ct2 * T + _tt * 128:ct2 * T + (_tt + 1) * 128],
                                       _pb[:, ct2 * 128:(ct2 + 1) * 128],
                                       SM[:, 104 + ct2:105 + ct2], None, ALU.mult)
            return last
        etok = p.step("vector", ebody, deps=[ttok] + g1buf.rd_deps() + (h2T.init if tt == 0 else []),
                      tag=f"l0f_h2e{tt}")
        bank.readers.append(etok)
        g1buf.readers.append(etok)
        h2T.w.append(etok)

    if sub < 6:
        return nxq
    w2slab = k.claim("BA", 0, 16384, "moe_w2")
    wstage = [k.claim("BA", 16384 + i * 1024, 16384 + (i + 1) * 1024, f"w1s{i}") for i in range(3)]
    gwbuf = k.claim("BA", 19456, 19520, "gatew")
    tok = p.dma("gpsimd", [
        (BA[:, 19456:19520].rearrange("p (ct o) -> p ct o", o=8),
         dram["gate_w"][:, :].rearrange("(ct p) o -> p ct o", p=128)),
    ], deps=gwbuf.init, tag="l0f_gw")
    gwbuf.w.append(tok)

    for tt in range(0, NT, 2):
        ltoks = []
        for z in range(2):
            tt2 = tt + z
            bank = psb[z]
            pbank = PS[z][:, 0:8]

            def gmb(e, _tt=tt2, _pb=pbank):
                last = None
                for ct in range(NC):
                    last = e.matmul(_pb, BB[:, ct * T + _tt * 128:ct * T + (_tt + 1) * 128],
                                    BA[:, 19456 + ct * 8:19456 + (ct + 1) * 8],
                                    start=(ct == 0), stop=(ct == NC - 1))
                return last
            mtok = p.step("tensor", gmb, deps=h2T.rd_deps() + gwbuf.rd_deps() + bank.wr_deps(),
                          tag=f"gt{tt2}")
            h2T.readers.append(mtok); gwbuf.readers.append(mtok)
            bank.w = [mtok]; bank.readers = []
            ltoks.append(mtok)
        par_g = (tt // 2) % 2
        eqA, eqB = SM[:, 64:72], SM[:, 72:80]
        l2A, l2B = SM[:, 80:88], SM[:, 88:96]
        CMB = lambda tt3: SM[:, 384 + tt3 * 8:392 + tt3 * 8]
        stA = lambda i, _b=112 + par_g * 8: SM[:, _b + i:_b + i + 1]
        stB = lambda i, _b=116 + par_g * 8: SM[:, _b + i:_b + i + 1]
        pA, pB = PS[0][:, 0:8], PS[1][:, 0:8]

        def cb1(e, _tt=tt):
            e.tensor_reduce(stA(0), pA, axis=AX.X, op=ALU.max)
            e.tensor_reduce(stB(0), pB, axis=AX.X, op=ALU.max)
            e.tensor_scalar(eqA, pA, stA(0), None, ALU.is_ge)
            e.tensor_scalar(eqB, pB, stB(0), None, ALU.is_ge)
            e.scalar_tensor_tensor(l2A, eqA, -1e30, pA, ALU.mult, ALU.add)
            e.scalar_tensor_tensor(l2B, eqB, -1e30, pB, ALU.mult, ALU.add)
            e.tensor_reduce(stA(1), l2A, axis=AX.X, op=ALU.max)
            e.tensor_reduce(stB(1), l2B, axis=AX.X, op=ALU.max)
            e.tensor_scalar(l2A, l2A, stA(1), None, ALU.is_ge)
            e.tensor_scalar(l2B, l2B, stB(1), None, ALU.is_ge)
            e.tensor_tensor(stA(2), stA(1), stA(0), ALU.subtract)
            return e.tensor_tensor(stB(2), stB(1), stB(0), ALU.subtract)
        c1 = p.step("vector", cb1, deps=ltoks + getattr(k, "_prev_cmb", []), tag=f"cmb1_{tt}")

        def cb2(e):
            e.activation(stA(2), stA(2), ACT.Exp)
            return e.activation(stB(2), stB(2), ACT.Exp)
        c2 = p.step("scalar", cb2, deps=[c1], tag=f"cmb2_{tt}")

        def cb3(e, _tt=tt):
            e.tensor_scalar(stA(3), stA(2), 1.0, None, ALU.add)
            e.tensor_scalar(stB(3), stB(2), 1.0, None, ALU.add)
            e.reciprocal(stA(3), stA(3))
            e.reciprocal(stB(3), stB(3))
            e.tensor_tensor(stA(2), stA(2), stA(3), ALU.mult)
            e.tensor_tensor(stB(2), stB(2), stB(3), ALU.mult)
            e.tensor_scalar(eqA, eqA, stA(3), None, ALU.mult)
            e.tensor_scalar(eqB, eqB, stB(3), None, ALU.mult)
            e.scalar_tensor_tensor(CMB(_tt), l2A, stA(2), eqA, ALU.mult, ALU.add)
            return e.scalar_tensor_tensor(CMB(_tt + 1), l2B, stB(2), eqB, ALU.mult, ALU.add)
        c3 = p.step("vector", cb3, deps=[c2], tag=f"cmb3_{tt}")
        psb[0].readers.append(c3); psb[1].readers.append(c3)
        k._prev_cmb = [c2, c3]

    if sub < 7:
        return nxq
    hidT = k.claim("BC", 0, 16384, "moe_hidT")
    w1slabA = k.claim("BA", 16384, 24576, "moe_w1a")
    w1slabB = k.claim("BB", 16384, 24576, "moe_w1b")

    def w1ap(ct, jt):
        if ct < 4:
            return BA[:, 16384 + ct * 2048 + jt * 128:16384 + ct * 2048 + (jt + 1) * 128]
        return BB[:, 16384 + (ct - 4) * 2048 + jt * 128:16384 + (ct - 4) * 2048 + (jt + 1) * 128]

    for e_i in range(8):
        w1tok = p.dma("gpsimd", [
            (BA[:, 16384:24576].rearrange("p (ct j) -> p ct j", j=2048),
             dram["moe_w1"][e_i].rearrange("(ct p) j -> p ct j", p=128)[:, 0:4, :]),
            (BB[:, 16384:24576].rearrange("p (ct j) -> p ct j", j=2048),
             dram["moe_w1"][e_i].rearrange("(ct p) j -> p ct j", p=128)[:, 4:8, :]),
        ], deps=w1slabA.wr_deps() + w1slabB.wr_deps(), tag=f"moe_w1_{e_i}")
        for b in (w1slabA, w1slabB):
            b.w = [w1tok]; b.readers = []; b.init = []
        w2tok = p.dma("gpsimd", [
            (BA[:, 0:16384].rearrange("p (jt2 c) -> p jt2 c", c=C),
             dram["moe_w2"][e_i].rearrange("(jt2 p) c -> p jt2 c", p=128)),
        ], deps=w2slab.wr_deps(), tag=f"moe_w2_{e_i}")
        w2slab.w = [w2tok]; w2slab.readers = []; w2slab.init = []
        for half in range(2):
            for jt in range(16):
                for ch in range(2):
                    bank = psb[ch]
                    pbank = PS[ch][:, 0:512]

                    def mb(e, _jt=jt, _h=half, _c=ch, _pb=pbank):
                        last = None
                        for ct in range(NC):
                            last = e.matmul(_pb, w1ap(ct, _jt),
                                            BB[:, ct * T + _h * 1024 + _c * 512:ct * T + _h * 1024 + (_c + 1) * 512],
                                            start=(ct == 0), stop=(ct == NC - 1))
                        return last
                    mtok = p.step("tensor", mb, deps=[w1tok] + h2T.rd_deps() + bank.wr_deps(),
                                  tag=f"hid{e_i}_{half}_{jt}_{ch}")
                    w1slabA.readers.append(mtok); w1slabB.readers.append(mtok)
                    h2T.readers.append(mtok)
                    bank.w = [mtok]; bank.readers = []

                    def ev(e, _jt=jt, _c=ch, _pb=pbank):
                        return e.activation(BC[:, _jt * 1024 + _c * 512:_jt * 1024 + (_c + 1) * 512],
                                            _pb, ACT.Silu)
                    etok = p.step("scalar", ev, deps=[mtok] + (hidT.wr_deps() if (jt == 0 and ch == 0) else []),
                                  tag=f"sil{e_i}_{half}_{jt}_{ch}")
                    bank.readers.append(etok)
                    if jt == 0 and ch == 0:
                        hidT.w = [etok]; hidT.readers = []; hidT.init = []
                    else:
                        hidT.w.append(etok)
            for qg in range(2):
                bdeps = []
                for i in range(4):
                    bdeps += psb[i].wr_deps() + k.psb2[i].wr_deps()

                def mb2(e, _h=half, _q=qg):
                    last = None
                    for jt in range(16):
                        for i in range(4):
                            for cc in range(2):
                                last = e.matmul(pbank2(i, cc),
                                                BC[:, jt * 1024 + _q * 512 + i * 128:jt * 1024 + _q * 512 + (i + 1) * 128],
                                                BA[:, jt * C + cc * 512:jt * C + (cc + 1) * 512],
                                                start=(jt == 0), stop=(jt == 15))
                    return last
                mtok = p.step("tensor", mb2, deps=hidT.rd_deps() + w2slab.rd_deps() + bdeps,
                              tag=f"ye{e_i}_{half}_{qg}")
                hidT.readers.append(mtok); w2slab.readers.append(mtok)
                for i in range(4):
                    psb[i].w = [mtok]; psb[i].readers = []
                    k.psb2[i].w = [mtok]; k.psb2[i].readers = []
                for i in range(4):
                    tt = half * 8 + qg * 4 + i
                    xq = xqs[tt // 4]

                    def ve(e, _i=i, _tt=tt, _e=e_i):
                        cmb = SM[:, 384 + _tt * 8 + _e:385 + _tt * 8 + _e]
                        xs0 = t["XF"][:, _tt * C:_tt * C + 512]
                        xs1 = t["XF"][:, _tt * C + 512:(_tt + 1) * C]
                        e.scalar_tensor_tensor(xs0, pbank2(_i, 0), cmb, xs0, ALU.mult, ALU.add)
                        return e.scalar_tensor_tensor(xs1, pbank2(_i, 1), cmb, xs1, ALU.mult, ALU.add)
                    vtok = p.step("vector", ve, deps=[mtok] + xq.wr_deps(), tag=f"moex{e_i}_{tt}")
                    xq.w.append(vtok)
                    psb[i].readers.append(vtok); k.psb2[i].readers.append(vtok)

    nx = k.claim("XF", 0, NT * C, "x_after_l0f")
    nx.w = [tok2 for q in xqs for tok2 in q.w]
    nx.readers = [tok2 for q in xqs for tok2 in q.readers]
    return nx


def emit_l1_attn(k, xbuf, cl):
    p, t, dram = k.p, k.t, k.dram
    BA, BB, BC, EW, CB, SM = t["BA"], t["BB"], t["BC"], t["EW"], t["CONST"], t["SM"]
    PS = t["PS"]
    psb = k.psb
    BAF = BA[:, :].bitcast(F32)

    wslab = k.claim("BB", 16384, 24576, "l1a_wk")
    tok = p.dma("gpsimd", [
        (BB[:, 16384:24576].rearrange("p (ct o) -> p ct o", o=C),
         dram["WkT"][:, :].rearrange("(ct p) o -> p ct o", p=128)),
    ], deps=wslab.init, tag="l1a_wk")
    wslab.w.append(tok)
    tabbuf = k.claim("EW", 2048, 4096, "l1a_tabs")
    tok = p.dma("gpsimd", [
        (EW[:, 2048:3072].rearrange("p (a d) -> p a d", d=64),
         dram["cst_l1a_cos"][:, :].rearrange("(a p) d -> p a d", p=128)),
        (EW[:, 3072:4096].rearrange("p (a d) -> p a d", d=64),
         dram["cst_l1a_sin"][:, :].rearrange("(a p) d -> p a d", p=128)),
    ], deps=tabbuf.init, tag="l1a_tabs")
    tabbuf.w.append(tok)
    lqbuf = k.claim("BB", 24576, 24576, "lq_dummy")
    lqb = k.claim("BA", 21504, 22528, "l1a_latq")
    tok = p.dma("gpsimd", [
        (BA[0:64, 21504:22528].rearrange("p (h d) -> p h d", d=64),
         dram["lat_qT"][:, :, :].rearrange("h d l -> d h l")),
        (BA[64:128, 21504:22528].rearrange("p (h d) -> p h d", d=64),
         dram["lat_qT"][:, :, :].rearrange("h d l -> d h l")),
    ], deps=lqb.init, tag="l1a_latq")
    lqb.w.append(tok)

    xnT = k.claim("BA", 0, 16384, "l1a_xnT")
    xnb = [k.claim("EW", 0, 1024, "l1a_xn_e"), k.claim("EW", 1024, 2048, "l1a_xn_o")]
    st_info = {}
    for tt in range(0, NT, 2):
        a, b, stok = _ln_stats_pair(k, xbuf, tt, 128 + tt * 16, tt + 1, 128 + (tt + 1) * 16)
        st_info[tt] = (a, stok); st_info[tt + 1] = (b, stok)
    for tt in range(NT):
        (rstd, negmur, _o, smb), stok = st_info[tt]
        xt = t["XF"][:, tt * C:(tt + 1) * C]
        eb = xnb[tt % 2]
        xn_ap = EW[:, (tt % 2) * 1024:(tt % 2) * 1024 + 1024]
        tok = p.step("scalar",
                     lambda e, _o2=xn_ap, _x=xt, _r=rstd, _b=negmur:
                     e.activation(_o2, _x, ACT.Identity, scale=_r, bias=_b),
                     deps=stok + eb.wr_deps() + xbuf.rd_deps(), tag=f"l1a_xn{tt}")
        smb.readers.append(tok)
        eb.w = [tok]; eb.readers = []
        xbuf.readers.append(tok)
        bank = psb[tt % 4]
        pb16 = PS[tt % 4][:, 0:512].bitcast(BF16)

        def tb(e, _src=xn_ap, _pb=pb16):
            last = None
            for ct in range(NC):
                last = e.transpose(_pb[:, ct * 128:(ct + 1) * 128],
                                   _src[:, ct * 128:(ct + 1) * 128], cl["ident_bf16"])
            return last
        ttok = p.step("tensor", tb, deps=[tok] + bank.wr_deps() + cl["cst_tok"], tag=f"l1a_tr{tt}")
        eb.readers.append(ttok)
        bank.w = [ttok]; bank.readers = []

        def ebd(e, _pb=pb16, _tt=tt):
            dst = BA[:, 0:16384].rearrange("p (ct t) -> p ct t", t=T)[:, :, _tt * 128:(_tt + 1) * 128]
            return e.tensor_copy(dst, _pb.rearrange("p (ct t) -> p ct t", t=128))
        etok = p.step("vector", ebd, deps=[ttok] + (xnT.init if tt == 0 else []), tag=f"l1a_te{tt}")
        bank.readers.append(etok)
        xnT.w.append(etok)

    ktbuf = k.claim("BB", 0, 16384, "l1a_KT")
    SEV = BAF[:, 8192:8704]; SOD = BAF[:, 8704:9216]
    REV = BAF[:, 9216:9728]; ROD = BAF[:, 9728:10240]
    KP = BA[:, 20480:21504]
    scrb = k.claim("BA", 16384, 21504, "l1a_kscr")
    def k_stage1(tt):
        bank = psb[tt % 2]
        pbank = PS[tt % 2][:, 0:1024]

        def mb(e, _tt=tt, _pb=pbank):
            last = None
            for cc in range(2):
                for ct in range(NC):
                    last = e.matmul(_pb[:, cc * 512:(cc + 1) * 512],
                                    BA[:, ct * T + _tt * 128:ct * T + (_tt + 1) * 128],
                                    BB[:, 16384 + ct * C + cc * 512:16384 + ct * C + (cc + 1) * 512],
                                    start=(ct == 0), stop=(ct == NC - 1))
            return last
        mtok = p.step("tensor", mb, deps=xnT.rd_deps() + wslab.rd_deps() + bank.wr_deps() +
                      (k.psb2[tt % 2].wr_deps()), tag=f"l1a_k{tt}")
        xnT.readers.append(mtok); wslab.readers.append(mtok)
        bank.w = [mtok]; bank.readers = []
        k.psb2[tt % 2].w = [mtok]; k.psb2[tt % 2].readers = []
        return (tt, mtok, bank, pbank)

    def k_stage2(state):
        tt, mtok, bank, pbank = state
        cte = EW[:, 2048 + tt * 64:2048 + (tt + 1) * 64][:, 0::2].unsqueeze(1).broadcast_to((128, 16, 32))
        ste = EW[:, 3072 + tt * 64:3072 + (tt + 1) * 64][:, 0::2].unsqueeze(1).broadcast_to((128, 16, 32))
        kev = pbank.rearrange("p (h d2 two) -> p h d2 two", d2=32, two=2)[:, :, :, 0]
        kod = pbank.rearrange("p (h d2 two) -> p h d2 two", d2=32, two=2)[:, :, :, 1]
        sev3 = SEV.rearrange("p (h d2) -> p h d2", d2=32)
        sod3 = SOD.rearrange("p (h d2) -> p h d2", d2=32)
        rev3 = REV.rearrange("p (h d2) -> p h d2", d2=32)
        rod3 = ROD.rearrange("p (h d2) -> p h d2", d2=32)
        kpe = KP.rearrange("p (h d2 two) -> p h d2 two", d2=32, two=2)[:, :, :, 0]
        kpo = KP.rearrange("p (h d2 two) -> p h d2 two", d2=32, two=2)[:, :, :, 1]

        def rp(e, _kev=kev, _kod=kod, _cte=cte, _ste=ste,
               _sev=sev3, _sod=sod3, _rev=rev3, _rod=rod3, _kpe=kpe, _kpo=kpo):
            e.tensor_tensor(_sev, _kev, _cte, ALU.mult)
            e.tensor_tensor(_sod, _kod, _cte, ALU.mult)
            e.tensor_tensor(_rev, _kod, _ste, ALU.mult)
            e.tensor_tensor(_rod, _kev, _ste, ALU.mult)
            e.tensor_tensor(_kpe, _sev, _rev, ALU.subtract)
            return e.tensor_tensor(_kpo, _sod, _rod, ALU.add)
        rtok = p.step("vector", rp, deps=[mtok] + tabbuf.rd_deps() + scrb.wr_deps(), tag=f"l1a_rp{tt}")
        bank.readers.append(rtok); k.psb2[tt % 2].readers.append(rtok)
        tabbuf.readers.append(rtok)
        scrb.w = [rtok]; scrb.readers = []; scrb.init = []

        bank2 = psb[2 + tt % 2]
        pb16 = PS[2 + tt % 2][:, 0:512].bitcast(BF16)

        def tb2(e, _pb=pb16):
            last = None
            for ct in range(NC):
                last = e.transpose(_pb[:, ct * 128:(ct + 1) * 128],
                                   KP[:, ct * 128:(ct + 1) * 128], cl["ident_bf16"])
            return last
        ttok = p.step("tensor", tb2, deps=[rtok] + bank2.wr_deps() + cl["cst_tok"], tag=f"l1a_ktr{tt}")
        scrb.readers.append(ttok)
        bank2.w = [ttok]; bank2.readers = []

        def ebd2(e, _pb=pb16, _tt=tt):
            dst = BB[:, 0:16384].rearrange("p (ct t) -> p ct t", t=T)[:, :, _tt * 128:(_tt + 1) * 128]
            return e.tensor_copy(dst, _pb.rearrange("p (ct t) -> p ct t", t=128))
        etok = p.step("vector", ebd2, deps=[ttok] + (ktbuf.init if tt == 0 else []), tag=f"l1a_kte{tt}")
        bank2.readers.append(etok)
        ktbuf.w.append(etok)

    pend = None
    for tt in range(NT):
        st1 = k_stage1(tt)
        if pend is not None:
            k_stage2(pend)
        pend = st1
    k_stage2(pend)

    wvslab = k.claim("BB", 16384, 24576, "l1a_wv")
    tok = p.dma("gpsimd", [
        (BB[:, 16384:24576].rearrange("p (ct o) -> p ct o", o=C),
         dram["WvT"][:, :].rearrange("(ct p) o -> p ct o", p=128)),
    ], deps=wvslab.init, tag="l1a_wv")
    wvslab.w.append(tok)
    vbuf = k.claim("BC", 0, 16384, "l1a_V")
    for tt in range(NT):
        for dc in range(2):
            idx = tt * 2 + dc
            bank = psb[idx % 4]
            pbank = PS[idx % 4][:, 0:512]

            def mb(e, _tt=tt, _dc=dc, _pb=pbank):
                last = None
                for ct in range(NC):
                    last = e.matmul(_pb, BA[:, ct * T + _tt * 128:ct * T + (_tt + 1) * 128],
                                    BB[:, 16384 + ct * C + _dc * 512:16384 + ct * C + (_dc + 1) * 512],
                                    start=(ct == 0), stop=(ct == NC - 1))
                return last
            mtok = p.step("tensor", mb, deps=xnT.rd_deps() + wvslab.rd_deps() + bank.wr_deps(),
                          tag=f"l1a_v{tt}_{dc}")
            xnT.readers.append(mtok); wvslab.readers.append(mtok)
            bank.w = [mtok]; bank.readers = []

            def ev(e, _tt=tt, _dc=dc, _pb=pbank):
                return e.copy(BC[:, _tt * 1024 + _dc * 512:_tt * 1024 + (_dc + 1) * 512], _pb)
            etok = p.step("scalar", ev, deps=[mtok] + (vbuf.init if idx == 0 else []), tag=f"l1a_ve{tt}_{dc}")
            bank.readers.append(etok)
            vbuf.w.append(etok)

    ptb = k.claim("EW", 0, 1024, "l1a_PT")
    attnb = k.claim("EW", 1024, 2048, "l1a_attn")
    accA, accB = SM[:, 384:448], SM[:, 448:512]
    dsc = SM[:, 112:128]
    for h in range(16):
        pbase = (h % 2) * 64
        kth = (h // 2) * 2048
        st_toks = []
        for g in range(4):
            bank = psb[g % 2]
            pbank = PS[g % 2][:, 0:256]

            def mb(e, _h=h, _g=g, _pb=pbank, _pbase=pbase, _kth=kth):
                last = None
                for n4 in range(4):
                    n = _g * 4 + n4
                    last = e.matmul(_pb[:, n4 * 64:(n4 + 1) * 64],
                                    BB[_pbase:_pbase + 64, _kth + n * 128:_kth + (n + 1) * 128],
                                    BA[_pbase:_pbase + 64, 21504 + _h * 64:21504 + (_h + 1) * 64],
                                    start=True, stop=True)
                return last
            mtok = p.step("tensor", mb, deps=ktbuf.rd_deps() + lqb.rd_deps() + bank.wr_deps(),
                          tag=f"l1a_st{h}_{g}")
            ktbuf.readers.append(mtok); lqb.readers.append(mtok)
            bank.w = [mtok]; bank.readers = []

            def ev(e, _g=g, _pb=pbank):
                return e.activation(EW[:, _g * 256:(_g + 1) * 256], _pb, ACT.Exp, scale=0.125)
            etok = p.step("scalar", ev, deps=[mtok] + (ptb.wr_deps() if g == 0 else []), tag=f"l1a_se{h}_{g}")
            bank.readers.append(etok)
            if g == 0:
                ptb.w = [etok]; ptb.readers = []; ptb.init = []
            else:
                ptb.w.append(etok)
            st_toks.append(etok)
        def mk(e):
            ap = EW[:, 0:1024].rearrange("p (n l) -> p n l", l=64)
            return e.tensor_tensor(ap, ap, cl["lat_mask"].unsqueeze(1).broadcast_to((128, 16, 64)), ALU.mult)
        mtok2 = p.step("vector", mk, deps=st_toks + cl["cst_tok"], tag=f"l1a_mk{h}")
        ptb.w.append(mtok2)
        pvbank = lambda g: PS[2 + g // 2][:, (g % 2) * 512:(g % 2) * 512 + 260]
        pvbuf = lambda g: (psb[2 + g // 2] if g % 2 == 0 else k.psb2[2 + g // 2])
        pv_toks = []
        for g in range(4):
            bank = pvbuf(g)
            pbank = pvbank(g)

            def pv(e, _h=h, _g=g, _pb=pbank):
                last = None
                for n4 in range(4):
                    n = _g * 4 + n4
                    e.matmul(_pb[0:64, n4 * 65:n4 * 65 + 64],
                             EW[:, n * 64:(n + 1) * 64],
                             BC[:, n * 1024 + _h * 64:n * 1024 + (_h + 1) * 64],
                             start=(n4 == 0), stop=False)
                    last = e.matmul(_pb[0:64, n4 * 65 + 64:n4 * 65 + 65],
                                    EW[:, n * 64:(n + 1) * 64], cl["ones_col"],
                                    start=False, stop=(n4 == 3))
                return last
            pvtok = p.step("tensor", pv, deps=ptb.rd_deps() + vbuf.rd_deps() + bank.wr_deps() + cl["cst_tok"],
                           tag=f"l1a_pv{h}_{g}")
            ptb.readers.append(pvtok); vbuf.readers.append(pvtok)
            bank.w = [pvtok]; bank.readers = []
            pv_toks.append(pvtok)
        def rc(e, _h=h):
            last = None
            for g in range(4):
                pb2 = pvbank(g)
                last = e.reciprocal(dsc[0:64, g * 4:(g + 1) * 4], pb2[0:64, 64:260:65])
            return last
        rtok = p.step("vector", rc, deps=pv_toks, tag=f"l1a_rc{h}")
        def ac(e, _h=h):
            e.memset(accA[0:64, :], 0.0)
            e.memset(accB[0:64, :], 0.0)
            for n in range(16):
                g, n4 = n // 4, n % 4
                pb2 = pvbank(g)
                acc = accA if n % 2 == 0 else accB
                e.scalar_tensor_tensor(acc[0:64, :], pb2[0:64, n4 * 65:n4 * 65 + 64],
                                       dsc[0:64, n:n + 1], acc[0:64, :], ALU.mult, ALU.add)
            e.memset(gsums[0:64, 0:1], 0.0)
            return e.tensor_tensor(EW[0:64, 1024 + _h * 64:1024 + (_h + 1) * 64],
                                   accA[0:64, :], accB[0:64, :], ALU.add)
        atok = p.step("vector", ac, deps=[rtok] + (attnb.wr_deps() if h == 0 else []), tag=f"l1a_ac{h}")
        for g in range(4):
            pvbuf(g).readers.append(atok)
        if h == 0:
            attnb.w = [atok]; attnb.readers = []; attnb.init = []
        else:
            attnb.w.append(atok)

    duptok = p.dma("sync", [(EW[64:128, 1024:2048], EW[0:64, 1024:2048])],
                   deps=attnb.wr_deps(), tag="l1a_attndup")
    attnb.w.append(duptok)
    attnb_ap = lambda _h: EW[(_h % 2) * 64:(_h % 2) * 64 + 64, 1024 + _h * 64:1024 + (_h + 1) * 64]

    wgslab = k.claim("BB", 16384, 24576, "l1a_wg")
    tok = p.dma("gpsimd", [
        (BB[:, 16384:24576].rearrange("p (ct o) -> p ct o", o=C),
         dram["WgT"][:, :].rearrange("(ct p) o -> p ct o", p=128)),
    ], deps=wgslab.init, tag="l1a_wg")
    wgslab.w.append(tok)
    wpslab = k.claim("BC", 0, 16384, "l1a_wp")
    tok = p.dma("gpsimd", [
        (BC[0:64, 0:16384].rearrange("p (ob c) -> p ob c", c=C),
         dram["WpT"][:, :].rearrange("(ob p) c -> p ob c", p=64)),
    ], deps=wpslab.init, tag="l1a_wp")
    wpslab.w.append(tok)
    gtbuf = k.claim("BB", 0, 16384, "l1a_GT")
    gscr = k.claim("EW", 0, 1024, "l1a_g")
    gsums = SM[:, 64:96]
    def g_stage1(tt):
        bank = psb[tt % 2]
        pbank = PS[tt % 2][:, 0:1024]

        def mb(e, _tt=tt, _pb=pbank):
            last = None
            for cc in range(2):
                for ct in range(NC):
                    last = e.matmul(_pb[:, cc * 512:(cc + 1) * 512],
                                    BA[:, ct * T + _tt * 128:ct * T + (_tt + 1) * 128],
                                    BB[:, 16384 + ct * C + cc * 512:16384 + ct * C + (cc + 1) * 512],
                                    start=(ct == 0), stop=(ct == NC - 1))
            return last
        mtok = p.step("tensor", mb, deps=xnT.rd_deps() + wgslab.rd_deps() + bank.wr_deps() + k.psb2[tt % 2].wr_deps(),
                      tag=f"l1a_g{tt}")
        xnT.readers.append(mtok); wgslab.readers.append(mtok)
        bank.w = [mtok]; bank.readers = []
        k.psb2[tt % 2].w = [mtok]; k.psb2[tt % 2].readers = []
        return (tt, mtok, bank, pbank)

    def g_stage2(state):
        tt, mtok, bank, pbank = state
        etok = p.step("scalar", lambda e, _pb=pbank: e.activation(EW[:, 0:1024], _pb, ACT.Exp, scale=0.125),
                      deps=[mtok] + gscr.wr_deps(), tag=f"l1a_ge{tt}")
        bank.readers.append(etok); k.psb2[tt % 2].readers.append(etok)
        gscr.w = [etok]; gscr.readers = []; gscr.init = []

        s1t = p.step("vector", lambda e: e.tensor_reduce(gsums[:, 0:16], EW[:, 0:1024].rearrange("p (h l) -> p h l", l=64), axis=AX.X, op=ALU.add),
                     deps=[etok], tag=f"l1a_gs{tt}")
        s2t = p.step("vector", lambda e: e.reciprocal(gsums[:, 16:32], gsums[:, 0:16]),
                     deps=[s1t], tag=f"l1a_gr{tt}")

        def nm(e):
            last = None
            for h in range(16):
                last = e.tensor_scalar(EW[:, h * 64:(h + 1) * 64], EW[:, h * 64:(h + 1) * 64],
                                       gsums[:, 16 + h:17 + h], None, ALU.mult)
            return last
        ntok = p.step("vector", nm, deps=[s2t], tag=f"l1a_gn{tt}")
        gscr.w.append(ntok)

        bank2 = psb[2 + tt % 2]
        pb16 = PS[2 + tt % 2][:, 0:512].bitcast(BF16)

        def tb3(e, _pb=pb16):
            last = None
            for ct in range(NC):
                last = e.transpose(_pb[:, ct * 128:(ct + 1) * 128],
                                   EW[:, ct * 128:(ct + 1) * 128], cl["ident_bf16"])
            return last
        ttok = p.step("tensor", tb3, deps=[ntok] + bank2.wr_deps() + cl["cst_tok"], tag=f"l1a_gtr{tt}")
        gscr.readers.append(ttok)
        bank2.w = [ttok]; bank2.readers = []

        def ebd3(e, _pb=pb16, _tt=tt):
            dst = BB[:, 0:16384].rearrange("p (ct t) -> p ct t", t=T)[:, :, _tt * 128:(_tt + 1) * 128]
            return e.tensor_copy(dst, _pb.rearrange("p (ct t) -> p ct t", t=128))
        etok2 = p.step("vector", ebd3, deps=[ttok] + (gtbuf.init if tt == 0 else []), tag=f"l1a_gte{tt}")
        bank2.readers.append(etok2)
        gtbuf.w.append(etok2)

    pend2 = None
    for tt in range(NT):
        st1 = g_stage1(tt)
        if pend2 is not None:
            g_stage2(pend2)
        pend2 = st1
    g_stage2(pend2)

    wscrs = [k.claim("EW", 0, 512, "l1a_wscr0"), k.claim("EW", 512, 1024, "l1a_wscr1")]
    for grp in range(8):
        tta, ttb = grp * 2, grp * 2 + 1
        bdeps = []
        for i in range(2):
            bdeps += psb[i].wr_deps() + k.psb2[i].wr_deps()
        acc_toks = []
        for h in range(16):
            pbase = (h % 2) * 64
            wscr = wscrs[h % 2]
            bankw = psb[2 + h % 2]
            pw = PS[2 + h % 2][:, 0:256]

            def wmm(e, _h=h, _grp=grp, _pb=pw, _pbase=pbase):
                return e.matmul(_pb[0:64, 0:256], attnb_ap(_h),
                                BB[_pbase:_pbase + 64, (_h // 2) * 2048 + _grp * 256:(_h // 2) * 2048 + (_grp + 1) * 256],
                                start=True, stop=True)
            wtok = p.step("tensor", wmm, deps=attnb.rd_deps() + gtbuf.rd_deps() + bankw.wr_deps(),
                          tag=f"l1a_w{grp}_{h}")
            attnb.readers.append(wtok); gtbuf.readers.append(wtok)
            bankw.w = [wtok]; bankw.readers = []

            def wev(e, _pb=pw, _h=h):
                return e.tensor_copy(EW[0:64, (_h % 2) * 512:(_h % 2) * 512 + 256], _pb[0:64, 0:256])
            wetok = p.step("vector", wev, deps=[wtok] + wscr.wr_deps(), tag=f"l1a_we{grp}_{h}")
            bankw.readers.append(wetok)
            wscr.w = [wetok]; wscr.readers = []; wscr.init = []

            def pmm(e, _h=h, _grp=grp):
                last = None
                for i in range(2):
                    for cc in range(2):
                        last = e.matmul(PS[i][:, cc * 512:(cc + 1) * 512],
                                        EW[0:64, (_h % 2) * 512 + i * 128:(_h % 2) * 512 + (i + 1) * 128],
                                        BC[0:64, _h * 1024 + cc * 512:_h * 1024 + (cc + 1) * 512],
                                        start=(_h == 0), stop=(_h == 15))
                return last
            ptok = p.step("tensor", pmm, deps=[wetok] + wpslab.rd_deps() + (bdeps if h == 0 else []),
                          tag=f"l1a_p{grp}_{h}")
            wscr.readers.append(ptok); wpslab.readers.append(ptok)
            for i in range(2):
                if h == 0:
                    psb[i].w = [ptok]; psb[i].readers = []
                    k.psb2[i].w = [ptok]; k.psb2[i].readers = []
                else:
                    psb[i].w.append(ptok); k.psb2[i].w.append(ptok)
            acc_toks.append(ptok)
        for i in range(2):
            tt = grp * 2 + i

            def av(e, _i=i, _tt=tt):
                xs0 = t["XF"][:, _tt * C:_tt * C + 512]
                xs1 = t["XF"][:, _tt * C + 512:(_tt + 1) * C]
                e.tensor_tensor(xs0, PS[_i][:, 0:512], xs0, ALU.add)
                return e.tensor_tensor(xs1, PS[_i][:, 512:1024], xs1, ALU.add)
            avtok = p.step("vector", av, deps=[acc_toks[-1]] + xbuf.wr_deps(), tag=f"l1a_x{tt}")
            xbuf.w.append(avtok)
            psb[i].readers.append(avtok); k.psb2[i].readers.append(avtok)
    return xbuf


def emit_l1_ffn(k, xbuf, cl):
    p, t, dram = k.p, k.t, k.dram
    BA, BB, BC, EW, CB, SM = t["BA"], t["BB"], t["BC"], t["EW"], t["CONST"], t["SM"]
    PS = t["PS"]
    psb, psb2 = k.psb, k.psb2
    XFf = t["XF"]
    EWF = EW[:, :].bitcast(F32)
    BBF = BB[:, :].bitcast(F32)
    BCF = BC[:, :].bitcast(F32)

    def bankap(i, cc):
        return PS[i][:, cc * 512:(cc + 1) * 512]

    def bankbuf(i, cc):
        return psb[i] if cc == 0 else psb2[i]

    st_info = {}
    for tt in range(0, NT, 2):
        a, b, stok = _ln_stats_pair(k, xbuf, tt, 128 + tt * 16, tt + 1, 128 + (tt + 1) * 16)
        st_info[tt] = (a, stok); st_info[tt + 1] = (b, stok)
    sp_tok = p.dma("sync", [(dram["xspill"][:, :].rearrange("(a p) c -> p a c", p=128),
                             XFf[:, :].rearrange("p (a c) -> p a c", c=C))],
                   deps=xbuf.wr_deps() + [tok for tt in range(0, NT, 2) for tok in st_info[tt][1]],
                   tag="l1f_spill")
    xbuf.readers.append(sp_tok)

    sT = k.claim("XF", 0, NT * C, "l1f_sT")
    ewb = k.claim("EW", 0, 4096, "l1f_p0")
    for tt in range(NT):
        (rstd, negmur, _opr, smb), stok = st_info[tt]
        xr = EWF[:, 0:1024]
        uu = EWF[:, 1024:2048]
        rtok = p.dma("sync", [(xr, dram["xspill"][:, :].rearrange("(a p) c -> p a c", p=128)[:, tt, :])],
                     deps=ewb.wr_deps() + [sp_tok], tag=f"l1f_xr{tt}")
        ewb.w = [rtok]; ewb.readers = []; ewb.init = []
        utok = p.step("scalar",
                      lambda e, _r=rstd, _b=negmur: e.activation(uu, xr, ACT.Identity, scale=_r, bias=_b),
                      deps=[rtok] + stok, tag=f"l1f_u{tt}")
        smb.readers.append(utok)
        ewb.w.append(utok)
        bank = psb[tt % 2]
        pbank = PS[tt % 2][:, 0:1024]

        def tb(e, _pb=pbank):
            last = None
            for ct in range(NC):
                last = e.transpose(_pb[:, ct * 128:(ct + 1) * 128],
                                   uu[:, ct * 128:(ct + 1) * 128], cl["ident_f32"])
            return last
        ttok = p.step("tensor", tb, deps=[utok] + bank.wr_deps() + psb2[tt % 2].wr_deps() + cl["cst_tok"],
                      tag=f"l1f_tr{tt}")
        ewb.readers.append(ttok)
        bank.w = [ttok]; bank.readers = []
        psb2[tt % 2].w = [ttok]; psb2[tt % 2].readers = []

        def ebd(e, _pb=pbank, _tt=tt):
            dst = XFf[:, :].rearrange("p (ct t) -> p ct t", t=T)[:, :, _tt * 128:(_tt + 1) * 128]
            return e.tensor_copy(dst, _pb.rearrange("p (ct t) -> p ct t", t=128))
        etok = p.step("vector", ebd, deps=[ttok] + (sT.init if tt == 0 else []), tag=f"l1f_se{tt}")
        bank.readers.append(etok); psb2[tt % 2].readers.append(etok)
        sT.w.append(etok)


    tabs = k.claim("BC", 8320, 12416, "l1f_tabs")
    tok = p.dma("gpsimd", [
        (BC[:, 8320 + i * 1024:8320 + (i + 1) * 1024].rearrange("p (a d) -> p a d", d=64),
         dram[nm][:, :].rearrange("(a p) d -> p a d", p=128))
        for i, nm in enumerate(["cst_xpos_qc", "cst_xpos_qs", "cst_xpos_kc", "cst_xpos_ks"])
    ], deps=tabs.init, tag="l1f_tabs")
    tabs.w.append(tok)

    ones_f32 = SM[0:1, 384:512]
    o32tok = p.step("vector", lambda e: e.memset(ones_f32, 1.0), tag="l1f_o32")

    for l in range(2):
        qrow, krow = SM[0:1, 0:64], SM[0:1, 64:128]
        tok = p.dma("sync", [(qrow, dram[f"qs{l}"][:, :]), (krow, dram[f"ks{l}"][:, :])],
                    deps=[], tag=f"l1f_qs{l}")
        qsb, ksb = SM[:, 128:192], SM[:, 192:256]

        def qb(e, _tok=tok):
            e.matmul(PS[2][:, 0:64], ones_f32, qrow, start=True, stop=True)
            return e.matmul(PS[2][:, 512:576], ones_f32, krow, start=True, stop=True)
        qtok = p.step("tensor", qb, deps=[tok, o32tok] + psb[2].wr_deps() + psb2[2].wr_deps(),
                      tag=f"l1f_qb{l}")
        psb[2].w = [qtok]; psb[2].readers = []
        psb2[2].w = [qtok]; psb2[2].readers = []

        def qe(e):
            e.tensor_copy(qsb, PS[2][:, 0:64])
            return e.tensor_copy(ksb, PS[2][:, 512:576])
        qetok = p.step("vector", qe, deps=[qtok], tag=f"l1f_qe{l}")
        psb[2].readers.append(qetok); psb2[2].readers.append(qetok)

        etabs = k.claim("BC", 12416, 16512, f"l1f_etabs{l}")
        ETC = 12416

        def mktabs(e):
            last = None
            for side, (ci, si, sb) in enumerate([(0, 1, qsb), (2, 3, ksb)]):
                raw_c = BC[:, 8320 + ci * 1024:8320 + (ci + 1) * 1024].rearrange("p (a d) -> p a d", d=64)
                raw_s = BC[:, 8320 + si * 1024:8320 + (si + 1) * 1024].rearrange("p (a d) -> p a d", d=64)
                eff_c = BC[:, ETC + ci * 1024:ETC + (ci + 1) * 1024].rearrange("p (a d) -> p a d", d=64)
                eff_s = BC[:, ETC + si * 1024:ETC + (si + 1) * 1024].rearrange("p (a d) -> p a d", d=64)
                e.tensor_tensor(eff_c, raw_c, sb.unsqueeze(1).broadcast_to((128, 16, 64)), ALU.mult)
                e.tensor_tensor(eff_s[:, :, 0:32], raw_s[:, :, 0:32],
                                sb[:, 32:64].unsqueeze(1).broadcast_to((128, 16, 32)), ALU.mult)
                last = e.tensor_tensor(eff_s[:, :, 32:64], raw_s[:, :, 32:64],
                                       sb[:, 0:32].unsqueeze(1).broadcast_to((128, 16, 32)), ALU.mult)
            return last
        mtok = p.step("vector", mktabs, deps=[qetok] + tabs.rd_deps() + etabs.wr_deps(), tag=f"l1f_mt{l}")
        tabs.readers.append(mtok)
        etabs.w = [mtok]; etabs.readers = []; etabs.init = list(etabs.init)

        hq = k.claim("BA", 0, 16384, f"l1f_hq{l}")

        def mkhq(e):
            last = None
            for ct in range(NC):
                src = XFf[:, ct * T:(ct + 1) * T]
                dst = BA[:, ct * T:(ct + 1) * T]
                if ct < 4:
                    last = e.copy(dst, src)
                else:
                    last = e.copy(dst[:, 1:T], src[:, 0:T - 1])
            return last
        hqt = p.step("scalar", mkhq, deps=sT.rd_deps() + hq.init, tag=f"l1f_hq{l}")
        sT.readers.append(hqt)

        def hqz(e):
            last = None
            for ct in range(4, NC):
                last = e.memset(BA[:, ct * T:ct * T + 1], 0.0)
            return last
        hqzt = p.step("vector", hqz, deps=[hqt], tag=f"l1f_hqz{l}")
        hq.w = [hqt, hqzt]

        wqkv = k.claim("BB", 0, 12288, f"l1f_wqkv{l}")
        tok = p.dma("gpsimd", [
            (BB[:, 0:12288].rearrange("p (ct o) -> p ct o", o=1536),
             dram[f"wqkv{l}"][:, :].rearrange("(ct p) o -> p ct o", p=128)),
        ], deps=wqkv.init, tag=f"l1f_wqkvL{l}")
        wqkv.w.append(tok)
        vbuf = k.claim("BC", 0, 8320, f"l1f_V{l}")
        onestok = p.step("vector",
                         lambda e: e.memset(BC[:, 0:8320].rearrange("p (tt h d) -> p tt h d", h=8, d=65)[:, :, :, 64:65], 1.0),
                         deps=vbuf.init, tag=f"l1f_vones{l}")
        vbuf.w.append(onestok)
        qtb = k.claim("BB", 12288, 20480, f"l1f_QT{l}")
        ktbB = k.claim("BB", 20480, 24576, f"l1f_KTb{l}")
        ktbA = k.claim("BA", 16384, 20480, f"l1f_KTa{l}")

        def kt_ap(h, c0, c1):
            kt = h // 2
            pb = (h % 2) * 64
            if kt < 2:
                return BB[pb:pb + 64, 20480 + kt * 2048 + c0:20480 + kt * 2048 + c1]
            return BA[pb:pb + 64, 16384 + (kt - 2) * 2048 + c0:16384 + (kt - 2) * 2048 + c1]

        def qt_ap(h, c0, c1):
            pb = (h % 2) * 64
            return BB[pb:pb + 64, 12288 + (h // 2) * 2048 + c0:12288 + (h // 2) * 2048 + c1]

        ewq = k.claim("EW", 0, 4096, f"l1f_ew{l}")
        for tt in range(NT):
            par = tt % 2
            QKV0 = PS[0] if par == 0 else PS[2]
            QKV1 = PS[1] if par == 0 else PS[3]
            qbufs = (psb[0], psb2[0], psb[1]) if par == 0 else (psb[2], psb2[2], psb[3])

            def mb(e, _tt=tt, _p0=QKV0, _p1=QKV1):
                last = None
                for oc in range(3):
                    pb = _p0[:, oc * 512:(oc + 1) * 512] if oc < 2 else _p1[:, 0:512]
                    for ct in range(NC):
                        last = e.matmul(pb, BA[:, ct * T + _tt * 128:ct * T + (_tt + 1) * 128],
                                        BB[:, ct * 1536 + oc * 512:ct * 1536 + (oc + 1) * 512],
                                        start=(ct == 0), stop=(ct == NC - 1))
                return last
            bdeps = []
            for b in qbufs:
                bdeps += b.wr_deps()
            mtok2 = p.step("tensor", mb, deps=hq.rd_deps() + wqkv.rd_deps() + bdeps, tag=f"l1f_qkv{l}_{tt}")
            hq.readers.append(mtok2); wqkv.readers.append(mtok2)
            for b in qbufs:
                b.w = [mtok2]; b.readers = []

            def sev(e, _tt=tt, _p0=QKV0, _p1=QKV1):
                e.copy(EWF[:, 0:512], _p0[:, 0:512])
                e.copy(EWF[:, 512:1024], _p0[:, 512:1024])
                e.activation(EWF[:, 1024:1536], _p0[:, 0:512], ACT.Square)
                e.activation(EWF[:, 1536:2048], _p0[:, 512:1024], ACT.Square)
                dst = BC[:, _tt * 520:(_tt + 1) * 520].rearrange("p (h d) -> p h d", d=65)[:, :, 0:64]
                return e.copy(dst, _p1[:, 0:512].rearrange("p (h d) -> p h d", d=64))
            setok = p.step("scalar", sev, deps=[mtok2] + ewq.wr_deps() + vbuf.wr_deps(), tag=f"l1f_sev{l}_{tt}")
            for b in qbufs:
                b.readers.append(setok)
            ewq.w = [setok]; ewq.readers = []; ewq.init = []
            vbuf.w.append(setok)

            if tt == 0:
                nrm = k.claim("SM", 256, 304, f"l1f_nrm{l}")
                k._l1f_nrm = nrm
            nrm = k._l1f_nrm

            def vn1(e):
                e.tensor_reduce(SM[:, 256:264], EWF[:, 1024:1536].rearrange("p (h d) -> p h d", d=64),
                                axis=AX.X, op=ALU.add)
                e.tensor_reduce(SM[:, 264:272], EWF[:, 1536:2048].rearrange("p (h d) -> p h d", d=64),
                                axis=AX.X, op=ALU.add)
                e.tensor_scalar(SM[:, 256:264], SM[:, 256:264], 1e-24, None, ALU.max)
                return e.tensor_scalar(SM[:, 264:272], SM[:, 264:272], 1e-24, None, ALU.max)
            v1t = p.step("vector", vn1, deps=[setok] + nrm.wr_deps(), tag=f"l1f_n1{l}_{tt}")
            nrm.w = [v1t]; nrm.readers = []; nrm.init = []
            s1t = p.step("scalar", lambda e: (e.activation(SM[:, 272:280], SM[:, 256:264], ACT.Sqrt),
                                              e.activation(SM[:, 280:288], SM[:, 264:272], ACT.Sqrt))[-1],
                         deps=[v1t], tag=f"l1f_n2{l}_{tt}")
            nrm.w.append(s1t)

            def vn2(e):
                e.reciprocal(SM[:, 288:296], SM[:, 272:280])
                e.reciprocal(SM[:, 296:304], SM[:, 280:288])
                e.tensor_tensor(SM[:, 272:280], SM[:, 288:296], SM[:, 288:296], ALU.mult)
                e.tensor_tensor(SM[:, 280:288], SM[:, 296:304], SM[:, 296:304], ALU.mult)
                e.tensor_tensor(SM[:, 272:280], SM[:, 272:280], SM[:, 256:264], ALU.mult)
                e.tensor_tensor(SM[:, 280:288], SM[:, 280:288], SM[:, 264:272], ALU.mult)
                e.tensor_scalar(SM[:, 272:280], SM[:, 272:280], -0.5, 1.5, ALU.mult, ALU.add)
                e.tensor_scalar(SM[:, 280:288], SM[:, 280:288], -0.5, 1.5, ALU.mult, ALU.add)
                e.tensor_tensor(SM[:, 288:296], SM[:, 288:296], SM[:, 272:280], ALU.mult)
                return e.tensor_tensor(SM[:, 296:304], SM[:, 296:304], SM[:, 280:288], ALU.mult)
            v2t = p.step("vector", vn2, deps=[s1t], tag=f"l1f_n3{l}_{tt}")
            nrm.w.append(v2t)

            def vnm(e):
                last = None
                for h in range(8):
                    last = e.tensor_scalar(EWF[:, h * 64:(h + 1) * 64], EWF[:, h * 64:(h + 1) * 64],
                                           SM[:, 288 + h:289 + h], None, ALU.mult)
                for h in range(8):
                    last = e.tensor_scalar(EWF[:, 512 + h * 64:512 + (h + 1) * 64],
                                           EWF[:, 512 + h * 64:512 + (h + 1) * 64],
                                           SM[:, 296 + h:297 + h], None, ALU.mult)
                return last
            v3t = p.step("vector", vnm, deps=[v2t], tag=f"l1f_nm{l}_{tt}")
            nrm.readers.append(v3t)
            ewq.w.append(v3t)

            def rope(e, _tt=tt):
                qn = EWF[:, 0:512].rearrange("p (h d) -> p h d", d=64)
                kn = EWF[:, 512:1024].rearrange("p (h d) -> p h d", d=64)
                qp = EW[:, 2048:2560].rearrange("p (h d) -> p h d", d=64)
                kp = EW[:, 2560:3072].rearrange("p (h d) -> p h d", d=64)
                bq = EW[:, 3072:3584].rearrange("p (h d) -> p h d", d=64)
                bk = EW[:, 3584:4096].rearrange("p (h d) -> p h d", d=64)
                tc = lambda i: BC[:, ETC + i * 1024 + _tt * 64:ETC + i * 1024 + (_tt + 1) * 64]
                qc = tc(0).unsqueeze(1).broadcast_to((128, 8, 64))
                qs_ = tc(1); kc = tc(2).unsqueeze(1).broadcast_to((128, 8, 64)); ks_ = tc(3)
                e.tensor_tensor(qp, qn, qc, ALU.mult)
                e.tensor_tensor(kp, kn, kc, ALU.mult)
                e.tensor_tensor(bq[:, :, 0:32], qn[:, :, 32:64],
                                qs_[:, 0:32].unsqueeze(1).broadcast_to((128, 8, 32)), ALU.mult)
                e.tensor_tensor(bk[:, :, 0:32], kn[:, :, 32:64],
                                ks_[:, 0:32].unsqueeze(1).broadcast_to((128, 8, 32)), ALU.mult)
                e.tensor_tensor(bq[:, :, 32:64], qn[:, :, 0:32],
                                qs_[:, 32:64].unsqueeze(1).broadcast_to((128, 8, 32)), ALU.mult)
                e.tensor_tensor(bk[:, :, 32:64], kn[:, :, 0:32],
                                ks_[:, 32:64].unsqueeze(1).broadcast_to((128, 8, 32)), ALU.mult)
                e.tensor_tensor(qp[:, :, 0:32], qp[:, :, 0:32], bq[:, :, 0:32], ALU.subtract)
                e.tensor_tensor(kp[:, :, 0:32], kp[:, :, 0:32], bk[:, :, 0:32], ALU.subtract)
                e.tensor_tensor(qp[:, :, 32:64], qp[:, :, 32:64], bq[:, :, 32:64], ALU.add)
                return e.tensor_tensor(kp[:, :, 32:64], kp[:, :, 32:64], bk[:, :, 32:64], ALU.add)
            v4t = p.step("gpsimd", rope, deps=[v3t] + etabs.rd_deps(), tag=f"l1f_rope{l}_{tt}")
            etabs.readers.append(v4t)
            ewq.w.append(v4t)

            bank2 = psb2[1] if par == 0 else psb2[3]
            pb16 = (PS[1] if par == 0 else PS[3])[:, 512:1024].bitcast(BF16)

            def tb2(e, _pb=pb16):
                last = None
                for z in range(4):
                    last = e.transpose(_pb[:, z * 128:(z + 1) * 128],
                                       EW[:, 2048 + z * 128:2048 + (z + 1) * 128], cl["ident_bf16"])
                for z in range(4):
                    last = e.transpose(_pb[:, 512 + z * 128:512 + (z + 1) * 128],
                                       EW[:, 2560 + z * 128:2560 + (z + 1) * 128], cl["ident_bf16"])
                return last
            t2t = p.step("tensor", tb2, deps=[v4t] + bank2.wr_deps() + cl["cst_tok"], tag=f"l1f_qtr{l}_{tt}")
            ewq.readers.append(t2t)
            bank2.w = [t2t]; bank2.readers = []

            def ebd2(e, _pb=pb16, _tt=tt):
                dstq = BB[:, 12288:20480].rearrange("p (qt t) -> p qt t", t=T)[:, :, _tt * 128:(_tt + 1) * 128]
                e.tensor_copy(dstq, _pb[:, 0:512].rearrange("p (qt t) -> p qt t", t=128))
                dstk1 = BB[:, 20480:24576].rearrange("p (kt t) -> p kt t", t=T)[:, :, _tt * 128:(_tt + 1) * 128]
                e.tensor_copy(dstk1, _pb[:, 512:768].rearrange("p (kt t) -> p kt t", t=128))
                dstk2 = BA[:, 16384:20480].rearrange("p (kt t) -> p kt t", t=T)[:, :, _tt * 128:(_tt + 1) * 128]
                return e.tensor_copy(dstk2, _pb[:, 768:1024].rearrange("p (kt t) -> p kt t", t=128))
            e2t = p.step("vector", ebd2,
                         deps=[t2t] + (qtb.init + ktbB.init + ktbA.init if tt == 0 else []),
                         tag=f"l1f_qte{l}_{tt}")
            bank2.readers.append(e2t)
            qtb.w.append(e2t); ktbB.w.append(e2t); ktbA.w.append(e2t)

        obuf = k.claim("BA", 0, 8192, f"l1f_o{l}")
        pscr = [k.claim("BA", 8192 + i * 640, 8192 + (i + 1) * 640, f"l1f_ps{l}_{i}") for i in range(2)]
        ptscr = [k.claim("BA", 9472 + i * 640, 9472 + (i + 1) * 640, f"l1f_pts{l}_{i}") for i in range(2)]
        items = [(h, w, s) for h in range(8) for w in range(4) for s in range(4)]

        def la_S(idx):
            h, w, s = items[idx]
            jq = w * 4 + s
            nk = (s + 1) if w == 0 else 5
            kst = 0 if w == 0 else (w - 1) * 512 + s * 128
            par = idx % 2
            sbank = psb[par]
            spb = PS[par][:, 0:1024]

            def smm(e, _h=h, _jq=jq, _kst=kst, _nk=nk, _pb=spb):
                n1 = min(_nk * 128, 512)
                last = e.matmul(_pb[:, 0:n1], qt_ap(_h, _jq * 128, (_jq + 1) * 128),
                                kt_ap(_h, _kst, _kst + n1), start=True, stop=True)
                if _nk * 128 > 512:
                    last = e.matmul(_pb[:, 512:_nk * 128], qt_ap(_h, _jq * 128, (_jq + 1) * 128),
                                    kt_ap(_h, _kst + 512, _kst + _nk * 128), start=True, stop=True)
                return last
            sdeps = qtb.rd_deps() + (ktbB.rd_deps() if h < 4 else ktbA.rd_deps()) + \
                sbank.wr_deps() + psb2[par].wr_deps()
            stok2 = p.step("tensor", smm, deps=sdeps, tag=f"l1f_s{l}_{h}_{jq}")
            qtb.readers.append(stok2)
            (ktbB if h < 4 else ktbA).readers.append(stok2)
            sbank.w = [stok2]; sbank.readers = []
            psb2[par].w = [stok2]; psb2[par].readers = []

            ngm = SM[:, 256 + par:257 + par]
            mxt = p.step("vector",
                         lambda e, _pb=spb, _nk=nk, _n=ngm:
                         e.tensor_reduce(_n, _pb[:, 0:_nk * 128], axis=AX.X, op=ALU.max, negate=True),
                         deps=[stok2], tag=f"l1f_mx{l}_{h}_{jq}")
            sbank.readers.append(mxt); psb2[par].readers.append(mxt)

            pb_ = pscr[par]
            pslot = 8192 + par * 640
            ext = p.step("scalar",
                         lambda e, _pb=spb, _nk=nk, _n=ngm, _ps=pslot:
                         e.activation(BA[:, _ps:_ps + _nk * 128], _pb[:, 0:_nk * 128],
                                      ACT.Exp, bias=_n),
                         deps=[mxt] + pb_.wr_deps(), tag=f"l1f_ex{l}_{h}_{jq}")
            sbank.readers.append(ext); psb2[par].readers.append(ext)
            pb_.w = [ext]; pb_.readers = []; pb_.init = []

            def msk(e, _w=w, _nk=nk, _ps=pslot):
                if _w == 0:
                    ap = BA[:, _ps + (_nk - 1) * 128:_ps + _nk * 128]
                    return e.tensor_tensor(ap, ap, cl["loc_diag"], ALU.mult)
                ap0 = BA[:, _ps:_ps + 128]
                e.tensor_tensor(ap0, ap0, cl["loc_left"], ALU.mult)
                ap1 = BA[:, _ps + 512:_ps + 640]
                return e.tensor_tensor(ap1, ap1, cl["loc_diag"], ALU.mult)
            mkt = p.step("gpsimd", msk, deps=[ext] + cl["cst_tok"], tag=f"l1f_mk{l}_{h}_{jq}")
            pb_.w.append(mkt)
            return (idx, nk, kst, pslot, mkt)

        def la_rest(state):
            idx, nk, kst, pslot, mkt = state
            h, w, s = items[idx]
            jq = w * 4 + s
            par = idx % 2
            pb_ = pscr[par]
            ptb_ = ptscr[par]
            tb_bank = psb2[2 + par]
            ptb16 = PS[2 + par][:, 512:1024].bitcast(BF16)

            def ptr(e, _nk=nk, _ps=pslot, _pb=ptb16):
                last = None
                for z in range(_nk):
                    last = e.transpose(_pb[:, z * 128:(z + 1) * 128],
                                       BA[:, _ps + z * 128:_ps + (z + 1) * 128], cl["ident_bf16"])
                return last
            ptt = p.step("tensor", ptr, deps=[mkt] + tb_bank.wr_deps() + cl["cst_tok"],
                         tag=f"l1f_pt{l}_{h}_{jq}")
            pb_.readers.append(ptt)
            tb_bank.w = [ptt]; tb_bank.readers = []

            ptslot = 9472 + par * 640
            pte = p.step("scalar",
                         lambda e, _nk=nk, _pb=ptb16, _ps=ptslot:
                         e.copy(BA[:, _ps:_ps + _nk * 128], _pb[:, 0:_nk * 128]),
                         deps=[ptt] + ptb_.wr_deps(), tag=f"l1f_pte{l}_{h}_{jq}")
            tb_bank.readers.append(pte)
            ptb_.w = [pte]; ptb_.readers = []; ptb_.init = []

            pvbank = psb[2 + par]
            pvpb = PS[2 + par][:, 0:65]

            def pvm(e, _h=h, _nk=nk, _kst=kst, _ps=ptslot, _pb=pvpb):
                last = None
                for z in range(_nk):
                    ktile = (_kst + z * 128) // 128
                    last = e.matmul(_pb, BA[:, _ps + z * 128:_ps + (z + 1) * 128],
                                    BC[:, ktile * 520 + _h * 65:ktile * 520 + (_h + 1) * 65],
                                    start=(z == 0), stop=(z == _nk - 1))
                return last
            pvt = p.step("tensor", pvm, deps=[pte] + vbuf.rd_deps() + pvbank.wr_deps(),
                         tag=f"l1f_pv{l}_{h}_{jq}")
            ptb_.readers.append(pvt); vbuf.readers.append(pvt)
            pvbank.w = [pvt]; pvbank.readers = []

            dcol = SM[:, 258 + par:259 + par]
            d1 = p.step("vector", lambda e, _pb=pvpb, _d=dcol: e.reciprocal(_d, _pb[:, 64:65]),
                        deps=[pvt], tag=f"l1f_dr{l}_{h}_{jq}")
            d2 = p.step("vector",
                        lambda e, _pb=pvpb, _d=dcol, _jq=jq, _h=h:
                        e.tensor_scalar(BA[:, _jq * 512 + _h * 64:_jq * 512 + (_h + 1) * 64],
                                        _pb[:, 0:64], _d, None, ALU.mult),
                        deps=[d1] + (obuf.init if idx == 0 else []),
                        tag=f"l1f_dv{l}_{h}_{jq}")
            pvbank.readers.append(d2)
            obuf.w.append(d2)

        pending = None
        for idx in range(len(items)):
            st = la_S(idx)
            if pending is not None:
                la_rest(pending)
            pending = st
        la_rest(pending)

        woslab = k.claim("BB", 0, 4096, f"l1f_wo{l}")
        tok = p.dma("gpsimd", [
            (BB[:, 0:4096].rearrange("p (ot c) -> p ot c", c=C),
             dram[f"wo{l}"][:, :].rearrange("(ot p) c -> p ot c", p=128)),
        ], deps=woslab.init, tag=f"l1f_woL{l}")
        woslab.w.append(tok)
        otb = k.claim("BA", 8192, 16384, f"l1f_oT{l}")
        for g in range(8):
            bankt = psb[g % 2]
            ptb16 = PS[g % 2][:, 0:512].bitcast(BF16)

            def trb(e, _g=g, _pb=ptb16):
                last = None
                for z in range(8):
                    idx = _g * 8 + z
                    ot, tt = idx // 16, idx % 16
                    last = e.transpose(_pb[:, z * 128:(z + 1) * 128],
                                       BA[:, tt * 512 + ot * 128:tt * 512 + (ot + 1) * 128],
                                       cl["ident_bf16"])
                return last
            ttok = p.step("tensor", trb, deps=obuf.rd_deps() + bankt.wr_deps() + cl["cst_tok"],
                          tag=f"l1f_otr{l}_{g}")
            obuf.readers.append(ttok)
            bankt.w = [ttok]; bankt.readers = []

            def tev(e, _g=g, _pb=ptb16):
                last = None
                for z in range(8):
                    idx = _g * 8 + z
                    ot, tt = idx // 16, idx % 16
                    last = e.tensor_copy(BA[:, 8192 + ot * T + tt * 128:8192 + ot * T + (tt + 1) * 128],
                                         _pb[:, z * 128:(z + 1) * 128])
                return last
            etok = p.step("vector", tev, deps=[ttok] + (otb.init if g == 0 else []), tag=f"l1f_ote{l}_{g}")
            bankt.readers.append(etok)
            otb.w.append(etok)

        for ct in range(NC):
            for ch in range(4):
                idx = ct * 4 + ch
                bank = psb[2 + idx % 2]
                pbank = PS[2 + idx % 2][:, 0:512]

                def womm(e, _ct=ct, _ch=ch, _pb=pbank):
                    last = None
                    for ot in range(4):
                        last = e.matmul(_pb, BB[:, ot * C + _ct * 128:ot * C + (_ct + 1) * 128],
                                        BA[:, 8192 + ot * T + _ch * 512:8192 + ot * T + (_ch + 1) * 512],
                                        start=(ot == 0), stop=(ot == 3))
                    return last
                wt = p.step("tensor", womm, deps=otb.rd_deps() + woslab.rd_deps() + bank.wr_deps(),
                            tag=f"l1f_wom{l}_{ct}_{ch}")
                otb.readers.append(wt); woslab.readers.append(wt)
                bank.w = [wt]; bank.readers = []

                def wadd(e, _ct=ct, _ch=ch, _pb=pbank):
                    ssl = XFf[:, _ct * T + _ch * 512:_ct * T + (_ch + 1) * 512]
                    return e.tensor_tensor(ssl, _pb, ssl, ALU.add)
                at = p.step("vector", wadd, deps=[wt] + sT.wr_deps(), tag=f"l1f_woa{l}_{ct}_{ch}")
                bank.readers.append(at)
                sT.w.append(at)

        colbuf = k.claim("SM", 288, 360, f"l1f_cols{l}")
        tok = p.dma("sync", [
            (SM[:, 288:296], dram[f"lg{l}"][0, :].rearrange("(ct p) -> p ct", p=128)),
            (SM[:, 296:304], dram[f"lb{l}"][0, :].rearrange("(ct p) -> p ct", p=128)),
            (SM[:, 304:312], dram[f"b2_{l}"][0, :].rearrange("(ct p) -> p ct", p=128)),
            (SM[:, 312:334], dram[f"b1a{l}"][0, :].rearrange("(jt p) -> p jt", p=128)),
            (SM[:, 334:356], dram[f"b1g{l}"][0, :].rearrange("(jt p) -> p jt", p=128)),
        ], deps=colbuf.init, tag=f"l1f_cols{l}", slow=True)
        colbuf.w.append(tok)

        hb = k.claim("BA", 0, 16384, f"l1f_hb{l}")

        def mkhb(e):
            last = None
            for ct in range(NC):
                src2 = XFf[:, ct * T:(ct + 1) * T]
                dst = BA[:, ct * T:(ct + 1) * T]
                if ct < 4:
                    last = e.copy(dst, src2)
                else:
                    last = e.copy(dst[:, 1:T], src2[:, 0:T - 1])
            return last
        hbt = p.step("scalar", mkhb, deps=sT.rd_deps() + hb.init, tag=f"l1f_hb{l}")
        sT.readers.append(hbt)
        hbz = p.step("vector", lambda e: [e.memset(BA[:, ct * T:ct * T + 1], 0.0) for ct in range(4, NC)][-1],
                     deps=[hbt], tag=f"l1f_hbz{l}")
        hb.w = [hbt, hbz]

        strows = k.claim("BB", 0, 20480, f"l1f_strows{l}")
        sqscr = k.claim("EW", 0, 4096, f"l1f_sq{l}")
        row_toks = []
        for ch in range(4):
            mm_deps = psb[2].wr_deps() + psb2[2].wr_deps()
            for ct in range(NC):
                sqt = p.step("scalar",
                             lambda e, _ct=ct, _ch=ch:
                             e.activation(EW[:, (_ct % 2) * 1024:(_ct % 2) * 1024 + 512],
                                          BA[:, _ct * T + _ch * 512:_ct * T + (_ch + 1) * 512],
                                          ACT.Square),
                             deps=hb.rd_deps() + sqscr.wr_deps(), tag=f"l1f_sqp{l}_{ch}_{ct}")
                hb.readers.append(sqt)
                sqscr.w = [sqt]; sqscr.readers = []; sqscr.init = []

                def omm(e, _ct=ct, _ch=ch):
                    e.matmul(PS[2][0:1, 0:512], cl["ones_col"],
                             BA[:, _ct * T + _ch * 512:_ct * T + (_ch + 1) * 512],
                             start=(_ct == 0), stop=(_ct == NC - 1))
                    return e.matmul(PS[2][0:1, 512:1024], cl["ones_col"],
                                    EW[:, (_ct % 2) * 1024:(_ct % 2) * 1024 + 512],
                                    start=(_ct == 0), stop=(_ct == NC - 1))
                omt = p.step("tensor", omm,
                             deps=[sqt] + hb.rd_deps() + (mm_deps if ct == 0 else []) + cl["cst_tok"],
                             tag=f"l1f_om{l}_{ch}_{ct}")
                hb.readers.append(omt)
                sqscr.readers.append(omt)
                if ct == 0:
                    psb[2].w = [omt]; psb[2].readers = []
                    psb2[2].w = [omt]; psb2[2].readers = []
                else:
                    psb[2].w.append(omt); psb2[2].w.append(omt)
            rt = p.step("vector",
                        lambda e, _ch=ch: (e.tensor_copy(BBF[0:1, _ch * 512:(_ch + 1) * 512], PS[2][0:1, 0:512]),
                                           e.tensor_copy(BBF[0:1, 2048 + _ch * 512:2048 + (_ch + 1) * 512],
                                                         PS[2][0:1, 512:1024]))[-1],
                        deps=psb[2].rd_deps() + psb2[2].rd_deps() + (strows.init if ch == 0 else []),
                        tag=f"l1f_row{l}_{ch}")
            psb[2].readers.append(rt); psb2[2].readers.append(rt)
            strows.w.append(rt)
            row_toks.append(rt)

        R = lambda c0, c1: BBF[0:1, c0:c1]

        def rows1(e):
            e.tensor_tensor(R(8192, 10240), R(0, 2048), R(0, 2048), ALU.mult)
            e.tensor_scalar(R(4096, 6144), R(0, 2048), 1.0 / C, None, ALU.mult)
            e.tensor_scalar(R(8192, 10240), R(8192, 10240), -1.0 / C, None, ALU.mult)
            e.tensor_scalar(R(0, 2048), R(0, 2048), 1.0, None, ALU.mult)
            e.tensor_tensor(R(2048, 4096), R(2048, 4096), R(8192, 10240), ALU.add)
            e.tensor_scalar(R(8192, 10240), R(8192, 10240), 1.0, None, ALU.mult)
            return e.tensor_scalar(R(2048, 4096), R(2048, 4096), 1.0 / C, 1e-5, ALU.mult, ALU.add)
        r1 = p.step("vector", rows1, deps=row_toks, tag=f"l1f_rs1{l}")
        r2 = p.step("scalar", lambda e: e.activation(R(8192, 10240), R(2048, 4096), ACT.Sqrt),
                    deps=[r1], tag=f"l1f_rs2{l}")

        def rows2(e):
            e.reciprocal(R(6144, 8192), R(8192, 10240))
            e.tensor_tensor(R(8192, 10240), R(6144, 8192), R(6144, 8192), ALU.mult)
            e.tensor_tensor(R(8192, 10240), R(8192, 10240), R(2048, 4096), ALU.mult)
            e.tensor_scalar(R(8192, 10240), R(8192, 10240), -0.5, 1.5, ALU.mult, ALU.add)
            e.tensor_tensor(R(6144, 8192), R(6144, 8192), R(8192, 10240), ALU.mult)
            e.tensor_scalar(R(4096, 6144), R(4096, 6144), -1.0, None, ALU.mult)
            return e.tensor_tensor(R(4096, 6144), R(4096, 6144), R(6144, 8192), ALU.mult)
        r3 = p.step("vector", rows2, deps=[r2], tag=f"l1f_rs3{l}")
        strows.w += [r1, r2, r3]
        rsp = p.dma("sync", [(dram["rowspill"][0:1, :], R(4096, 6144)),
                             (dram["rowspill"][1:2, :], R(6144, 8192))],
                    deps=[r3], tag=f"l1f_rsp{l}")
        strows.readers.append(rsp)

        w2slab = k.claim("BB", 0, 22528, f"l1f_w2{l}")
        tokw2 = p.dma("gpsimd", [
            (BB[:, 0:22528].rearrange("p (jt c) -> p jt c", c=C),
             dram[f"w2p{l}"][:, :].rearrange("(jt p) c -> p jt c", p=128)),
        ], deps=w2slab.init + [rsp], tag=f"l1f_w2L{l}")
        w2slab.w.append(tokw2)

        wst = [k.claim("BC", 2048 + i * 1024, 2048 + (i + 1) * 1024, f"l1f_w1s{l}_{i}") for i in range(6)]

        for qtr in range(4):
            rowrest = k.claim("EW", 2048, 4096, f"l1f_rows{l}_{qtr}")
            rrt = p.dma("sync", [(EWF[0:1, 1024:1536], dram["rowspill"][0:1, qtr * 512:(qtr + 1) * 512]),
                                 (EWF[0:1, 1536:2048], dram["rowspill"][1:2, qtr * 512:(qtr + 1) * 512])],
                        deps=rowrest.init + [rsp], tag=f"l1f_rr{l}_{qtr}")
            rowrest.w = [rrt]
            def bc1(e):
                e.matmul(PS[3][:, 0:512], ones_f32, EWF[0:1, 1024:1536], start=True, stop=True)
                return e.matmul(PS[3][:, 512:1024], ones_f32, EWF[0:1, 1536:2048], start=True, stop=True)
            b1t = p.step("tensor", bc1, deps=[rrt, o32tok] + psb[3].wr_deps() + psb2[3].wr_deps(),
                         tag=f"l1f_bc{l}_{qtr}")
            rowrest.readers.append(b1t)
            psb[3].w = [b1t]; psb[3].readers = []
            psb2[3].w = [b1t]; psb2[3].readers = []
            b2t = p.step("vector", lambda e: (e.tensor_copy(EWF[:, 0:512], PS[3][:, 0:512]),
                                              e.tensor_copy(EWF[:, 512:1024], PS[3][:, 512:1024]))[-1],
                         deps=[b1t] + sqscr.wr_deps() + sqscr.readers, tag=f"l1f_bce{l}_{qtr}")
            psb[3].readers.append(b2t); psb2[3].readers.append(b2t)
            sqscr.w = [b2t]; sqscr.readers = []

            h2nq = k.claim("BA", 0, 4096, f"l1f_h2n{l}_{qtr}")
            t0buf = k.claim("BC", 0, 2048, f"l1f_t0{l}_{qtr}")
            t0A, t0B = BCF[:, 0:512], BCF[:, 512:1024]

            def h2pair(e, _ca, _cb, _q):
                lo = _q * 512
                for t0, ct in ((t0A, _ca), (t0B, _cb)):
                    if ct < 4 or lo > 0:
                        off = 0 if ct < 4 else -1
                        e.tensor_tensor(t0, XFf[:, ct * T + lo + off:ct * T + lo + off + 512],
                                        EWF[:, 512:1024], ALU.mult)
                    else:
                        e.tensor_tensor(t0[:, 1:512], XFf[:, ct * T:ct * T + 511],
                                        EWF[:, 513:1024], ALU.mult)
                        e.memset(t0[:, 0:1], 0.0)
                for t0, ct in ((t0A, _ca), (t0B, _cb)):
                    e.tensor_tensor(t0, t0, EWF[:, 0:512], ALU.add)
                last = None
                for t0, ct in ((t0A, _ca), (t0B, _cb)):
                    last = e.tensor_scalar(BA[:, ct * 512:(ct + 1) * 512], t0,
                                           SM[:, 288 + ct:289 + ct], SM[:, 296 + ct:297 + ct],
                                           ALU.mult, ALU.add)
                return last
            for cp in range(0, NC, 2):
                htok = p.step("vector",
                              lambda e, _ca=cp, _cb=cp + 1, _q=qtr: h2pair(e, _ca, _cb, _q),
                              deps=[b2t] + sT.rd_deps() + colbuf.rd_deps() + h2nq.wr_deps() +
                                   (h2nq.init + t0buf.init if cp == 0 else []),
                              tag=f"l1f_h2n{l}_{qtr}_{cp}")
                sT.readers.append(htok); colbuf.readers.append(htok)
                h2nq.w.append(htok)
                t0buf.w.append(htok)
                sqscr.readers.append(htok)

            hts = k.claim("BA", 4096, 15360, f"l1f_ht{l}_{qtr}")
            gsc2 = k.claim("EW", 2048, 4096, f"l1f_ab{l}_{qtr}")
            for jt in range(22):
                sa, sg = wst[(jt % 3) * 2], wst[(jt % 3) * 2 + 1]
                base_a = 2048 + (jt % 3) * 2 * 1024
                base_g = base_a + 1024
                wtok = p.dma("gpsimd", [
                    (BC[:, base_a:base_a + 1024].rearrange("p (ct j) -> p ct j", j=128),
                     dram[f"w1a{l}"][:, jt * 128:(jt + 1) * 128].rearrange("(ct p) j -> p ct j", p=128)),
                    (BC[:, base_g:base_g + 1024].rearrange("p (ct j) -> p ct j", j=128),
                     dram[f"w1g{l}"][:, jt * 128:(jt + 1) * 128].rearrange("(ct p) j -> p ct j", p=128)),
                ], deps=sa.wr_deps() + sg.wr_deps(), tag=f"l1f_w1{l}_{qtr}_{jt}")
                sa.w = [wtok]; sa.readers = []; sa.init = []
                sg.w = [wtok]; sg.readers = []; sg.init = []

                bank = psb[jt % 2]
                pa = PS[jt % 2][:, 0:512]
                pg = PS[jt % 2][:, 512:1024]

                def ymm(e, _ba=base_a, _bg=base_g, _pa=pa, _pg=pg):
                    last = None
                    for ct in range(NC):
                        e.matmul(_pa, BC[:, _ba + ct * 128:_ba + (ct + 1) * 128],
                                 BA[:, ct * 512:(ct + 1) * 512], start=(ct == 0), stop=(ct == NC - 1))
                        last = e.matmul(_pg, BC[:, _bg + ct * 128:_bg + (ct + 1) * 128],
                                        BA[:, ct * 512:(ct + 1) * 512], start=(ct == 0), stop=(ct == NC - 1))
                    return last
                ymt = p.step("tensor", ymm,
                             deps=[wtok] + h2nq.rd_deps() + bank.wr_deps() + psb2[jt % 2].wr_deps(),
                             tag=f"l1f_y{l}_{qtr}_{jt}")
                sa.readers.append(ymt); sg.readers.append(ymt)
                h2nq.readers.append(ymt)
                bank.w = [ymt]; bank.readers = []
                psb2[jt % 2].w = [ymt]; psb2[jt % 2].readers = []

                def gact(e, _jt=jt, _pa=pa, _pg=pg):
                    e.activation(EW[:, 2048 + (_jt % 2) * 1024:2048 + (_jt % 2) * 1024 + 512], _pa,
                                 ACT.Identity, bias=SM[:, 312 + _jt:313 + _jt])
                    return e.activation(EW[:, 2048 + (_jt % 2) * 1024 + 512:2048 + (_jt % 2) * 1024 + 1024],
                                        _pg, ACT.Gelu, bias=SM[:, 334 + _jt:335 + _jt])
                gat = p.step("scalar", gact, deps=[ymt] + gsc2.wr_deps() + colbuf.rd_deps(),
                             tag=f"l1f_g{l}_{qtr}_{jt}")
                bank.readers.append(gat); psb2[jt % 2].readers.append(gat)
                colbuf.readers.append(gat)
                gsc2.w = [gat]; gsc2.readers = []; gsc2.init = []

                htk = p.step("vector",
                             lambda e, _jt=jt:
                             e.tensor_tensor(BA[:, 4096 + _jt * 512:4096 + (_jt + 1) * 512],
                                             EW[:, 2048 + (_jt % 2) * 1024:2048 + (_jt % 2) * 1024 + 512],
                                             EW[:, 2048 + (_jt % 2) * 1024 + 512:2048 + (_jt % 2) * 1024 + 1024],
                                             ALU.mult),
                             deps=[gat] + (hts.init if jt == 0 else []), tag=f"l1f_h{l}_{qtr}_{jt}")
                gsc2.readers.append(htk)
                hts.w.append(htk)

            for ct in range(NC):
                bank = psb[2 + ct % 2]
                pbank = PS[2 + ct % 2][:, 0:512]

                def wmm2(e, _ct=ct, _pb=pbank):
                    last = None
                    for jt in range(22):
                        last = e.matmul(_pb, BB[:, jt * C + _ct * 128:jt * C + (_ct + 1) * 128],
                                        BA[:, 4096 + jt * 512:4096 + (jt + 1) * 512],
                                        start=(jt == 0), stop=(jt == 21))
                    return last
                wt2 = p.step("tensor", wmm2, deps=hts.rd_deps() + w2slab.rd_deps() + bank.wr_deps(),
                             tag=f"l1f_w2m{l}_{qtr}_{ct}")
                hts.readers.append(wt2); w2slab.readers.append(wt2)
                bank.w = [wt2]; bank.readers = []

                def wadd2(e, _ct=ct, _q=qtr, _pb=pbank):
                    ssl = XFf[:, _ct * T + _q * 512:_ct * T + (_q + 1) * 512]
                    return e.scalar_tensor_tensor(ssl, _pb, SM[:, 304 + _ct:305 + _ct], ssl,
                                                  ALU.add, ALU.add)
                at2 = p.step("vector", wadd2, deps=[wt2] + sT.wr_deps() + colbuf.rd_deps(),
                             tag=f"l1f_w2a{l}_{qtr}_{ct}")
                bank.readers.append(at2)
                colbuf.readers.append(at2)
                sT.w.append(at2)

    k.y_done = True
    yscr = k.claim("EW", 0, 4096, "l1f_fin")
    ytoks = []
    for it in range(NT // 2):
        ta, tb = it * 2, it * 2 + 1

        def ftb(e, _ta=ta, _tb=tb):
            last = None
            for z, tt in enumerate((_ta, _tb)):
                pb = PS[z][:, 0:1024]
                for ct in range(NC):
                    last = e.transpose(pb[:, ct * 128:(ct + 1) * 128],
                                       XFf[:, ct * T + tt * 128:ct * T + (_tb - _ta and tt + 1) * 128]
                                       if False else XFf[:, ct * T + tt * 128:ct * T + (tt + 1) * 128],
                                       cl["ident_f32"])
            return last
        bdeps = psb[0].wr_deps() + psb2[0].wr_deps() + psb[1].wr_deps() + psb2[1].wr_deps()
        ftok = p.step("tensor", ftb, deps=sT.rd_deps() + bdeps + cl["cst_tok"], tag=f"l1f_fy{it}")
        sT.readers.append(ftok)
        for b in (psb[0], psb2[0], psb[1], psb2[1]):
            b.w = [ftok]; b.readers = []

        rtok = p.dma("sync", [(EWF[:, 0:2048].rearrange("p (a c) -> p a c", c=C),
                               dram["xspill"][:, :].rearrange("(a p) c -> p a c", p=128)[:, ta:tb + 1, :])],
                     deps=yscr.wr_deps() + [sp_tok], tag=f"l1f_fx{it}")
        yscr.w = [rtok]; yscr.readers = []; yscr.init = []

        def vadd(e):
            e.tensor_tensor(EWF[:, 0:1024], EWF[:, 0:1024], PS[0][:, 0:1024], ALU.add)
            return e.tensor_tensor(EWF[:, 1024:2048], EWF[:, 1024:2048], PS[1][:, 0:1024], ALU.add)
        vt = p.step("vector", vadd, deps=[ftok, rtok], tag=f"l1f_fv{it}")
        for b in (psb[0], psb2[0], psb[1], psb2[1]):
            b.readers.append(vt)
        yscr.w.append(vt)

        ytok = p.dma("sync", [(dram["y"][:, :].rearrange("(a p) c -> p a c", p=128)[:, ta:tb + 1, :],
                               EWF[:, 0:2048].rearrange("p (a c) -> p a c", c=C))],
                     deps=[vt], tag=f"l1f_fyo{it}")
        yscr.readers.append(ytok)
        ytoks.append(ytok)
    p.step("sync", lambda e: e.nop(), deps=ytoks, tag="l1f_ywait")
    return xbuf


def _prep_inputs(inputs):
    g = {n: np.asarray(v) for n, v in inputs.items()}
    cst = _host_constants()
    base = {}
    base["L0_W1"] = np.ascontiguousarray(g["L0_attn_W1"], np.float32)
    base["L0_W2"] = np.ascontiguousarray(g["L0_attn_W2"], np.float32)
    base["L0_bias"] = np.ascontiguousarray(g["L0_attn_bias"][None, :], np.float32)
    etT = np.zeros((C, 32), np.float32)
    etT[:, :21] = g["L0_attn_embeddings_table"].T
    base["L0_etT"] = etT
    base["g1"] = np.ascontiguousarray(g["L0_ffn_g1"][None, :], np.float32)
    base["g2"] = np.ascontiguousarray(g["L0_ffn_g2"][None, :], np.float32)
    base["Wq"] = np.ascontiguousarray(g["L0_ffn_Wq"], np.float32)
    base["Wdkv"] = np.ascontiguousarray(g["L0_ffn_Wdkv"], np.float32)
    base["Wuk"] = np.ascontiguousarray(g["L0_ffn_Wuk"], np.float32)
    base["Wuv"] = np.ascontiguousarray(g["L0_ffn_Wuv"], np.float32)
    base["Wo"] = np.ascontiguousarray(g["L0_ffn_Wo"], np.float32)
    base["gate_w"] = np.ascontiguousarray(g["L0_ffn_gate_w"], np.float32)
    base["moe_w1"] = np.ascontiguousarray(g["L0_ffn_w1"], np.float32)
    base["moe_w2"] = np.ascontiguousarray(g["L0_ffn_w2"], np.float32)
    base["lat_qT"] = np.ascontiguousarray(np.transpose(g["L1_attn_latent_queries"][0], (1, 2, 0)), np.float32)
    base["WkT"] = np.ascontiguousarray(g["L1_attn_W_k"].T, np.float32)
    base["WvT"] = np.ascontiguousarray(g["L1_attn_W_v"].T, np.float32)
    base["WgT"] = np.ascontiguousarray(g["L1_attn_W_g"].T, np.float32)
    base["WpT"] = np.ascontiguousarray(g["L1_attn_W_p"].T, np.float32)
    for l in range(2):
        base[f"wqkv{l}"] = np.ascontiguousarray(g["L1_ffn_wqkv"][l], np.float32)
        base[f"qs{l}"] = np.ascontiguousarray(g["L1_ffn_q_scale"][l][None, :], np.float32)
        base[f"ks{l}"] = np.ascontiguousarray(g["L1_ffn_k_scale"][l][None, :], np.float32)
        base[f"wo{l}"] = np.ascontiguousarray(g["L1_ffn_wo"][l], np.float32)
        base[f"lg{l}"] = np.ascontiguousarray(g["L1_ffn_ln_g"][l][None, :], np.float32)
        base[f"lb{l}"] = np.ascontiguousarray(g["L1_ffn_ln_b"][l][None, :], np.float32)
        w1 = np.asarray(g["L1_ffn_w1"][l], np.float32)
        w1a = np.zeros((C, 2816), np.float32); w1a[:, :2730] = w1[:, :2730]
        w1g = np.zeros((C, 2816), np.float32); w1g[:, :2730] = w1[:, 2730:]
        base[f"w1a{l}"] = w1a; base[f"w1g{l}"] = w1g
        b1 = np.asarray(g["L1_ffn_b1"][l], np.float32)
        b1a = np.zeros((1, 2816), np.float32); b1a[0, :2730] = b1[:2730]
        b1g = np.zeros((1, 2816), np.float32); b1g[0, :2730] = b1[2730:]
        base[f"b1a{l}"] = b1a; base[f"b1g{l}"] = b1g
        w2p = np.zeros((2816, C), np.float32); w2p[:2730] = np.asarray(g["L1_ffn_w2"][l], np.float32)
        base[f"w2p{l}"] = w2p
        base[f"b2_{l}"] = np.ascontiguousarray(g["L1_ffn_b2"][l][None, :], np.float32)
    for name, v in cst.items():
        base["cst_" + name] = np.ascontiguousarray(v)
    in_maps = []
    X = np.ascontiguousarray(g["L0_attn_input_data"], np.float32)
    for b in range(8):
        m = dict(base)
        m["x_in"] = np.ascontiguousarray(X[b])
        in_maps.append(m)
    return in_maps


def kernel(**inputs):
    in_maps = _prep_inputs(inputs)
    nc = build_program()
    res = run_bass_kernel_spmd(nc, in_maps, core_ids=list(range(8)))
    return np.stack([np.asarray(r["y"], np.float32) for r in res.results], 0)
